# version 31
# baseline (speedup 1.0000x reference)
import math
import numpy as np
import ml_dtypes

import concourse.bass as bass
import concourse.mybir as mybir
from concourse.bass_utils import run_bass_kernel_spmd

F32 = mybir.dt.float32
BF16 = mybir.dt.bfloat16
FP8 = mybir.dt.float8e4
I32 = mybir.dt.int32
AF = mybir.ActivationFunctionType
ALU = mybir.AluOpType

HID = 1024
DV = 2048
CH = 128
N_CHUNKS = 16
EPS = 1e-5
LN16 = math.log(16.0)
MAGIC = 0x5F3759DF

ENGINES = ("pe", "act", "dve", "pool", "sp")


class Emitter:

    DUR = {"pe": 240, "dve": 860, "act": 1120, "pool": 2400, "sp": 1000}
    XLAT = 380

    def __init__(self, nc):
        self.nc = nc
        self.ops = []
        self.last_write = {}
        self.readers = {}
        self.last_on = {}
        self.regions = []

    @staticmethod
    def _key(b):
        if isinstance(b, str):
            return b
        t = getattr(b, "tensor", b)
        n = getattr(t, "name", None)
        return n if n is not None else id(t)

    def op(self, eng, fn, reads=(), writes=(), stream=None, inc=1, cost=None,
           lat=0):
        import sys
        line = sys._getframe(1).f_lineno
        idx = len(self.ops)
        deps = set()
        if stream is not None:
            inc = 16
        rkeys = [self._key(b) for b in reads]
        wkeys = [self._key(b) for b in writes]
        for kb in rkeys:
            lw = self.last_write.get(kb)
            if lw is not None:
                deps.add(lw)
        for kb in wkeys:
            lw = self.last_write.get(kb)
            if lw is not None:
                deps.add(lw)
            for r in self.readers.get(kb, ()):
                deps.add(r)
        sig = stream if stream is not None else eng
        if stream is not None and sig in self.last_on:
            deps.add(self.last_on[sig])
        if stream is not None:
            self.last_on[sig] = idx
        for kb in rkeys:
            self.readers.setdefault(kb, []).append(idx)
        for kb in wkeys:
            self.last_write[kb] = idx
            self.readers[kb] = []
        self.ops.append({"eng": eng, "fn": fn, "sig": sig, "inc": inc,
                         "deps": deps, "cost": cost or self.DUR[eng],
                         "lat": lat, "line": line})

    def barrier(self):
        self.regions.append(len(self.ops))

    def _schedule(self, lo, hi):
        import heapq
        ops = self.ops
        indeg = {}
        succ = {i: [] for i in range(lo, hi)}
        for i in range(lo, hi):
            d = [j for j in ops[i]["deps"] if lo <= j < hi]
            indeg[i] = len(d)
            for j in d:
                succ[j].append(i)
        finish = {j: 0 for j in range(lo)}
        engine_time = {e: 0 for e in ENGINES}
        bind = {}
        blevel = {}
        for i in range(hi - 1, lo - 1, -1):
            blevel[i] = ops[i]["cost"] + ops[i]["lat"] + max(
                (blevel[k] for k in succ[i]), default=0)
        ready = []
        for i in range(lo, hi):
            if indeg[i] == 0:
                heapq.heappush(ready, (0, -blevel[i], i))
        order = {e: [] for e in ENGINES}
        scheduled = 0
        while ready:
            _, _, i = heapq.heappop(ready)
            o = ops[i]
            e = o["eng"]
            dep_t = 0
            dep_i = None
            for j in o["deps"]:
                if j < lo:
                    continue
                f = finish.get(j)
                if ops[j]["eng"] != e:
                    lat = self.XLAT
                elif e == "pe":
                    lat = 0
                else:
                    lat = 150
                if f + lat > dep_t:
                    dep_t = f + lat
                    dep_i = j
            if dep_t > engine_time[e]:
                bind[i] = ("dep", dep_i)
            else:
                bind[i] = ("queue", order[e][-1] if order[e] else None)
            st = max(engine_time[e], dep_t)
            fin = st + o["cost"]
            engine_time[e] = fin
            finish[i] = fin + o["lat"]
            order[e].append(i)
            scheduled += 1
            for k in succ[i]:
                indeg[k] -= 1
                if indeg[k] == 0:
                    dt = max((finish.get(j, 0) for j in ops[k]["deps"] if j >= lo),
                             default=0)
                    heapq.heappush(ready, (dt, -blevel[k], k))
        assert scheduled == hi - lo, (scheduled, hi - lo)
        busy = {e: 0 for e in ENGINES}
        for i in range(lo, hi):
            busy[ops[i]["eng"]] += ops[i]["cost"]
        print(f"[sched] region {lo}:{hi} predicted makespan "
              f"{max(engine_time.values()) / 1e3:.1f}us "
              + " ".join(f"{e}={engine_time[e] / 1e3:.0f}" for e in ENGINES)
              + " | busy "
              + " ".join(f"{e}={busy[e] / 1e3:.0f}" for e in ENGINES))
        import os
        if os.environ.get("SCHED_CP"):
            cur = max((i for i in range(lo, hi)), key=lambda i: finish[i])
            agg = {}
            while cur is not None and cur >= lo:
                o = ops[cur]
                kind, prev = bind.get(cur, ("dep", None))
                k = (o["eng"], o["line"], kind)
                a = agg.setdefault(k, [0, 0])
                a[0] += 1
                a[1] += o["cost"] + (o["lat"] if kind == "dep" else 0)
                cur = prev
            for k, (n, tot) in sorted(agg.items(), key=lambda kv: -kv[1][1])[:16]:
                print(f"  [cp] {k[0]:4s} line={k[1]} {k[2]:5s} n={n:4d} "
                      f"tot={tot / 1e3:7.1f}us")
        return order

    def emit(self, block, make_sem, handles):
        regions = self.regions + [len(self.ops)]
        region_lists = []
        count_of = {}
        counts = {}
        snapshots = []
        lo = 0
        for hi in regions:
            order = self._schedule(lo, hi)
            region_lists.append(order)
            for e in ENGINES:
                for i in order[e]:
                    sig = self.ops[i]["sig"]
                    counts[sig] = counts.get(sig, 0) + self.ops[i]["inc"]
                    count_of[i] = counts[sig]
            snapshots.append(dict(counts))
            lo = hi
        sems = {name: make_sem(name) for name in counts}
        final = dict(counts)

        def run_stream(eng):
            eh = handles[eng]
            hw = {}
            for ri, order in enumerate(region_lists):
                if ri > 0:
                    for sig2, c in snapshots[ri - 1].items():
                        if sig2 != eng and c > hw.get(sig2, 0):
                            eh.wait_ge(sems[sig2], c)
                            hw[sig2] = c
                for i in order[eng]:
                    o = self.ops[i]
                    waits = {}
                    for j in o["deps"]:
                        sj = self.ops[j]["sig"]
                        if sj == "pe" and eng == "pe":
                            continue
                        waits[sj] = max(waits.get(sj, 0), count_of[j])
                    for sig2, vv in sorted(waits.items(), key=lambda kv: str(kv[0])):
                        if vv > hw.get(sig2, 0):
                            eh.wait_ge(sems[sig2], vv)
                            hw[sig2] = vv
                    o["fn"]().then_inc(sems[o["sig"]], o["inc"])
            for sig2, c in final.items():
                if sig2 != eng and c > hw.get(sig2, 0):
                    eh.wait_ge(sems[sig2], c)

        for eng in ENGINES:
            deco = {"pe": block.tensor, "act": block.scalar,
                    "dve": block.vector, "pool": block.gpsimd,
                    "sp": block.sync}[eng]

            def body(handle, _eng=eng):
                run_stream(_eng)
            deco(body)


def build_program(n_chunks=N_CHUNKS, debug=False):
    nc = bass.Bass("TRN2", target_bir_lowering=False)
    SEG = n_chunks * CH

    xin = nc.dram_tensor("x_seg", [SEG, HID], F32, kind="ExternalInput")
    xpre_d = nc.dram_tensor("x_pre", [SEG, HID], F32, kind="ExternalInput")
    wq_d = nc.dram_tensor("Wq", [HID, 1024], F32, kind="ExternalInput")
    wk_d = nc.dram_tensor("Wk", [HID, 1024], F32, kind="ExternalInput")
    wv_d = nc.dram_tensor("Wv", [HID, 2048], F32, kind="ExternalInput")
    wg_d = nc.dram_tensor("Wg", [HID, 2048], F32, kind="ExternalInput")
    wgk1_d = nc.dram_tensor("Wgk1", [HID, 16], F32, kind="ExternalInput")
    wgk2x_d = nc.dram_tensor("Wgk2x", [17, 1024], BF16, kind="ExternalInput")
    onesrow_d = nc.dram_tensor("onesrow", [1, 128], BF16, kind="ExternalInput")
    biasrow_d = nc.dram_tensor("biasrow", [1, 1024], BF16, kind="ExternalInput")
    gnw_d = nc.dram_tensor("gnw", [1, 512], F32, kind="ExternalInput")
    wo_d = nc.dram_tensor("Wo", [2048, 1024], F32, kind="ExternalInput")
    wf_d = nc.dram_tensor("Wffn", [1024, 1024], F32, kind="ExternalInput")
    bf_d = nc.dram_tensor("bffn", [1, 1024], F32, kind="ExternalInput")
    role_d = nc.dram_tensor("role", [1, 1], F32, kind="ExternalInput")
    triu_d = nc.dram_tensor("triu", [128, 128], BF16, kind="ExternalInput")
    mask4_d = nc.dram_tensor("mask4", [128, 512], BF16, kind="ExternalInput")
    ones128_d = nc.dram_tensor("ones128", [128, 128], BF16, kind="ExternalInput")
    onescol_d = nc.dram_tensor("onescol", [128, 1], BF16, kind="ExternalInput")

    y = nc.dram_tensor("y_seg", [SEG, HID], F32, kind="ExternalOutput")
    spill_kind = "ExternalOutput" if debug else "Internal"
    osp_d = nc.dram_tensor("o_sp", [SEG, DV], BF16, kind=spill_kind)
    hsp_d = nc.dram_tensor("h_sp", [SEG, HID], BF16, kind=spill_kind)
    if debug:
        sdump_d = nc.dram_tensor("s_dump", [128, 4096], F32, kind="ExternalOutput")
        smid_d = nc.dram_tensor("smid_dump", [128, 4096], F32, kind="ExternalOutput")

    from contextlib import ExitStack
    with ExitStack() as es:
        def sb(name, shape, dtype, **kw):
            return es.enter_context(nc.sbuf_tensor(name, shape, dtype, **kw))

        triu = sb("triu_sb", [128, 128], BF16)
        mask4 = sb("mask4_sb", [128, 512], BF16)
        ones128 = sb("ones128_sb", [128, 128], BF16)
        onescol = sb("onescol_sb", [128, 1], BF16)
        gnwc = sb("gnwc", [128, 4], F32)
        rolec = sb("rolec", [128, 1], F32)
        magic = sb("magic", [128, 4], I32)
        ebl4 = [sb(f"ebl{i}", [128, 8], F32) for i in range(4)]
        ecc4 = [sb(f"ecc{i}", [128, 8], F32) for i in range(4)]
        t8a = sb("t8a", [128, 4], F32)
        t8i = sb("t8i", [128, 4], I32)
        t8c = sb("t8c", [128, 4], F32)
        t8j = sb("t8j", [128, 4], I32)
        t8e = sb("t8e", [128, 4], F32)
        t8f = sb("t8f", [128, 4], I32)
        lnc = sb("lnc", [128, 16], F32)
        lnc2 = sb("lnc2", [128, 16], F32)
        ccols = sb("ccols", [128, 8], F32)
        zeros1k = sb("zeros1k", [128, 1024], F32)
        dump = sb("dump", [128, 512], F32)

        xt = [sb(f"xt{i}", [128, 1024], F32) for i in range(4)]
        hln = [sb(f"hln{i}", [128, 1024], BF16) for i in range(2)]
        hfm = [sb(f"hfm{i}", [128, 1024], BF16, is_transpose_dest=True) for i in range(4)]

        PT = [es.enter_context(nc.psum_tensor(f"PT{i}", [128, 1024], F32)) for i in range(4)]
        def PBv(i):
            return PT[i // 2][:, 512 * (i % 2):512 * (i % 2 + 1)]
        PB = [PBv(i) for i in range(8)]
        PBH = [f"PT{i // 2}:{i % 2}" for i in range(8)]
        PTH = [[f"PT{j}:0", f"PT{j}:1"] for j in range(4)]

        E = Emitter(nc)
        v = nc.vector
        s = nc.scalar
        t = nc.tensor
        g = nc.gpsimd
        sy = nc.sync

        def quake_rsqrt(dst, dst_h, src, src_h, tf, ti, n_iter=1):
            w = dst.shape[1]
            E.op("dve", lambda: v.tensor_single_scalar(
                out=ti[:, :w], in_=src.bitcast(I32), scalar=1,
                op=ALU.logical_shift_right), reads=[src_h], writes=[ti], cost=150)
            E.op("dve", lambda: v.tensor_tensor(
                out=dst.bitcast(I32), in0=magic[:, :w], in1=ti[:, :w],
                op=ALU.subtract), reads=[magic, ti], writes=[dst_h], cost=150)
            for _ in range(n_iter):
                E.op("dve", lambda: v.tensor_tensor(
                    out=tf[:, :w], in0=src, in1=dst, op=ALU.mult),
                    reads=[src_h, dst_h], writes=[tf], cost=150)
                E.op("dve", lambda: v.tensor_tensor(
                    out=tf[:, :w], in0=tf[:, :w], in1=dst, op=ALU.mult),
                    reads=[tf, dst_h], writes=[tf], cost=150)
                E.op("dve", lambda: v.tensor_scalar(
                    out=tf[:, :w], in0=tf[:, :w], scalar1=-0.5,
                    scalar2=1.5, op0=ALU.mult, op1=ALU.add),
                    reads=[tf], writes=[tf], cost=150)
                E.op("dve", lambda: v.tensor_tensor(
                    out=dst, in0=dst, in1=tf[:, :w], op=ALU.mult),
                    reads=[dst_h, tf], writes=[dst_h], cost=150)

        def layernorm(x_ap, out_ap, rh, wh, scr, tf, ti):
            E.op("dve", lambda: v.bn_stats(out=scr[:, 0:6], in_=x_ap[:, 0:512]),
                 reads=rh, writes=[scr], cost=600)
            E.op("dve", lambda: v.bn_stats(out=scr[:, 6:12], in_=x_ap[:, 512:1024]),
                 reads=rh, writes=[scr], cost=600)
            E.op("dve", lambda: v.bn_aggr(out=scr[:, 12:14], in_=scr[:, 0:12]),
                 reads=[scr], writes=[scr], cost=200)
            E.op("dve", lambda: v.tensor_scalar(
                out=scr[:, 14:15], in0=scr[:, 13:14], scalar1=1.0, scalar2=EPS,
                op0=ALU.mult, op1=ALU.add), reads=[scr], writes=[scr], cost=150)
            quake_rsqrt(scr[:, 15:16], scr, scr[:, 14:15], scr, tf, ti)
            E.op("dve", lambda: v.scalar_tensor_tensor(
                out=scr[:, 14:15], in0=scr[:, 12:13], scalar=-1.0, in1=scr[:, 15:16],
                op0=ALU.mult, op1=ALU.mult), reads=[scr], writes=[scr], cost=150)
            E.op("act", lambda: s.activation(out=out_ap[:, :], in_=x_ap[:, :], func=AF.Identity,
                                             scale=scr[:, 15:16], bias=scr[:, 14:15]),
                 reads=rh + [scr], writes=wh)

        def dmaT(dst, src, blocks, stream, eng="sp"):
            issuer = sy if eng == "sp" else s
            E.op(eng, lambda: issuer.dma_start(
                out=dst[:, :].rearrange("p (c t) -> p c t", c=blocks),
                in_=src[:, :].rearrange("t (c p) -> t c p", c=blocks),
                transpose=True), reads=[src], writes=[dst], stream=stream,
                cost=1400, lat=1500)

        E.op("pool", lambda: g.dma_start(out=triu[:, :], in_=triu_d[:, :]), writes=[triu], stream="c0")
        E.op("pool", lambda: g.dma_start(out=mask4[:, :], in_=mask4_d[:, :]), writes=[mask4], stream="c1")
        E.op("pool", lambda: g.dma_start(out=ones128[:, :], in_=ones128_d[:, :]), writes=[ones128], stream="c2")
        E.op("pool", lambda: g.dma_start(out=onescol[:, :], in_=onescol_d[:, :]), writes=[onescol], stream="c3")
        E.op("pool", lambda: g.dma_start(out=rolec[:, :], in_=role_d[:, :].broadcast_to([128, 1])),
             writes=[rolec], stream="c2")
        E.op("pool", lambda: g.dma_start(
            out=gnwc[:, :], in_=gnw_d[:, :].rearrange("o (c p) -> p (o c)", p=128),
            allow_slow_non_contiguous=True), writes=[gnwc], stream="c3")
        E.op("dve", lambda: v.memset(magic[:, :], MAGIC), writes=[magic])
        E.op("dve", lambda: v.memset(zeros1k[:, :], 0.0), writes=[zeros1k], cost=1200)
        for _i, _v in enumerate([-LN16, 1.0 / 16.0, -1.0 / 16.0, 1.0 / 32.0, -1.0 / 32.0, 0.5]):
            E.op("dve", lambda _i=_i, _v=_v: v.memset(ccols[:, _i:_i + 1], _v), writes=[ccols])
        C_NLN16 = ccols[:, 0:1]
        C_116 = ccols[:, 1:2]
        C_N116 = ccols[:, 2:3]
        C_132 = ccols[:, 3:4]
        C_N132 = ccols[:, 4:5]
        C_HALF = ccols[:, 5:6]
        with ExitStack() as esA:
            def sbA(name, shape, dtype, **kw):
                return esA.enter_context(nc.sbuf_tensor(name, shape, dtype, **kw))

            S8 = [sbA(f"S{i}", [128, 512], F32) for i in range(8)]
            S8b = [sbA(f"Sb{i}", [128, 512], BF16) for i in range(8)]
            for _i in range(8):
                E.op("dve", lambda _i=_i: v.memset(S8[_i][:, :], 0.0), writes=[S8[_i]])

            wq = sbA("wq", [128, 8 * 1024], BF16)
            wk = sbA("wk", [128, 8 * 1024], BF16)
            wv = sbA("wv", [128, 8 * 2048], BF16)
            wgk1 = sbA("wgk1", [128, 8 * 16], BF16)


            def load_w(dst, src, stream, blocks=8, lat=14000):
                E.op("pool", lambda: g.dma_start(
                    out=dst[:, :].rearrange("p (k n) -> p k n", k=blocks),
                    in_=src[:, :].rearrange("(k p) n -> p k n", p=128)),
                    writes=[dst], stream=stream, cost=1100, lat=lat)

            load_w(wk, wk_d, "wa2")
            load_w(wv, wv_d, "wa3", lat=28000)
            load_w(wgk1, wgk1_d, "wa2", lat=2000)
            wgk2x = sbA("wgk2x", [17, 1024], BF16)
            E.op("pool", lambda: g.dma_start(out=wgk2x[:, :], in_=wgk2x_d[:, :]),
                 writes=[wgk2x], stream="wa2", cost=650, lat=1500)
            A17_2 = [sbA(f"A17_{i}", [17, 128], BF16) for i in range(2)]
            for _i in range(2):
                E.op("pool", lambda _i=_i: g.dma_start(
                    out=A17_2[_i][16:17, :], in_=onesrow_d[:, :]),
                    writes=[A17_2[_i]], stream="c3", cost=650, lat=1500)
            load_w(wq, wq_d, "wa1")

            raws = sbA("raw0", [128, 1024], F32)
            raw2 = [raws, raws]
            min02 = [sbA(f"min0{i}", [128, 1024], BF16) for i in range(2)]
            FA52 = [sbA(f"FA5{i}", [128, 1024], F32) for i in range(2)]
            EcB2 = [sbA(f"EcB{i}", [128, 1024], F32) for i in range(2)]
            EnB2 = [sbA(f"EnB{i}", [128, 1024], F32) for i in range(2)]
            E1c2 = [sbA(f"E1c{i}", [128, 1024], BF16) for i in range(3)]
            Engc2 = [sbA(f"Engc{i}", [128, 1024], BF16) for i in range(3)]
            EnS2 = [sbA(f"EnS{i}", [128, 1024], BF16) for i in range(3)]
            gku2 = [sbA(f"gku{i}", [128, 1024], BF16) for i in range(3)]
            qg = sbA("qg", [128, 1024], BF16)
            ki = sbA("ki", [128, 1024], BF16)
            ks = sbA("ks", [128, 1024], BF16)
            qgfm = sbA("qgfm", [128, 1024], BF16, is_transpose_dest=True)
            kifm = sbA("kifm", [128, 1024], BF16, is_transpose_dest=True)
            vbf = sbA("vbf", [128, 2048], BF16)
            scbf = sbA("scbf", [128, 512], BF16)
            ospt = sbA("ospt", [128, 2048], BF16)


            def a_s0(i):
                c, src_d, full = descs[i]
                par = i % 4
                x_sb, hl = xt[par], hln[i % 2]
                E.op("sp", lambda x_sb=x_sb, c=c, src_d=src_d: sy.dma_start(
                    out=x_sb[:, :], in_=src_d[c * CH:(c + 1) * CH, :]),
                    writes=[x_sb], stream=f"x{par}", cost=700, lat=4500)
                layernorm(x_sb, hl, [x_sb], [hl], lnc, t8a, t8i)
                dmaT(hfm[par], hl, 8, f"th{par}")
                if full:
                    E.op("pool", lambda c=c, par=par: g.dma_start(
                        out=hsp_d[c * CH:(c + 1) * CH, :], in_=hfm[par][:, :]),
                        reads=[hfm[par]], writes=[hsp_d], stream=f"hs{c % 2}",
                        cost=650, lat=4500)

            def a_s1a(i):
                hf = hfm[i % 4]
                gku = gku2[i % 3]
                raw, min0, FA5 = raw2[i % 2], min02[i % 2], FA52[i % 2]
                for k in range(8):
                    E.op("pe", lambda k=k, hf=hf: t.matmul(
                        PB[1][:16, 0:128], wgk1[:, 16 * k:16 * (k + 1)],
                        hf[:, 128 * k:128 * (k + 1)],
                        start=(k == 0), stop=(k == 7)),
                        reads=[wgk1, hf], writes=[PBH[1]])
                A17 = A17_2[i % 2]
                E.op("act", lambda A17=A17: s.copy(out=A17[0:16, :], in_=PB[1][:16, 0:128]),
                     reads=[PBH[1]], writes=[A17], cost=200)
                for nh in range(2):
                    E.op("pe", lambda nh=nh, A17=A17: t.matmul(
                        PB[nh][:, :], A17[:, :], wgk2x[:, 512 * nh:512 * (nh + 1)],
                        start=True, stop=True),
                        reads=[A17, wgk2x], writes=[PBH[nh]])
                E.op("act", lambda: s.copy(out=raw[:, :], in_=PT[0][:, :]),
                     reads=PTH[0], writes=[raw], cost=1150)
                E.op("dve", lambda min0=min0, raw=raw: v.tensor_scalar_min(
                    out=min0[:, :], in0=raw[:, :], scalar1=0.0),
                    reads=[raw], writes=[min0])
                E.op("act", lambda: s.activation(out=FA5[:, :], in_=raw[:, :], func=AF.Abs),
                     reads=[raw], writes=[FA5])
                E.op("act", lambda: s.activation(out=raw[:, :], in_=FA5[:, :], func=AF.Exp, scale=-1.0),
                     reads=[FA5], writes=[raw])
                E.op("act", lambda: s.activation(out=FA5[:, :], in_=raw[:, :], func=AF.Ln, bias=1.0),
                     reads=[raw], writes=[FA5])
                E.op("pool", lambda gku=gku, min0=min0, FA5=FA5: g.tensor_tensor(
                    out=gku[:, :], in0=min0[:, :], in1=FA5[:, :], op=ALU.subtract),
                    reads=[min0, FA5], writes=[gku], cost=2350)

            def a_s1b(i):
                c, _, full = descs[i]
                gku = gku2[i % 3]
                E1c, Engc, EnS = E1c2[i % 3], Engc2[i % 3], EnS2[i % 3]
                if full:
                    for nh in range(2):
                        E.op("pe", lambda nh=nh, gku=gku: t.matmul(
                            PB[nh][:, :], triu[:, :], gku[:, 512 * nh:512 * (nh + 1)],
                            start=True, stop=True), reads=[triu, gku], writes=[PBH[nh]])
                    E.op("act", lambda Engc=Engc: s.activation(
                        out=Engc[:, :], in_=PT[0][:, :], func=AF.Exp,
                        scale=C_N116), reads=PTH[0] + [ccols], writes=[Engc])
                    E.op("act", lambda E1c=E1c: s.activation(
                        out=E1c[:, :], in_=PT[0][:, :], func=AF.Exp,
                        scale=C_116, bias=C_NLN16), reads=PTH[0] + [ccols], writes=[E1c])
                for nh in range(2):
                    E.op("pe", lambda nh=nh, gku=gku: t.matmul(
                        PB[nh][:, :], ones128[:, :], gku[:, 512 * nh:512 * (nh + 1)],
                        start=True, stop=True), reads=[ones128, gku], writes=[PBH[nh]])
                E.op("act", lambda EnS=EnS: s.activation(
                    out=EnS[:, :], in_=PT[0][:, :], func=AF.Exp,
                    scale=C_116), reads=PTH[0] + [ccols], writes=[EnS])
                for blk in range(8):
                    E.op("pe", lambda blk=blk, gku=gku: t.matmul(
                        PB[0][:, blk:blk + 1], gku[:, 128 * blk:128 * (blk + 1)], onescol[:, :],
                        start=True, stop=True), reads=[gku, onescol], writes=[PBH[0]])
                ebl, ecc = ebl4[i % 4], ecc4[i % 4]
                E.op("act", lambda ebl=ebl: s.activation(out=ebl[:, :], in_=PB[0][:, 0:8], func=AF.Exp,
                                                 scale=C_116),
                     reads=[PBH[0], ccols], writes=[ebl], cost=250)
                if full:
                    E.op("act", lambda ecc=ecc: s.activation(out=ecc[:, :], in_=PB[0][:, 0:8], func=AF.Exp,
                                                     scale=C_132),
                         reads=[PBH[0], ccols], writes=[ecc], cost=250)

            def a_s2(i):
                c, _, full = descs[i]
                hf = hfm[i % 4]
                E1c, Engc, EnS = E1c2[i % 3], Engc2[i % 3], EnS2[i % 3]
                for k in range(8):
                    for nh in range(2):
                        E.op("pe", lambda k=k, nh=nh, hf=hf: t.matmul(
                            PB[4 + nh][:, :], hf[:, 128 * k:128 * (k + 1)],
                            wk[:, 1024 * k + 512 * nh:1024 * k + 512 * (nh + 1)],
                            start=(k == 0), stop=(k == 7)),
                            reads=[hf, wk], writes=[PBH[4 + nh]])
                if full:
                    for k in range(8):
                        for nh in range(2):
                            E.op("pe", lambda k=k, nh=nh, hf=hf: t.matmul(
                                PB[6 + nh][:, :], hf[:, 128 * k:128 * (k + 1)],
                                wq[:, 1024 * k + 512 * nh:1024 * k + 512 * (nh + 1)],
                                start=(k == 0), stop=(k == 7)),
                                reads=[hf, wq], writes=[PBH[6 + nh]])
                    E.op("dve", lambda: v.tensor_tensor(
                        out=qg[:, :], in0=PT[3][:, :], in1=E1c[:, :], op=ALU.mult),
                        reads=PTH[3] + [E1c], writes=[qg])
                    E.op("dve", lambda: v.tensor_tensor(
                        out=ki[:, :], in0=PT[2][:, :], in1=Engc[:, :], op=ALU.mult),
                        reads=PTH[2] + [Engc], writes=[ki])
                E.op("dve", lambda: v.tensor_tensor(
                    out=ks[:, :], in0=PT[2][:, :], in1=EnS[:, :], op=ALU.mult),
                    reads=PTH[2] + [EnS], writes=[ks])
                if full:
                    dmaT(qgfm, qg, 8, "tq")
                    dmaT(kifm, ki, 8, "tk")
                for k in range(8):
                    for ns in range(4):
                        E.op("pe", lambda k=k, ns=ns, hf=hf: t.matmul(
                            PB[2 + ns][:, :], hf[:, 128 * k:128 * (k + 1)],
                            wv[:, 2048 * k + 512 * ns:2048 * k + 512 * (ns + 1)],
                            start=(k == 0), stop=(k == 7)),
                            reads=[hf, wv], writes=[PBH[2 + ns]])
                E.op("dve", lambda: v.tensor_copy(out=vbf[:, 0:1024], in_=PT[1][:, :]),
                     reads=PTH[1], writes=[vbf])
                E.op("act", lambda: s.copy(out=vbf[:, 1024:2048], in_=PT[2][:, :]),
                     reads=PTH[2], writes=[vbf])

            def a_s3(i):
                c, _, full = descs[i]
                ebl, ecc = ebl4[i % 4], ecc4[i % 4]
                if full:
                    for blk in range(8):
                        if blk % 4 == 0:
                            E.op("dve", lambda blk=blk, ecc=ecc: v.tensor_scalar_mul(
                                out=S8b[blk][:, :], in0=S8[blk][:, :],
                                scalar1=ecc[:, blk:blk + 1]),
                                reads=[S8[blk], ecc], writes=[S8b[blk]])
                        else:
                            E.op("act", lambda blk=blk, ecc=ecc: s.activation(
                                out=S8b[blk][:, :], in_=S8[blk][:, :], func=AF.Identity,
                                scale=ecc[:, blk:blk + 1]),
                                reads=[S8[blk], ecc], writes=[S8b[blk]])
                    for h in range(4):
                        for kc in range(2):
                            blk = 2 * h + kc
                            E.op("pe", lambda h=h, kc=kc, blk=blk: t.matmul(
                                PB[6][:, 128 * h:128 * (h + 1)],
                                kifm[:, 128 * blk:128 * (blk + 1)],
                                qgfm[:, 128 * blk:128 * (blk + 1)],
                                start=(kc == 0), stop=(kc == 1)),
                                reads=[kifm, qgfm], writes=[PBH[6]])
                    E.op("dve", lambda: v.tensor_tensor(out=scbf[:, :], in0=PB[6][:, :],
                                                        in1=mask4[:, :], op=ALU.mult),
                         reads=[PBH[6], mask4], writes=[scbf])
                    for h in range(4):
                        E.op("pe", lambda h=h: t.matmul(
                            PB[2 + h][:, :], scbf[:, 128 * h:128 * (h + 1)],
                            vbf[:, 512 * h:512 * (h + 1)], start=True, stop=False),
                            reads=[scbf, vbf], writes=[PBH[2 + h]])
                        for kc in range(2):
                            blk = 2 * h + kc
                            E.op("pe", lambda h=h, kc=kc, blk=blk: t.matmul(
                                PB[2 + h][:, :], qgfm[:, 128 * blk:128 * (blk + 1)],
                                S8b[blk][:, :],
                                start=False, stop=(kc == 1)),
                                reads=[qgfm, S8b[blk]], writes=[PBH[2 + h]])
                    E.op("dve", lambda: v.tensor_copy(out=ospt[:, 0:1024], in_=PT[1][:, :]),
                         reads=PTH[1], writes=[ospt])
                    E.op("act", lambda: s.copy(out=ospt[:, 1024:2048], in_=PT[2][:, :]),
                         reads=PTH[2], writes=[ospt])
                    E.op("pool", lambda c=c: g.dma_start(
                        out=osp_d[c * CH:(c + 1) * CH, :], in_=ospt[:, :]),
                        reads=[ospt], writes=[osp_d], stream="so",
                        cost=650, lat=4500)
                for h in range(4):
                    for mc in range(2):
                        blk = 2 * h + mc
                        pb = PB[6 + (blk % 2)]; pbh = PBH[6 + (blk % 2)]
                        E.op("pe", lambda blk=blk, h=h, pb=pb: t.matmul(
                            pb[:, :], ks[:, 128 * blk:128 * (blk + 1)],
                            vbf[:, 512 * h:512 * (h + 1)], start=True, stop=True),
                            reads=[ks, vbf], writes=[pbh])
                        E.op("dve", lambda blk=blk, pb=pb, ebl=ebl: v.scalar_tensor_tensor(
                            out=S8[blk][:, :],
                            in0=S8[blk][:, :],
                            scalar=ebl[:, blk:blk + 1], in1=pb[:, :],
                            op0=ALU.mult, op1=ALU.add),
                            reads=[S8[blk], ebl, pbh], writes=[S8[blk]])

            descs = [(c, xpre_d, False) for c in range(n_chunks)] + \
                    [(c, xin, True) for c in range(n_chunks)]
            N = len(descs)

            def a_mask(i_done):
                if i_done == n_chunks - 1:
                    for blk in range(8):
                        E.op("dve", lambda blk=blk: v.tensor_scalar_mul(
                            out=S8[blk][:, :], in0=S8[blk][:, :], scalar1=rolec[:, :]),
                            reads=[S8[blk], rolec], writes=[S8[blk]])
                    if debug:
                        for blk in range(8):
                            E.op("pool", lambda blk=blk: g.dma_start(
                                out=smid_d[:, 512 * blk:512 * (blk + 1)], in_=S8[blk][:, :]),
                                reads=[S8[blk]], writes=[smid_d], stream="so")

            for j in range(min(3, N)):
                a_s0(j)
                a_s1a(j)
            a_s1b(0)
            if N > 3:
                a_s0(3)
                a_s1a(3)
            if N > 1:
                a_s1b(1)
            for i in range(N):
                if i + 2 < N:
                    a_s1b(i + 2)
                if i >= 1:
                    a_s3(i - 1)
                    a_mask(i - 1)
                a_s2(i)
                if i + 4 < N:
                    a_s0(i + 4)
                    a_s1a(i + 4)
            a_s3(N - 1)
            a_mask(N - 1)

            if debug:
                for blk in range(8):
                    E.op("pool", lambda blk=blk: g.dma_start(
                        out=sdump_d[:, 512 * blk:512 * (blk + 1)], in_=S8[blk][:, :]),
                        reads=[S8[blk]], writes=[sdump_d], stream="so")

        E.barrier()

        with ExitStack() as esB:
            def sbB(name, shape, dtype, **kw):
                return esB.enter_context(nc.sbuf_tensor(name, shape, dtype, **kw))

            wg = sbB("wg", [128, 8 * 2048], BF16)
            wo = sbB("wo", [128, 16 * 1024], BF16)
            w8t = sbB("w8t", [128, 8 * 1024], BF16)
            wstage2 = [sbB(f"wstage{i}", [128, 1024], F32) for i in range(2)]
            onesrowt = sbB("onesrowt", [1, 128], BF16)
            biasrow = sbB("biasrow_sb", [1, 1024], BF16)
            E.op("pool", lambda: g.dma_start(out=onesrowt[:, :], in_=onesrow_d[:, :]),
                 writes=[onesrowt], stream="c0", cost=650, lat=1500)
            E.op("pool", lambda: g.dma_start(out=biasrow[:, :], in_=biasrow_d[:, :]),
                 writes=[biasrow], stream="c1", cost=650, lat=1500)
            x8_2 = [sbB(f"x8_{i}", [128, 1024], FP8) for i in range(2)]
            x8b_2 = [sbB(f"x8b_{i}", [128, 1024], BF16) for i in range(2)]

            for wh in range(2):
                E.op("pool", lambda wh=wh: g.dma_start(
                    out=wg[:, 8192 * wh:8192 * (wh + 1)].rearrange("p (k n) -> p k n", k=4),
                    in_=wg_d[512 * wh:512 * (wh + 1), :].rearrange("(k p) n -> p k n", p=128)),
                    writes=[wg], stream=("wa1", "wa2")[wh], cost=1100, lat=14000)
            for dc in range(16):
                ws = wstage2[dc % 2]
                E.op("pool", lambda dc=dc, ws=ws: g.dma_start(
                    out=ws[:, :], in_=wo_d[128 * dc:128 * (dc + 1), :]),
                    writes=[ws], stream=("wa3", "so")[dc % 2], cost=650, lat=4500)
                E.op("dve", lambda dc=dc, ws=ws: v.tensor_scalar_mul(
                    out=wo[:, 1024 * dc:1024 * (dc + 1)], in0=ws[:, :],
                    scalar1=gnwc[:, dc % 4:dc % 4 + 1]),
                    reads=[ws, gnwc], writes=[wo])
            for fc in range(8):
                ws = wstage2[fc % 2]
                w8q, w8b = x8_2[fc % 2], x8b_2[fc % 2]
                E.op("pool", lambda fc=fc, ws=ws: g.dma_start(
                    out=ws[:, :], in_=wf_d[128 * fc:128 * (fc + 1), :]),
                    writes=[ws], stream=("wa3", "so")[fc % 2], cost=650, lat=4500)
                E.op("dve", lambda ws=ws: v.tensor_scalar(
                    out=ws[:, :], in0=ws[:, :], scalar1=448.0, scalar2=-448.0,
                    op0=ALU.min, op1=ALU.max), reads=[ws], writes=[ws])
                E.op("dve", lambda ws=ws, w8q=w8q: v.tensor_copy(out=w8q[:, :], in_=ws[:, :]),
                     reads=[ws], writes=[w8q])
                E.op("act", lambda w8q=w8q, w8b=w8b: s.copy(out=w8b[:, :], in_=w8q[:, :]),
                     reads=[w8q], writes=[w8b])
                E.op("sp", lambda fc=fc, w8b=w8b: sy.dma_start(
                    out=w8t[:, :].rearrange("p (blk f) -> p blk f", blk=8)[:, :, 128 * fc:128 * (fc + 1)],
                    in_=w8b[:, :].rearrange("f (blk p) -> f blk p", blk=8),
                    transpose=True), reads=[w8b], writes=[w8t], stream=("tq", "tk")[fc % 2],
                    cost=1400, lat=1500)

            G1_2 = [sbB(f"G1_{i}", [128, 2048], BF16) for i in range(3)]
            G2s = sbB("G2_0", [128, 2048], BF16)
            G2_2 = [G2s, G2s]
            ogs = sbB("og0", [128, 2048], BF16)
            og2 = [ogs, ogs]
            gate2 = [sbB(f"gate{i}", [128, 2048], BF16) for i in range(3)]
            ogfm2 = [sbB(f"ogfm{i}", [128, 2048], BF16, is_transpose_dest=True) for i in range(2)]
            oloc = [sbB(f"oloc{i}", [128, 2048], BF16) for i in range(4)]
            x8fms = sbB("x8fm0", [128, 1024], BF16, is_transpose_dest=True)
            x8fm2 = [x8fms, x8fms]
            racc2 = [sbB(f"racc{i}", [128, 4], F32) for i in range(2)]
            rstd2 = [sbB(f"rstd{i}", [128, 4], F32) for i in range(2)]
            XM2 = [sbB(f"XM{i}", [128, 1024], F32) for i in range(2)]
            XLns = sbB("XLn0", [128, 1024], F32)
            XLn2 = [XLns, XLns]
            OUT2 = wstage2

            def b_t0(c):
                par = c % 4
                x_sb = xt[par]
                E.op("sp", lambda c=c, x_sb=x_sb: sy.dma_start(
                    out=x_sb[:, :], in_=xin[c * CH:(c + 1) * CH, :]),
                    writes=[x_sb], stream=f"x{par}", cost=700, lat=4500)
                E.op("pool", lambda c=c: g.dma_start(
                    out=oloc[c % 4][:, :], in_=osp_d[c * CH:(c + 1) * CH, :]),
                    reads=[osp_d], writes=[oloc[c % 4]], stream=f"ol{c % 4}",
                    cost=650, lat=4500),
                E.op("sp", lambda c=c, par=par: sy.dma_start(
                    out=hfm[par][:, :], in_=hsp_d[c * CH:(c + 1) * CH, :]),
                    reads=[hsp_d], writes=[hfm[par]], stream=f"th{par % 3}",
                    cost=700, lat=4500)

            def b_t1a(c):
                hf = hfm[c % 4]
                G1 = G1_2[c % 3]
                for k in range(8):
                    for ns in range(4):
                        E.op("pe", lambda k=k, ns=ns, hf=hf: t.matmul(
                            PB[ns][:, :], hf[:, 128 * k:128 * (k + 1)],
                            wg[:, 2048 * k + 512 * ns:2048 * k + 512 * (ns + 1)],
                            start=(k == 0), stop=(k == 7)),
                            reads=[hf, wg], writes=[PBH[ns]])
                E.op("act", lambda G1=G1: s.copy(out=G1[:, 0:1024], in_=PT[0][:, :]),
                     reads=PTH[0], writes=[G1])
                E.op("act", lambda G1=G1: s.copy(out=G1[:, 1024:2048], in_=PT[1][:, :]),
                     reads=PTH[1], writes=[G1])

            def b_t1b(c):
                G1, G2 = G1_2[c % 3], G2_2[c % 2]
                E.op("act", lambda G1=G1, G2=G2: s.activation(out=G2[:, :], in_=G1[:, :], func=AF.Tanh,
                                                 scale=C_HALF),
                     reads=[G1, ccols], writes=[G2], cost=2100)
                gb = gate2[c % 3]
                E.op("dve", lambda gb=gb, G1=G1, G2=G2: v.scalar_tensor_tensor(
                    out=gb[:, :], in0=G2[:, :], scalar=1.0, in1=G1[:, :],
                    op0=ALU.add, op1=ALU.mult),
                    reads=[G1, G2], writes=[gb], cost=1300)

            def b_t2(c):
                ol = oloc[c % 4]
                racc, rstd = racc2[c % 2], rstd2[c % 2]
                og, ogfm = og2[c % 2], ogfm2[c % 2]
                gb = gate2[c % 3]
                XM = XM2[c % 2]
                x8, x8b = x8_2[c % 2], x8b_2[c % 2]
                x8fm = x8fm2[c % 2]
                for h in range(4):
                    E.op("act", lambda h=h, ol=ol, racc=racc: s.activation(
                        out=dump[:, :], in_=ol[:, 512 * h:512 * (h + 1)], func=AF.Square,
                        accum_out=racc[:, h:h + 1]),
                        reads=[ol], writes=[dump, racc])
                E.op("dve", lambda racc=racc: v.tensor_scalar(
                    out=racc[:, :], in0=racc[:, :], scalar1=4.0 / 512.0, scalar2=4.0 * EPS,
                    op0=ALU.mult, op1=ALU.add), reads=[racc], writes=[racc])
                quake_rsqrt(rstd[:, :], rstd, racc[:, :], racc, t8e, t8f)
                for h in range(4):
                    E.op("dve", lambda h=h, ol=ol, og=og, rstd=rstd, gb=gb: v.scalar_tensor_tensor(
                        out=og[:, 512 * h:512 * (h + 1)], in0=ol[:, 512 * h:512 * (h + 1)],
                        scalar=rstd[:, h:h + 1], in1=gb[:, 512 * h:512 * (h + 1)],
                        op0=ALU.mult, op1=ALU.mult),
                        reads=[ol, rstd, gb], writes=[og])
                dmaT(ogfm, og, 16, f"to{c % 2}")
                for dc in range(16):
                    for nh in range(2):
                        E.op("pe", lambda dc=dc, nh=nh, ogfm=ogfm: t.matmul(
                            PB[4 + nh][:, :], ogfm[:, 128 * dc:128 * (dc + 1)],
                            wo[:, 1024 * dc + 512 * nh:1024 * dc + 512 * (nh + 1)],
                            start=(dc == 0), stop=(dc == 15)),
                            reads=[ogfm, wo], writes=[PBH[4 + nh]])
                E.op("dve", lambda c=c, XM=XM: v.tensor_tensor(
                    out=XM[:, :], in0=PT[2][:, :], in1=xt[c % 4][:, :], op=ALU.add),
                    reads=PTH[2] + [xt[c % 4]], writes=[XM])
                XLn = XLn2[c % 2]
                layernorm(XM, XLn, [XM], [XLn], lnc2, t8c, t8j)
                E.op("dve", lambda XLn=XLn, x8=x8: v.tensor_copy(out=x8[:, :], in_=XLn[:, :]),
                     reads=[XLn], writes=[x8])
                E.op("act", lambda x8=x8, x8b=x8b: s.copy(out=x8b[:, :], in_=x8[:, :]),
                     reads=[x8], writes=[x8b])
                dmaT(x8fm, x8b, 8, f"t8{c % 2}")

            def b_t3(c):
                XM = XM2[c % 2]
                OUT = OUT2[c % 2]
                x8fm = x8fm2[c % 2]
                for nh in range(2):
                    E.op("pe", lambda nh=nh: t.matmul(
                        PB[6 + nh][:, :], onesrowt[:, :],
                        biasrow[:, 512 * nh:512 * (nh + 1)],
                        start=True, stop=False),
                        reads=[onesrowt, biasrow], writes=[PBH[6 + nh]], cost=120)
                for k in range(8):
                    for nh in range(2):
                        E.op("pe", lambda k=k, nh=nh, x8fm=x8fm: t.matmul(
                            PB[6 + nh][:, :], x8fm[:, 128 * k:128 * (k + 1)],
                            w8t[:, 1024 * k + 512 * nh:1024 * k + 512 * (nh + 1)],
                            start=False, stop=(k == 7)),
                            reads=[x8fm, w8t], writes=[PBH[6 + nh]])
                E.op("dve", lambda XM=XM, OUT=OUT: v.tensor_tensor(
                    out=OUT[:, :], in0=PT[3][:, :], in1=XM[:, :], op=ALU.add),
                    reads=PTH[3] + [XM], writes=[OUT])
                E.op("pool", lambda c=c, OUT=OUT: g.dma_start(
                    out=y[c * CH:(c + 1) * CH, :], in_=OUT[:, :]),
                    reads=[OUT], writes=[y], stream=f"y{c % 2}",
                    cost=650, lat=4500)

            for j in range(min(3, n_chunks)):
                b_t0(j)
            for j in range(min(3, n_chunks)):
                b_t1a(j)
            for j in range(min(2, n_chunks)):
                b_t1b(j)
            for c in range(n_chunks):
                if c + 3 < n_chunks:
                    b_t0(c + 3)
                    b_t1a(c + 3)
                if c + 2 < n_chunks:
                    b_t1b(c + 2)
                if c >= 1:
                    b_t3(c - 1)
                b_t2(c)
            b_t3(n_chunks - 1)

        block = es.enter_context(nc.Block())
        handles = {"pe": nc.tensor, "act": nc.scalar, "dve": nc.vector,
                   "pool": nc.gpsimd, "sp": nc.sync}
        E.emit(block, lambda name: es.enter_context(nc.semaphore(f"sem_{name}")), handles)

    return nc


def _consts():
    tri = np.triu(np.ones((128, 128), dtype=np.float32))
    mask4 = np.tile(tri, (1, 4)).astype(ml_dtypes.bfloat16)
    mmat = (tri - 0.5).astype(ml_dtypes.bfloat16)
    nmat = (1.0 - tri).astype(ml_dtypes.bfloat16)
    onescol = np.ones((128, 1), dtype=ml_dtypes.bfloat16)
    return mmat, mask4, nmat, onescol


def make_in_maps(x, inputs, n_chunks=N_CHUNKS, n_cores=8):
    tri, mask4, ones128, onescol = _consts()
    seg = n_chunks * CH
    common = dict(
        Wq=np.asarray(inputs["L0_attn_Wq"], np.float32),
        Wk=np.asarray(inputs["L0_attn_Wk"], np.float32),
        Wv=np.asarray(inputs["L0_attn_Wv"], np.float32),
        Wg=np.asarray(inputs["L0_attn_Wg"], np.float32),
        Wgk1=np.asarray(inputs["L0_attn_Wgk1"], np.float32),
        Wgk2x=np.vstack([
            np.asarray(inputs["L0_attn_Wgk2"], np.float32),
            np.asarray(inputs["L0_attn_bgk2"], np.float32).reshape(1, 1024),
        ]).astype(ml_dtypes.bfloat16),
        onesrow=np.ones((1, 128), dtype=ml_dtypes.bfloat16),
        biasrow=np.asarray(inputs["L0_ffn_bias"], np.float32).reshape(1, 1024).astype(ml_dtypes.bfloat16),
        gnw=np.asarray(inputs["L0_attn_g_norm_w"], np.float32).reshape(1, 512),
        Wo=np.asarray(inputs["L0_attn_Wo"], np.float32),
        Wffn=np.asarray(inputs["L0_ffn_weight"], np.float32),
        bffn=np.asarray(inputs["L0_ffn_bias"], np.float32).reshape(1, 1024),
        triu=tri, mask4=mask4, ones128=ones128, onescol=onescol,
    )
    zeros = np.zeros((seg, HID), np.float32)
    in_maps = []
    for core in range(n_cores):
        b, h = core // 2, core % 2
        m = dict(common)
        m["x_seg"] = np.ascontiguousarray(x[b, h * seg:(h + 1) * seg, :])
        m["x_pre"] = np.ascontiguousarray(x[b, 0:seg, :]) if h == 1 else zeros
        m["role"] = np.array([[float(h)]], np.float32)
        in_maps.append(m)
    return in_maps


def kernel(L0_attn_hidden_states, L0_attn_Wq, L0_attn_Wk, L0_attn_Wv, L0_attn_Wg,
           L0_attn_Wgk1, L0_attn_Wgk2, L0_attn_bgk2, L0_attn_g_norm_w, L0_attn_Wo,
           L0_ffn_weight, L0_ffn_bias):
    x = np.asarray(L0_attn_hidden_states, dtype=np.float32)
    assert x.shape == (4, 4096, 1024)
    inputs = dict(
        L0_attn_Wq=L0_attn_Wq, L0_attn_Wk=L0_attn_Wk, L0_attn_Wv=L0_attn_Wv,
        L0_attn_Wg=L0_attn_Wg, L0_attn_Wgk1=L0_attn_Wgk1, L0_attn_Wgk2=L0_attn_Wgk2,
        L0_attn_bgk2=L0_attn_bgk2, L0_attn_g_norm_w=L0_attn_g_norm_w,
        L0_attn_Wo=L0_attn_Wo, L0_ffn_weight=L0_ffn_weight, L0_ffn_bias=L0_ffn_bias,
    )
    nc = build_program()
    in_maps = make_in_maps(x, inputs)
    res = run_bass_kernel_spmd(nc, in_maps, core_ids=list(range(8)))
    out = np.empty((4, 4096, 1024), np.float32)
    for core in range(8):
        b, h = core // 2, core % 2
        out[b, h * 2048:(h + 1) * 2048, :] = res.results[core]["y_seg"]
    return out


# version 33
# speedup vs baseline: 1.0170x; 1.0170x over previous
import math
import numpy as np
import ml_dtypes

import concourse.bass as bass
import concourse.mybir as mybir
from concourse.bass_utils import run_bass_kernel_spmd

F32 = mybir.dt.float32
BF16 = mybir.dt.bfloat16
FP8 = mybir.dt.float8e4
I32 = mybir.dt.int32
AF = mybir.ActivationFunctionType
ALU = mybir.AluOpType

HID = 1024
DV = 2048
CH = 128
N_CHUNKS = 16
EPS = 1e-5
LN16 = math.log(16.0)
MAGIC = 0x5F3759DF

ENGINES = ("pe", "act", "dve", "pool", "sp")


class Emitter:

    DUR = {"pe": 240, "dve": 900, "act": 1150, "pool": 2400, "sp": 1000}
    XLAT = 380

    def __init__(self, nc):
        self.nc = nc
        self.ops = []
        self.last_write = {}
        self.readers = {}
        self.last_on = {}
        self.regions = []

    @staticmethod
    def _key(b):
        if isinstance(b, str):
            return b
        t = getattr(b, "tensor", b)
        n = getattr(t, "name", None)
        return n if n is not None else id(t)

    def op(self, eng, fn, reads=(), writes=(), stream=None, inc=1, cost=None,
           lat=0):
        import sys
        line = sys._getframe(1).f_lineno
        idx = len(self.ops)
        deps = set()
        if stream is not None:
            inc = 16
        rkeys = [self._key(b) for b in reads]
        wkeys = [self._key(b) for b in writes]
        for kb in rkeys:
            lw = self.last_write.get(kb)
            if lw is not None:
                deps.add(lw)
        for kb in wkeys:
            lw = self.last_write.get(kb)
            if lw is not None:
                deps.add(lw)
            for r in self.readers.get(kb, ()):
                deps.add(r)
        sig = stream if stream is not None else eng
        if stream is not None and sig in self.last_on:
            deps.add(self.last_on[sig])
        if stream is not None:
            self.last_on[sig] = idx
        for kb in rkeys:
            self.readers.setdefault(kb, []).append(idx)
        for kb in wkeys:
            self.last_write[kb] = idx
            self.readers[kb] = []
        self.ops.append({"eng": eng, "fn": fn, "sig": sig, "inc": inc,
                         "deps": deps, "cost": cost or self.DUR[eng],
                         "lat": lat, "line": line})

    def barrier(self):
        self.regions.append(len(self.ops))

    def _schedule(self, lo, hi):
        import heapq
        ops = self.ops
        indeg = {}
        succ = {i: [] for i in range(lo, hi)}
        for i in range(lo, hi):
            d = [j for j in ops[i]["deps"] if lo <= j < hi]
            indeg[i] = len(d)
            for j in d:
                succ[j].append(i)
        finish = {j: 0 for j in range(lo)}
        engine_time = {e: 0 for e in ENGINES}
        bind = {}
        blevel = {}
        for i in range(hi - 1, lo - 1, -1):
            blevel[i] = ops[i]["cost"] + ops[i]["lat"] + max(
                (blevel[k] for k in succ[i]), default=0)
        ready = []
        for i in range(lo, hi):
            if indeg[i] == 0:
                heapq.heappush(ready, (0, -blevel[i], i))
        order = {e: [] for e in ENGINES}
        pops = []
        scheduled = 0
        while ready:
            _, _, i = heapq.heappop(ready)
            pops.append(i)
            o = ops[i]
            e = o["eng"]
            dep_t = 0
            dep_i = None
            for j in o["deps"]:
                if j < lo:
                    continue
                f = finish.get(j)
                if ops[j]["eng"] != e:
                    lat = self.XLAT
                elif e == "pe":
                    lat = 0
                else:
                    lat = 150
                if f + lat > dep_t:
                    dep_t = f + lat
                    dep_i = j
            if dep_t > engine_time[e]:
                bind[i] = ("dep", dep_i)
            else:
                bind[i] = ("queue", order[e][-1] if order[e] else None)
            st = max(engine_time[e], dep_t)
            fin = st + o["cost"]
            engine_time[e] = fin
            finish[i] = fin + o["lat"]
            order[e].append(i)
            scheduled += 1
            for k in succ[i]:
                indeg[k] -= 1
                if indeg[k] == 0:
                    dt = max((finish.get(j, 0) for j in ops[k]["deps"] if j >= lo),
                             default=0)
                    heapq.heappush(ready, (dt, -blevel[k], k))
        assert scheduled == hi - lo, (scheduled, hi - lo)
        busy = {e: 0 for e in ENGINES}
        for i in range(lo, hi):
            busy[ops[i]["eng"]] += ops[i]["cost"]
        print(f"[sched] region {lo}:{hi} predicted makespan "
              f"{max(engine_time.values()) / 1e3:.1f}us "
              + " ".join(f"{e}={engine_time[e] / 1e3:.0f}" for e in ENGINES)
              + " | busy "
              + " ".join(f"{e}={busy[e] / 1e3:.0f}" for e in ENGINES))
        import os
        if os.environ.get("SCHED_CP"):
            cur = max((i for i in range(lo, hi)), key=lambda i: finish[i])
            agg = {}
            while cur is not None and cur >= lo:
                o = ops[cur]
                kind, prev = bind.get(cur, ("dep", None))
                k = (o["eng"], o["line"], kind)
                a = agg.setdefault(k, [0, 0])
                a[0] += 1
                a[1] += o["cost"] + (o["lat"] if kind == "dep" else 0)
                cur = prev
            for k, (n, tot) in sorted(agg.items(), key=lambda kv: -kv[1][1])[:16]:
                print(f"  [cp] {k[0]:4s} line={k[1]} {k[2]:5s} n={n:4d} "
                      f"tot={tot / 1e3:7.1f}us")
        self._last_pops = pops
        return order

    def emit(self, block, make_sem, handles):
        regions = self.regions + [len(self.ops)]
        region_lists = []
        count_of = {}
        counts = {}
        snapshots = []
        lo = 0
        self._pop_lists = []
        for hi in regions:
            order = self._schedule(lo, hi)
            self._pop_lists.append(self._last_pops)
            region_lists.append(order)
            for e in ENGINES:
                for i in order[e]:
                    sig = self.ops[i]["sig"]
                    counts[sig] = counts.get(sig, 0) + self.ops[i]["inc"]
                    count_of[i] = counts[sig]
            snapshots.append(dict(counts))
            lo = hi
        sems = {name: make_sem(name) for name in counts}
        final = dict(counts)

        ops = self.ops
        KNOW = {}
        KS = {}
        hw = {e: {} for e in ENGINES}
        fence_waits = {}
        for ri, pops in enumerate(self._pop_lists):
            if ri > 0:
                snap = snapshots[ri - 1]
                for e in ENGINES:
                    fw = []
                    h = hw[e]
                    for sig2, c in sorted(snap.items(), key=lambda kv: str(kv[0])):
                        if sig2 != e and c > h.get(sig2, 0):
                            fw.append((sig2, c))
                        if h.get(sig2, 0) < c:
                            h[sig2] = c
                    fence_waits[(ri, e)] = fw
            for i in pops:
                o = ops[i]
                e = o["eng"]
                needed = {}
                for j in o["deps"]:
                    sj = ops[j]["sig"]
                    if sj == "pe" and e == "pe":
                        continue
                    v = count_of[j]
                    if v > needed.get(sj, 0):
                        needed[sj] = v
                emitted = []
                h = hw[e]
                pend = {s: v for s, v in needed.items() if v > h.get(s, 0)}
                while pend:
                    if len(pend) == 1:
                        sig2, v = next(iter(pend.items()))
                    else:
                        def cover(it):
                            s0, v0 = it
                            kn = KNOW.get((s0, v0), {})
                            return sum(1 for s1, v1 in pend.items()
                                       if kn.get(s1, 0) >= v1)
                        sig2, v = max(sorted(pend.items(), key=lambda kv: str(kv[0])),
                                      key=cover)
                    emitted.append((sig2, v))
                    for s2, c2 in KNOW.get((sig2, v), {sig2: v}).items():
                        if h.get(s2, 0) < c2:
                            h[s2] = c2
                    if h.get(sig2, 0) < v:
                        h[sig2] = v
                    pend = {s: vv for s, vv in pend.items() if vv > h.get(s, 0)}
                o["waits_final"] = emitted
                ki = dict(h)
                own = count_of[i]
                if ki.get(o["sig"], 0) < own:
                    ki[o["sig"]] = own
                prev = KS.get(o["sig"])
                if prev:
                    for s2, c2 in prev.items():
                        if ki.get(s2, 0) < c2:
                            ki[s2] = c2
                KS[o["sig"]] = ki
                KNOW[(o["sig"], own)] = ki
        nw = sum(len(o.get("waits_final", ())) for o in ops)
        print(f"[emit] waits after transitive elimination: {nw}")

        def run_stream(eng):
            eh = handles[eng]
            for ri, order in enumerate(region_lists):
                if ri > 0:
                    for sig2, c in fence_waits[(ri, eng)]:
                        eh.wait_ge(sems[sig2], c)
                for i in order[eng]:
                    o = self.ops[i]
                    for sig2, vv in o["waits_final"]:
                        eh.wait_ge(sems[sig2], vv)
                    o["fn"]().then_inc(sems[o["sig"]], o["inc"])
            for sig2, c in sorted(final.items(), key=lambda kv: str(kv[0])):
                if sig2 != eng and c > hw[eng].get(sig2, 0):
                    eh.wait_ge(sems[sig2], c)

        for eng in ENGINES:
            deco = {"pe": block.tensor, "act": block.scalar,
                    "dve": block.vector, "pool": block.gpsimd,
                    "sp": block.sync}[eng]

            def body(handle, _eng=eng):
                run_stream(_eng)
            deco(body)


def build_program(n_chunks=N_CHUNKS, debug=False):
    nc = bass.Bass("TRN2", target_bir_lowering=False)
    SEG = n_chunks * CH

    xin = nc.dram_tensor("x_seg", [SEG, HID], F32, kind="ExternalInput")
    xpre_d = nc.dram_tensor("x_pre", [SEG, HID], F32, kind="ExternalInput")
    wq_d = nc.dram_tensor("Wq", [HID, 1024], F32, kind="ExternalInput")
    wk_d = nc.dram_tensor("Wk", [HID, 1024], F32, kind="ExternalInput")
    wv_d = nc.dram_tensor("Wv", [HID, 2048], F32, kind="ExternalInput")
    wg_d = nc.dram_tensor("Wg", [HID, 2048], F32, kind="ExternalInput")
    wgk1_d = nc.dram_tensor("Wgk1", [HID, 16], F32, kind="ExternalInput")
    wgk2x_d = nc.dram_tensor("Wgk2x", [17, 1024], BF16, kind="ExternalInput")
    onesrow_d = nc.dram_tensor("onesrow", [1, 128], BF16, kind="ExternalInput")
    biasrow_d = nc.dram_tensor("biasrow", [1, 1024], BF16, kind="ExternalInput")
    gnw_d = nc.dram_tensor("gnw", [1, 512], F32, kind="ExternalInput")
    wo_d = nc.dram_tensor("Wo", [2048, 1024], F32, kind="ExternalInput")
    wf_d = nc.dram_tensor("Wffn", [1024, 1024], F32, kind="ExternalInput")
    bf_d = nc.dram_tensor("bffn", [1, 1024], F32, kind="ExternalInput")
    role_d = nc.dram_tensor("role", [1, 1], F32, kind="ExternalInput")
    triu_d = nc.dram_tensor("triu", [128, 128], BF16, kind="ExternalInput")
    mask4_d = nc.dram_tensor("mask4", [128, 512], BF16, kind="ExternalInput")
    ones128_d = nc.dram_tensor("ones128", [128, 128], BF16, kind="ExternalInput")
    onescol_d = nc.dram_tensor("onescol", [128, 1], BF16, kind="ExternalInput")

    y = nc.dram_tensor("y_seg", [SEG, HID], F32, kind="ExternalOutput")
    spill_kind = "ExternalOutput" if debug else "Internal"
    osp_d = nc.dram_tensor("o_sp", [SEG, DV], BF16, kind=spill_kind)
    hsp_d = nc.dram_tensor("h_sp", [SEG, HID], BF16, kind=spill_kind)
    if debug:
        sdump_d = nc.dram_tensor("s_dump", [128, 4096], F32, kind="ExternalOutput")
        smid_d = nc.dram_tensor("smid_dump", [128, 4096], F32, kind="ExternalOutput")

    from contextlib import ExitStack
    with ExitStack() as es:
        def sb(name, shape, dtype, **kw):
            return es.enter_context(nc.sbuf_tensor(name, shape, dtype, **kw))

        triu = sb("triu_sb", [128, 128], BF16)
        mask4 = sb("mask4_sb", [128, 512], BF16)
        ones128 = sb("ones128_sb", [128, 128], BF16)
        onescol = sb("onescol_sb", [128, 1], BF16)
        gnwc = sb("gnwc", [128, 4], F32)
        rolec = sb("rolec", [128, 1], F32)
        magic = sb("magic", [128, 4], I32)
        ebl4 = [sb(f"ebl{i}", [128, 8], F32) for i in range(4)]
        ecc4 = [sb(f"ecc{i}", [128, 8], F32) for i in range(4)]
        t8a = sb("t8a", [128, 4], F32)
        t8i = sb("t8i", [128, 4], I32)
        t8c = sb("t8c", [128, 4], F32)
        t8j = sb("t8j", [128, 4], I32)
        t8e = sb("t8e", [128, 4], F32)
        t8f = sb("t8f", [128, 4], I32)
        lnc = sb("lnc", [128, 16], F32)
        lnc2 = sb("lnc2", [128, 16], F32)
        ccols = sb("ccols", [128, 8], F32)
        zeros1k = sb("zeros1k", [128, 1024], F32)
        dump = sb("dump", [128, 512], F32)

        xt = [sb(f"xt{i}", [128, 1024], F32) for i in range(4)]
        hln = [sb(f"hln{i}", [128, 1024], BF16) for i in range(2)]
        hfm = [sb(f"hfm{i}", [128, 1024], BF16, is_transpose_dest=True) for i in range(4)]

        PT = [es.enter_context(nc.psum_tensor(f"PT{i}", [128, 1024], F32)) for i in range(4)]
        def PBv(i):
            return PT[i // 2][:, 512 * (i % 2):512 * (i % 2 + 1)]
        PB = [PBv(i) for i in range(8)]
        PBH = [f"PT{i // 2}:{i % 2}" for i in range(8)]
        PTH = [[f"PT{j}:0", f"PT{j}:1"] for j in range(4)]

        E = Emitter(nc)
        v = nc.vector
        s = nc.scalar
        t = nc.tensor
        g = nc.gpsimd
        sy = nc.sync

        def quake_rsqrt(dst, dst_h, src, src_h, tf, ti, n_iter=1):
            w = dst.shape[1]
            E.op("dve", lambda: v.tensor_single_scalar(
                out=ti[:, :w], in_=src.bitcast(I32), scalar=1,
                op=ALU.logical_shift_right), reads=[src_h], writes=[ti], cost=150)
            E.op("dve", lambda: v.tensor_tensor(
                out=dst.bitcast(I32), in0=magic[:, :w], in1=ti[:, :w],
                op=ALU.subtract), reads=[magic, ti], writes=[dst_h], cost=150)
            for _ in range(n_iter):
                E.op("dve", lambda: v.tensor_tensor(
                    out=tf[:, :w], in0=src, in1=dst, op=ALU.mult),
                    reads=[src_h, dst_h], writes=[tf], cost=150)
                E.op("dve", lambda: v.tensor_tensor(
                    out=tf[:, :w], in0=tf[:, :w], in1=dst, op=ALU.mult),
                    reads=[tf, dst_h], writes=[tf], cost=150)
                E.op("dve", lambda: v.tensor_scalar(
                    out=tf[:, :w], in0=tf[:, :w], scalar1=-0.5,
                    scalar2=1.5, op0=ALU.mult, op1=ALU.add),
                    reads=[tf], writes=[tf], cost=150)
                E.op("dve", lambda: v.tensor_tensor(
                    out=dst, in0=dst, in1=tf[:, :w], op=ALU.mult),
                    reads=[dst_h, tf], writes=[dst_h], cost=150)

        def layernorm(x_ap, out_ap, rh, wh, scr, tf, ti):
            E.op("dve", lambda: v.bn_stats(out=scr[:, 0:6], in_=x_ap[:, 0:512]),
                 reads=rh, writes=[scr], cost=600)
            E.op("dve", lambda: v.bn_stats(out=scr[:, 6:12], in_=x_ap[:, 512:1024]),
                 reads=rh, writes=[scr], cost=600)
            E.op("dve", lambda: v.bn_aggr(out=scr[:, 12:14], in_=scr[:, 0:12]),
                 reads=[scr], writes=[scr], cost=200)
            E.op("dve", lambda: v.tensor_scalar(
                out=scr[:, 14:15], in0=scr[:, 13:14], scalar1=1.0, scalar2=EPS,
                op0=ALU.mult, op1=ALU.add), reads=[scr], writes=[scr], cost=150)
            quake_rsqrt(scr[:, 15:16], scr, scr[:, 14:15], scr, tf, ti)
            E.op("dve", lambda: v.scalar_tensor_tensor(
                out=scr[:, 14:15], in0=scr[:, 12:13], scalar=-1.0, in1=scr[:, 15:16],
                op0=ALU.mult, op1=ALU.mult), reads=[scr], writes=[scr], cost=150)
            E.op("act", lambda: s.activation(out=out_ap[:, :], in_=x_ap[:, :], func=AF.Identity,
                                             scale=scr[:, 15:16], bias=scr[:, 14:15]),
                 reads=rh + [scr], writes=wh)

        def dmaT(dst, src, blocks, stream, eng="sp"):
            issuer = sy if eng == "sp" else s
            E.op(eng, lambda: issuer.dma_start(
                out=dst[:, :].rearrange("p (c t) -> p c t", c=blocks),
                in_=src[:, :].rearrange("t (c p) -> t c p", c=blocks),
                transpose=True), reads=[src], writes=[dst], stream=stream,
                cost=1400, lat=1500)

        E.op("pool", lambda: g.dma_start(out=triu[:, :], in_=triu_d[:, :]), writes=[triu], stream="c0")
        E.op("pool", lambda: g.dma_start(out=mask4[:, :], in_=mask4_d[:, :]), writes=[mask4], stream="c1")
        E.op("pool", lambda: g.dma_start(out=ones128[:, :], in_=ones128_d[:, :]), writes=[ones128], stream="c2")
        E.op("pool", lambda: g.dma_start(out=onescol[:, :], in_=onescol_d[:, :]), writes=[onescol], stream="c3")
        E.op("pool", lambda: g.dma_start(out=rolec[:, :], in_=role_d[:, :].broadcast_to([128, 1])),
             writes=[rolec], stream="c2")
        E.op("pool", lambda: g.dma_start(
            out=gnwc[:, :], in_=gnw_d[:, :].rearrange("o (c p) -> p (o c)", p=128),
            allow_slow_non_contiguous=True), writes=[gnwc], stream="c3")
        E.op("dve", lambda: v.memset(magic[:, :], MAGIC), writes=[magic])
        E.op("dve", lambda: v.memset(zeros1k[:, :], 0.0), writes=[zeros1k], cost=1200)
        for _i, _v in enumerate([-LN16, 1.0 / 16.0, -1.0 / 16.0, 1.0 / 32.0, -1.0 / 32.0, 0.5]):
            E.op("dve", lambda _i=_i, _v=_v: v.memset(ccols[:, _i:_i + 1], _v), writes=[ccols])
        C_NLN16 = ccols[:, 0:1]
        C_116 = ccols[:, 1:2]
        C_N116 = ccols[:, 2:3]
        C_132 = ccols[:, 3:4]
        C_N132 = ccols[:, 4:5]
        C_HALF = ccols[:, 5:6]
        with ExitStack() as esA:
            def sbA(name, shape, dtype, **kw):
                return esA.enter_context(nc.sbuf_tensor(name, shape, dtype, **kw))

            S8 = [sbA(f"S{i}", [128, 512], F32) for i in range(8)]
            S8b = [sbA(f"Sb{i}", [128, 512], BF16) for i in range(8)]
            for _i in range(8):
                E.op("dve", lambda _i=_i: v.memset(S8[_i][:, :], 0.0), writes=[S8[_i]])

            wq = sbA("wq", [128, 8 * 1024], BF16)
            wk = sbA("wk", [128, 8 * 1024], BF16)
            wv = sbA("wv", [128, 8 * 2048], BF16)
            wgk1 = sbA("wgk1", [128, 8 * 16], BF16)


            def load_w(dst, src, stream, blocks=8, lat=14000):
                E.op("pool", lambda: g.dma_start(
                    out=dst[:, :].rearrange("p (k n) -> p k n", k=blocks),
                    in_=src[:, :].rearrange("(k p) n -> p k n", p=128)),
                    writes=[dst], stream=stream, cost=1100, lat=lat)

            load_w(wk, wk_d, "wa2")
            load_w(wv, wv_d, "wa3", lat=28000)
            load_w(wgk1, wgk1_d, "wa2", lat=2000)
            wgk2x = sbA("wgk2x", [17, 1024], BF16)
            E.op("pool", lambda: g.dma_start(out=wgk2x[:, :], in_=wgk2x_d[:, :]),
                 writes=[wgk2x], stream="wa2", cost=650, lat=1500)
            A17_2 = [sbA(f"A17_{i}", [17, 128], BF16) for i in range(2)]
            for _i in range(2):
                E.op("pool", lambda _i=_i: g.dma_start(
                    out=A17_2[_i][16:17, :], in_=onesrow_d[:, :]),
                    writes=[A17_2[_i]], stream="c3", cost=650, lat=1500)
            load_w(wq, wq_d, "wa1")

            raws = sbA("raw0", [128, 1024], F32)
            raw2 = [raws, raws]
            min02 = [sbA(f"min0{i}", [128, 1024], BF16) for i in range(2)]
            FA52 = [sbA(f"FA5{i}", [128, 1024], F32) for i in range(2)]
            EcB2 = [sbA(f"EcB{i}", [128, 1024], F32) for i in range(2)]
            EnB2 = [sbA(f"EnB{i}", [128, 1024], F32) for i in range(2)]
            E1c2 = [sbA(f"E1c{i}", [128, 1024], BF16) for i in range(3)]
            Engc2 = [sbA(f"Engc{i}", [128, 1024], BF16) for i in range(3)]
            EnS2 = [sbA(f"EnS{i}", [128, 1024], BF16) for i in range(3)]
            gku2 = [sbA(f"gku{i}", [128, 1024], BF16) for i in range(3)]
            qg = sbA("qg", [128, 1024], BF16)
            ki = sbA("ki", [128, 1024], BF16)
            ks = sbA("ks", [128, 1024], BF16)
            qgfm = sbA("qgfm", [128, 1024], BF16, is_transpose_dest=True)
            kifm = sbA("kifm", [128, 1024], BF16, is_transpose_dest=True)
            vbf = sbA("vbf", [128, 2048], BF16)
            scbf = sbA("scbf", [128, 512], BF16)
            ospt = sbA("ospt", [128, 2048], BF16)


            def a_s0(i):
                c, src_d, full = descs[i]
                par = i % 4
                x_sb, hl = xt[par], hln[i % 2]
                E.op("sp", lambda x_sb=x_sb, c=c, src_d=src_d: sy.dma_start(
                    out=x_sb[:, :], in_=src_d[c * CH:(c + 1) * CH, :]),
                    writes=[x_sb], stream=f"x{par}", cost=700, lat=4500)
                layernorm(x_sb, hl, [x_sb], [hl], lnc, t8a, t8i)
                dmaT(hfm[par], hl, 8, f"th{par}")
                if full:
                    E.op("pool", lambda c=c, par=par: g.dma_start(
                        out=hsp_d[c * CH:(c + 1) * CH, :], in_=hfm[par][:, :]),
                        reads=[hfm[par]], writes=[hsp_d], stream=f"hs{c % 2}",
                        cost=650, lat=4500)

            def a_s1a(i):
                hf = hfm[i % 4]
                gku = gku2[i % 3]
                raw, min0, FA5 = raw2[i % 2], min02[i % 2], FA52[i % 2]
                for k in range(8):
                    E.op("pe", lambda k=k, hf=hf: t.matmul(
                        PB[1][:16, 0:128], wgk1[:, 16 * k:16 * (k + 1)],
                        hf[:, 128 * k:128 * (k + 1)],
                        start=(k == 0), stop=(k == 7)),
                        reads=[wgk1, hf], writes=[PBH[1]])
                A17 = A17_2[i % 2]
                E.op("act", lambda A17=A17: s.copy(out=A17[0:16, :], in_=PB[1][:16, 0:128]),
                     reads=[PBH[1]], writes=[A17], cost=200)
                for nh in range(2):
                    E.op("pe", lambda nh=nh, A17=A17: t.matmul(
                        PB[nh][:, :], A17[:, :], wgk2x[:, 512 * nh:512 * (nh + 1)],
                        start=True, stop=True),
                        reads=[A17, wgk2x], writes=[PBH[nh]])
                E.op("act", lambda: s.copy(out=raw[:, :], in_=PT[0][:, :]),
                     reads=PTH[0], writes=[raw], cost=1150)
                E.op("dve", lambda min0=min0, raw=raw: v.tensor_scalar_min(
                    out=min0[:, :], in0=raw[:, :], scalar1=0.0),
                    reads=[raw], writes=[min0])
                E.op("act", lambda: s.activation(out=FA5[:, :], in_=raw[:, :], func=AF.Abs),
                     reads=[raw], writes=[FA5])
                E.op("act", lambda: s.activation(out=raw[:, :], in_=FA5[:, :], func=AF.Exp, scale=-1.0),
                     reads=[FA5], writes=[raw])
                E.op("act", lambda: s.activation(out=FA5[:, :], in_=raw[:, :], func=AF.Ln, bias=1.0),
                     reads=[raw], writes=[FA5])
                E.op("pool", lambda gku=gku, min0=min0, FA5=FA5: g.tensor_tensor(
                    out=gku[:, :], in0=min0[:, :], in1=FA5[:, :], op=ALU.subtract),
                    reads=[min0, FA5], writes=[gku], cost=2350)

            def a_s1b(i):
                c, _, full = descs[i]
                gku = gku2[i % 3]
                E1c, Engc, EnS = E1c2[i % 3], Engc2[i % 3], EnS2[i % 3]
                if full:
                    for nh in range(2):
                        E.op("pe", lambda nh=nh, gku=gku: t.matmul(
                            PB[nh][:, :], triu[:, :], gku[:, 512 * nh:512 * (nh + 1)],
                            start=True, stop=True), reads=[triu, gku], writes=[PBH[nh]])
                    E.op("act", lambda Engc=Engc: s.activation(
                        out=Engc[:, :], in_=PT[0][:, :], func=AF.Exp,
                        scale=C_N116), reads=PTH[0] + [ccols], writes=[Engc])
                    E.op("act", lambda E1c=E1c: s.activation(
                        out=E1c[:, :], in_=PT[0][:, :], func=AF.Exp,
                        scale=C_116, bias=C_NLN16), reads=PTH[0] + [ccols], writes=[E1c])
                for nh in range(2):
                    E.op("pe", lambda nh=nh, gku=gku: t.matmul(
                        PB[nh][:, :], ones128[:, :], gku[:, 512 * nh:512 * (nh + 1)],
                        start=True, stop=True), reads=[ones128, gku], writes=[PBH[nh]])
                E.op("act", lambda EnS=EnS: s.activation(
                    out=EnS[:, :], in_=PT[0][:, :], func=AF.Exp,
                    scale=C_116), reads=PTH[0] + [ccols], writes=[EnS])
                for blk in range(8):
                    E.op("pe", lambda blk=blk, gku=gku: t.matmul(
                        PB[0][:, blk:blk + 1], gku[:, 128 * blk:128 * (blk + 1)], onescol[:, :],
                        start=True, stop=True), reads=[gku, onescol], writes=[PBH[0]])
                ebl, ecc = ebl4[i % 4], ecc4[i % 4]
                E.op("act", lambda ebl=ebl: s.activation(out=ebl[:, :], in_=PB[0][:, 0:8], func=AF.Exp,
                                                 scale=C_116),
                     reads=[PBH[0], ccols], writes=[ebl], cost=250)
                if full:
                    E.op("act", lambda ecc=ecc: s.activation(out=ecc[:, :], in_=PB[0][:, 0:8], func=AF.Exp,
                                                     scale=C_132),
                         reads=[PBH[0], ccols], writes=[ecc], cost=250)

            def a_s2(i):
                c, _, full = descs[i]
                hf = hfm[i % 4]
                E1c, Engc, EnS = E1c2[i % 3], Engc2[i % 3], EnS2[i % 3]
                for k in range(8):
                    for nh in range(2):
                        E.op("pe", lambda k=k, nh=nh, hf=hf: t.matmul(
                            PB[4 + nh][:, :], hf[:, 128 * k:128 * (k + 1)],
                            wk[:, 1024 * k + 512 * nh:1024 * k + 512 * (nh + 1)],
                            start=(k == 0), stop=(k == 7)),
                            reads=[hf, wk], writes=[PBH[4 + nh]])
                if full:
                    for k in range(8):
                        for nh in range(2):
                            E.op("pe", lambda k=k, nh=nh, hf=hf: t.matmul(
                                PB[6 + nh][:, :], hf[:, 128 * k:128 * (k + 1)],
                                wq[:, 1024 * k + 512 * nh:1024 * k + 512 * (nh + 1)],
                                start=(k == 0), stop=(k == 7)),
                                reads=[hf, wq], writes=[PBH[6 + nh]])
                    E.op("dve", lambda: v.tensor_tensor(
                        out=qg[:, :], in0=PT[3][:, :], in1=E1c[:, :], op=ALU.mult),
                        reads=PTH[3] + [E1c], writes=[qg])
                    E.op("dve", lambda: v.tensor_tensor(
                        out=ki[:, :], in0=PT[2][:, :], in1=Engc[:, :], op=ALU.mult),
                        reads=PTH[2] + [Engc], writes=[ki])
                E.op("dve", lambda: v.tensor_tensor(
                    out=ks[:, :], in0=PT[2][:, :], in1=EnS[:, :], op=ALU.mult),
                    reads=PTH[2] + [EnS], writes=[ks])
                if full:
                    dmaT(qgfm, qg, 8, "tq")
                    dmaT(kifm, ki, 8, "tk")
                for k in range(8):
                    for ns in range(4):
                        E.op("pe", lambda k=k, ns=ns, hf=hf: t.matmul(
                            PB[2 + ns][:, :], hf[:, 128 * k:128 * (k + 1)],
                            wv[:, 2048 * k + 512 * ns:2048 * k + 512 * (ns + 1)],
                            start=(k == 0), stop=(k == 7)),
                            reads=[hf, wv], writes=[PBH[2 + ns]])
                E.op("dve", lambda: v.tensor_copy(out=vbf[:, 0:1024], in_=PT[1][:, :]),
                     reads=PTH[1], writes=[vbf])
                E.op("act", lambda: s.copy(out=vbf[:, 1024:2048], in_=PT[2][:, :]),
                     reads=PTH[2], writes=[vbf])

            def a_s3(i):
                c, _, full = descs[i]
                ebl, ecc = ebl4[i % 4], ecc4[i % 4]
                if full:
                    for blk in range(8):
                        if blk % 4 == 0:
                            E.op("dve", lambda blk=blk, ecc=ecc: v.tensor_scalar_mul(
                                out=S8b[blk][:, :], in0=S8[blk][:, :],
                                scalar1=ecc[:, blk:blk + 1]),
                                reads=[S8[blk], ecc], writes=[S8b[blk]])
                        else:
                            E.op("act", lambda blk=blk, ecc=ecc: s.activation(
                                out=S8b[blk][:, :], in_=S8[blk][:, :], func=AF.Identity,
                                scale=ecc[:, blk:blk + 1]),
                                reads=[S8[blk], ecc], writes=[S8b[blk]])
                    for h in range(4):
                        for kc in range(2):
                            blk = 2 * h + kc
                            E.op("pe", lambda h=h, kc=kc, blk=blk: t.matmul(
                                PB[6][:, 128 * h:128 * (h + 1)],
                                kifm[:, 128 * blk:128 * (blk + 1)],
                                qgfm[:, 128 * blk:128 * (blk + 1)],
                                start=(kc == 0), stop=(kc == 1)),
                                reads=[kifm, qgfm], writes=[PBH[6]])
                    E.op("dve", lambda: v.tensor_tensor(out=scbf[:, :], in0=PB[6][:, :],
                                                        in1=mask4[:, :], op=ALU.mult),
                         reads=[PBH[6], mask4], writes=[scbf])
                    for h in range(4):
                        E.op("pe", lambda h=h: t.matmul(
                            PB[2 + h][:, :], scbf[:, 128 * h:128 * (h + 1)],
                            vbf[:, 512 * h:512 * (h + 1)], start=True, stop=False),
                            reads=[scbf, vbf], writes=[PBH[2 + h]])
                        for kc in range(2):
                            blk = 2 * h + kc
                            E.op("pe", lambda h=h, kc=kc, blk=blk: t.matmul(
                                PB[2 + h][:, :], qgfm[:, 128 * blk:128 * (blk + 1)],
                                S8b[blk][:, :],
                                start=False, stop=(kc == 1)),
                                reads=[qgfm, S8b[blk]], writes=[PBH[2 + h]])
                    E.op("dve", lambda: v.tensor_copy(out=ospt[:, 0:1024], in_=PT[1][:, :]),
                         reads=PTH[1], writes=[ospt])
                    E.op("act", lambda: s.copy(out=ospt[:, 1024:2048], in_=PT[2][:, :]),
                         reads=PTH[2], writes=[ospt])
                    E.op("pool", lambda c=c: g.dma_start(
                        out=osp_d[c * CH:(c + 1) * CH, :], in_=ospt[:, :]),
                        reads=[ospt], writes=[osp_d], stream="so",
                        cost=650, lat=4500)
                for h in range(4):
                    for mc in range(2):
                        blk = 2 * h + mc
                        pb = PB[6 + (blk % 2)]; pbh = PBH[6 + (blk % 2)]
                        E.op("pe", lambda blk=blk, h=h, pb=pb: t.matmul(
                            pb[:, :], ks[:, 128 * blk:128 * (blk + 1)],
                            vbf[:, 512 * h:512 * (h + 1)], start=True, stop=True),
                            reads=[ks, vbf], writes=[pbh])
                        E.op("dve", lambda blk=blk, pb=pb, ebl=ebl: v.scalar_tensor_tensor(
                            out=S8[blk][:, :],
                            in0=S8[blk][:, :],
                            scalar=ebl[:, blk:blk + 1], in1=pb[:, :],
                            op0=ALU.mult, op1=ALU.add),
                            reads=[S8[blk], ebl, pbh], writes=[S8[blk]])

            descs = [(c, xpre_d, False) for c in range(n_chunks)] + \
                    [(c, xin, True) for c in range(n_chunks)]
            N = len(descs)

            def a_mask(i_done):
                if i_done == n_chunks - 1:
                    for blk in range(8):
                        E.op("dve", lambda blk=blk: v.tensor_scalar_mul(
                            out=S8[blk][:, :], in0=S8[blk][:, :], scalar1=rolec[:, :]),
                            reads=[S8[blk], rolec], writes=[S8[blk]])
                    if debug:
                        for blk in range(8):
                            E.op("pool", lambda blk=blk: g.dma_start(
                                out=smid_d[:, 512 * blk:512 * (blk + 1)], in_=S8[blk][:, :]),
                                reads=[S8[blk]], writes=[smid_d], stream="so")

            for j in range(min(3, N)):
                a_s0(j)
                a_s1a(j)
            a_s1b(0)
            if N > 3:
                a_s0(3)
                a_s1a(3)
            if N > 1:
                a_s1b(1)
            for i in range(N):
                if i + 2 < N:
                    a_s1b(i + 2)
                if i >= 1:
                    a_s3(i - 1)
                    a_mask(i - 1)
                a_s2(i)
                if i + 4 < N:
                    a_s0(i + 4)
                    a_s1a(i + 4)
            a_s3(N - 1)
            a_mask(N - 1)

            if debug:
                for blk in range(8):
                    E.op("pool", lambda blk=blk: g.dma_start(
                        out=sdump_d[:, 512 * blk:512 * (blk + 1)], in_=S8[blk][:, :]),
                        reads=[S8[blk]], writes=[sdump_d], stream="so")

        E.barrier()

        with ExitStack() as esB:
            def sbB(name, shape, dtype, **kw):
                return esB.enter_context(nc.sbuf_tensor(name, shape, dtype, **kw))

            wg = sbB("wg", [128, 8 * 2048], BF16)
            wo = sbB("wo", [128, 16 * 1024], BF16)
            w8t = sbB("w8t", [128, 8 * 1024], BF16)
            wstage2 = [sbB(f"wstage{i}", [128, 1024], F32) for i in range(2)]
            onesrowt = sbB("onesrowt", [1, 128], BF16)
            biasrow = sbB("biasrow_sb", [1, 1024], BF16)
            E.op("pool", lambda: g.dma_start(out=onesrowt[:, :], in_=onesrow_d[:, :]),
                 writes=[onesrowt], stream="c0", cost=650, lat=1500)
            E.op("pool", lambda: g.dma_start(out=biasrow[:, :], in_=biasrow_d[:, :]),
                 writes=[biasrow], stream="c1", cost=650, lat=1500)
            x8_2 = [sbB(f"x8_{i}", [128, 1024], FP8) for i in range(2)]
            x8b_2 = [sbB(f"x8b_{i}", [128, 1024], BF16) for i in range(2)]

            for wh in range(2):
                E.op("pool", lambda wh=wh: g.dma_start(
                    out=wg[:, 8192 * wh:8192 * (wh + 1)].rearrange("p (k n) -> p k n", k=4),
                    in_=wg_d[512 * wh:512 * (wh + 1), :].rearrange("(k p) n -> p k n", p=128)),
                    writes=[wg], stream=("wa1", "wa2")[wh], cost=1100, lat=14000)
            for dc in range(16):
                ws = wstage2[dc % 2]
                E.op("pool", lambda dc=dc, ws=ws: g.dma_start(
                    out=ws[:, :], in_=wo_d[128 * dc:128 * (dc + 1), :]),
                    writes=[ws], stream=("wa3", "so")[dc % 2], cost=650, lat=4500)
                E.op("dve", lambda dc=dc, ws=ws: v.tensor_scalar_mul(
                    out=wo[:, 1024 * dc:1024 * (dc + 1)], in0=ws[:, :],
                    scalar1=gnwc[:, dc % 4:dc % 4 + 1]),
                    reads=[ws, gnwc], writes=[wo])
            for fc in range(8):
                ws = wstage2[fc % 2]
                w8q, w8b = x8_2[fc % 2], x8b_2[fc % 2]
                E.op("pool", lambda fc=fc, ws=ws: g.dma_start(
                    out=ws[:, :], in_=wf_d[128 * fc:128 * (fc + 1), :]),
                    writes=[ws], stream=("wa3", "so")[fc % 2], cost=650, lat=4500)
                E.op("dve", lambda ws=ws: v.tensor_scalar(
                    out=ws[:, :], in0=ws[:, :], scalar1=448.0, scalar2=-448.0,
                    op0=ALU.min, op1=ALU.max), reads=[ws], writes=[ws])
                E.op("dve", lambda ws=ws, w8q=w8q: v.tensor_copy(out=w8q[:, :], in_=ws[:, :]),
                     reads=[ws], writes=[w8q])
                E.op("act", lambda w8q=w8q, w8b=w8b: s.copy(out=w8b[:, :], in_=w8q[:, :]),
                     reads=[w8q], writes=[w8b])
                E.op("sp", lambda fc=fc, w8b=w8b: sy.dma_start(
                    out=w8t[:, :].rearrange("p (blk f) -> p blk f", blk=8)[:, :, 128 * fc:128 * (fc + 1)],
                    in_=w8b[:, :].rearrange("f (blk p) -> f blk p", blk=8),
                    transpose=True), reads=[w8b], writes=[w8t], stream=("tq", "tk")[fc % 2],
                    cost=1400, lat=1500)

            G1_2 = [sbB(f"G1_{i}", [128, 2048], BF16) for i in range(3)]
            G2s = sbB("G2_0", [128, 2048], BF16)
            G2_2 = [G2s, G2s]
            ogs = sbB("og0", [128, 2048], BF16)
            og2 = [ogs, ogs]
            gate2 = [sbB(f"gate{i}", [128, 2048], BF16) for i in range(3)]
            ogfm2 = [sbB(f"ogfm{i}", [128, 2048], BF16, is_transpose_dest=True) for i in range(2)]
            oloc = [sbB(f"oloc{i}", [128, 2048], BF16) for i in range(4)]
            x8fms = sbB("x8fm0", [128, 1024], BF16, is_transpose_dest=True)
            x8fm2 = [x8fms, x8fms]
            racc2 = [sbB(f"racc{i}", [128, 4], F32) for i in range(2)]
            rstd2 = [sbB(f"rstd{i}", [128, 4], F32) for i in range(2)]
            XM2 = [sbB(f"XM{i}", [128, 1024], F32) for i in range(2)]
            XLns = sbB("XLn0", [128, 1024], F32)
            XLn2 = [XLns, XLns]
            OUT2 = wstage2

            def b_t0(c):
                par = c % 4
                x_sb = xt[par]
                E.op("sp", lambda c=c, x_sb=x_sb: sy.dma_start(
                    out=x_sb[:, :], in_=xin[c * CH:(c + 1) * CH, :]),
                    writes=[x_sb], stream=f"x{par}", cost=700, lat=4500)
                E.op("pool", lambda c=c: g.dma_start(
                    out=oloc[c % 4][:, :], in_=osp_d[c * CH:(c + 1) * CH, :]),
                    reads=[osp_d], writes=[oloc[c % 4]], stream=f"ol{c % 4}",
                    cost=650, lat=4500),
                E.op("sp", lambda c=c, par=par: sy.dma_start(
                    out=hfm[par][:, :], in_=hsp_d[c * CH:(c + 1) * CH, :]),
                    reads=[hsp_d], writes=[hfm[par]], stream=f"th{par % 3}",
                    cost=700, lat=4500)

            def b_t1a(c):
                hf = hfm[c % 4]
                G1 = G1_2[c % 3]
                for k in range(8):
                    for ns in range(4):
                        E.op("pe", lambda k=k, ns=ns, hf=hf: t.matmul(
                            PB[ns][:, :], hf[:, 128 * k:128 * (k + 1)],
                            wg[:, 2048 * k + 512 * ns:2048 * k + 512 * (ns + 1)],
                            start=(k == 0), stop=(k == 7)),
                            reads=[hf, wg], writes=[PBH[ns]])
                E.op("act", lambda G1=G1: s.copy(out=G1[:, 0:1024], in_=PT[0][:, :]),
                     reads=PTH[0], writes=[G1])
                E.op("act", lambda G1=G1: s.copy(out=G1[:, 1024:2048], in_=PT[1][:, :]),
                     reads=PTH[1], writes=[G1])

            def b_t1b(c):
                G1, G2 = G1_2[c % 3], G2_2[c % 2]
                E.op("act", lambda G1=G1, G2=G2: s.activation(out=G2[:, :], in_=G1[:, :], func=AF.Tanh,
                                                 scale=C_HALF),
                     reads=[G1, ccols], writes=[G2], cost=2100)
                gb = gate2[c % 3]
                E.op("dve", lambda gb=gb, G1=G1, G2=G2: v.scalar_tensor_tensor(
                    out=gb[:, :], in0=G2[:, :], scalar=1.0, in1=G1[:, :],
                    op0=ALU.add, op1=ALU.mult),
                    reads=[G1, G2], writes=[gb], cost=1300)

            def b_t2(c):
                ol = oloc[c % 4]
                racc, rstd = racc2[c % 2], rstd2[c % 2]
                og, ogfm = og2[c % 2], ogfm2[c % 2]
                gb = gate2[c % 3]
                XM = XM2[c % 2]
                x8, x8b = x8_2[c % 2], x8b_2[c % 2]
                x8fm = x8fm2[c % 2]
                for h in range(4):
                    E.op("act", lambda h=h, ol=ol, racc=racc: s.activation(
                        out=dump[:, :], in_=ol[:, 512 * h:512 * (h + 1)], func=AF.Square,
                        accum_out=racc[:, h:h + 1]),
                        reads=[ol], writes=[dump, racc])
                E.op("dve", lambda racc=racc: v.tensor_scalar(
                    out=racc[:, :], in0=racc[:, :], scalar1=4.0 / 512.0, scalar2=4.0 * EPS,
                    op0=ALU.mult, op1=ALU.add), reads=[racc], writes=[racc])
                quake_rsqrt(rstd[:, :], rstd, racc[:, :], racc, t8e, t8f)
                for h in range(4):
                    E.op("dve", lambda h=h, ol=ol, og=og, rstd=rstd, gb=gb: v.scalar_tensor_tensor(
                        out=og[:, 512 * h:512 * (h + 1)], in0=ol[:, 512 * h:512 * (h + 1)],
                        scalar=rstd[:, h:h + 1], in1=gb[:, 512 * h:512 * (h + 1)],
                        op0=ALU.mult, op1=ALU.mult),
                        reads=[ol, rstd, gb], writes=[og])
                dmaT(ogfm, og, 16, f"to{c % 2}")
                for dc in range(16):
                    for nh in range(2):
                        E.op("pe", lambda dc=dc, nh=nh, ogfm=ogfm: t.matmul(
                            PB[4 + nh][:, :], ogfm[:, 128 * dc:128 * (dc + 1)],
                            wo[:, 1024 * dc + 512 * nh:1024 * dc + 512 * (nh + 1)],
                            start=(dc == 0), stop=(dc == 15)),
                            reads=[ogfm, wo], writes=[PBH[4 + nh]])
                E.op("dve", lambda c=c, XM=XM: v.tensor_tensor(
                    out=XM[:, :], in0=PT[2][:, :], in1=xt[c % 4][:, :], op=ALU.add),
                    reads=PTH[2] + [xt[c % 4]], writes=[XM])
                XLn = XLn2[c % 2]
                layernorm(XM, XLn, [XM], [XLn], lnc2, t8c, t8j)
                E.op("dve", lambda XLn=XLn, x8=x8: v.tensor_copy(out=x8[:, :], in_=XLn[:, :]),
                     reads=[XLn], writes=[x8])
                E.op("act", lambda x8=x8, x8b=x8b: s.copy(out=x8b[:, :], in_=x8[:, :]),
                     reads=[x8], writes=[x8b])
                dmaT(x8fm, x8b, 8, f"t8{c % 2}")

            def b_t3(c):
                XM = XM2[c % 2]
                OUT = OUT2[c % 2]
                x8fm = x8fm2[c % 2]
                for nh in range(2):
                    E.op("pe", lambda nh=nh: t.matmul(
                        PB[6 + nh][:, :], onesrowt[:, :],
                        biasrow[:, 512 * nh:512 * (nh + 1)],
                        start=True, stop=False),
                        reads=[onesrowt, biasrow], writes=[PBH[6 + nh]], cost=120)
                for k in range(8):
                    for nh in range(2):
                        E.op("pe", lambda k=k, nh=nh, x8fm=x8fm: t.matmul(
                            PB[6 + nh][:, :], x8fm[:, 128 * k:128 * (k + 1)],
                            w8t[:, 1024 * k + 512 * nh:1024 * k + 512 * (nh + 1)],
                            start=False, stop=(k == 7)),
                            reads=[x8fm, w8t], writes=[PBH[6 + nh]])
                E.op("dve", lambda XM=XM, OUT=OUT: v.tensor_tensor(
                    out=OUT[:, :], in0=PT[3][:, :], in1=XM[:, :], op=ALU.add),
                    reads=PTH[3] + [XM], writes=[OUT])
                E.op("pool", lambda c=c, OUT=OUT: g.dma_start(
                    out=y[c * CH:(c + 1) * CH, :], in_=OUT[:, :]),
                    reads=[OUT], writes=[y], stream=f"y{c % 2}",
                    cost=650, lat=4500)

            for j in range(min(3, n_chunks)):
                b_t0(j)
            for j in range(min(3, n_chunks)):
                b_t1a(j)
            for j in range(min(2, n_chunks)):
                b_t1b(j)
            for c in range(n_chunks):
                if c + 3 < n_chunks:
                    b_t0(c + 3)
                    b_t1a(c + 3)
                if c + 2 < n_chunks:
                    b_t1b(c + 2)
                if c >= 1:
                    b_t3(c - 1)
                b_t2(c)
            b_t3(n_chunks - 1)

        block = es.enter_context(nc.Block())
        handles = {"pe": nc.tensor, "act": nc.scalar, "dve": nc.vector,
                   "pool": nc.gpsimd, "sp": nc.sync}
        E.emit(block, lambda name: es.enter_context(nc.semaphore(f"sem_{name}")), handles)

    return nc


def _consts():
    tri = np.triu(np.ones((128, 128), dtype=np.float32))
    mask4 = np.tile(tri, (1, 4)).astype(ml_dtypes.bfloat16)
    mmat = (tri - 0.5).astype(ml_dtypes.bfloat16)
    nmat = (1.0 - tri).astype(ml_dtypes.bfloat16)
    onescol = np.ones((128, 1), dtype=ml_dtypes.bfloat16)
    return mmat, mask4, nmat, onescol


def make_in_maps(x, inputs, n_chunks=N_CHUNKS, n_cores=8):
    tri, mask4, ones128, onescol = _consts()
    seg = n_chunks * CH
    common = dict(
        Wq=np.asarray(inputs["L0_attn_Wq"], np.float32),
        Wk=np.asarray(inputs["L0_attn_Wk"], np.float32),
        Wv=np.asarray(inputs["L0_attn_Wv"], np.float32),
        Wg=np.asarray(inputs["L0_attn_Wg"], np.float32),
        Wgk1=np.asarray(inputs["L0_attn_Wgk1"], np.float32),
        Wgk2x=np.vstack([
            np.asarray(inputs["L0_attn_Wgk2"], np.float32),
            np.asarray(inputs["L0_attn_bgk2"], np.float32).reshape(1, 1024),
        ]).astype(ml_dtypes.bfloat16),
        onesrow=np.ones((1, 128), dtype=ml_dtypes.bfloat16),
        biasrow=np.asarray(inputs["L0_ffn_bias"], np.float32).reshape(1, 1024).astype(ml_dtypes.bfloat16),
        gnw=np.asarray(inputs["L0_attn_g_norm_w"], np.float32).reshape(1, 512),
        Wo=np.asarray(inputs["L0_attn_Wo"], np.float32),
        Wffn=np.asarray(inputs["L0_ffn_weight"], np.float32),
        bffn=np.asarray(inputs["L0_ffn_bias"], np.float32).reshape(1, 1024),
        triu=tri, mask4=mask4, ones128=ones128, onescol=onescol,
    )
    zeros = np.zeros((seg, HID), np.float32)
    in_maps = []
    for core in range(n_cores):
        b, h = core // 2, core % 2
        m = dict(common)
        m["x_seg"] = np.ascontiguousarray(x[b, h * seg:(h + 1) * seg, :])
        m["x_pre"] = np.ascontiguousarray(x[b, 0:seg, :]) if h == 1 else zeros
        m["role"] = np.array([[float(h)]], np.float32)
        in_maps.append(m)
    return in_maps


def kernel(L0_attn_hidden_states, L0_attn_Wq, L0_attn_Wk, L0_attn_Wv, L0_attn_Wg,
           L0_attn_Wgk1, L0_attn_Wgk2, L0_attn_bgk2, L0_attn_g_norm_w, L0_attn_Wo,
           L0_ffn_weight, L0_ffn_bias):
    x = np.asarray(L0_attn_hidden_states, dtype=np.float32)
    assert x.shape == (4, 4096, 1024)
    inputs = dict(
        L0_attn_Wq=L0_attn_Wq, L0_attn_Wk=L0_attn_Wk, L0_attn_Wv=L0_attn_Wv,
        L0_attn_Wg=L0_attn_Wg, L0_attn_Wgk1=L0_attn_Wgk1, L0_attn_Wgk2=L0_attn_Wgk2,
        L0_attn_bgk2=L0_attn_bgk2, L0_attn_g_norm_w=L0_attn_g_norm_w,
        L0_attn_Wo=L0_attn_Wo, L0_ffn_weight=L0_ffn_weight, L0_ffn_bias=L0_ffn_bias,
    )
    nc = build_program()
    in_maps = make_in_maps(x, inputs)
    res = run_bass_kernel_spmd(nc, in_maps, core_ids=list(range(8)))
    out = np.empty((4, 4096, 1024), np.float32)
    for core in range(8):
        b, h = core // 2, core % 2
        out[b, h * 2048:(h + 1) * 2048, :] = res.results[core]["y_seg"]
    return out


# version 34
# speedup vs baseline: 1.0233x; 1.0061x over previous
import math
import numpy as np
import ml_dtypes

import concourse.bass as bass
import concourse.mybir as mybir
from concourse.bass_utils import run_bass_kernel_spmd

F32 = mybir.dt.float32
BF16 = mybir.dt.bfloat16
FP8 = mybir.dt.float8e4
I32 = mybir.dt.int32
AF = mybir.ActivationFunctionType
ALU = mybir.AluOpType

HID = 1024
DV = 2048
CH = 128
N_CHUNKS = 16
EPS = 1e-5
LN16 = math.log(16.0)
MAGIC = 0x5F3759DF

ENGINES = ("pe", "act", "dve", "pool", "sp")


class Emitter:

    DUR = {"pe": 240, "dve": 900, "act": 1150, "pool": 2400, "sp": 1000}
    XLAT = 380

    def __init__(self, nc):
        self.nc = nc
        self.ops = []
        self.last_write = {}
        self.readers = {}
        self.last_on = {}
        self.regions = []

    @staticmethod
    def _key(b):
        if isinstance(b, str):
            return b
        t = getattr(b, "tensor", b)
        n = getattr(t, "name", None)
        return n if n is not None else id(t)

    def op(self, eng, fn, reads=(), writes=(), stream=None, inc=1, cost=None,
           lat=0):
        import sys
        line = sys._getframe(1).f_lineno
        idx = len(self.ops)
        deps = set()
        if stream is not None:
            inc = 16
        rkeys = [self._key(b) for b in reads]
        wkeys = [self._key(b) for b in writes]
        for kb in rkeys:
            lw = self.last_write.get(kb)
            if lw is not None:
                deps.add(lw)
        for kb in wkeys:
            lw = self.last_write.get(kb)
            if lw is not None:
                deps.add(lw)
            for r in self.readers.get(kb, ()):
                deps.add(r)
        sig = stream if stream is not None else eng
        if stream is not None and sig in self.last_on:
            deps.add(self.last_on[sig])
        if stream is not None:
            self.last_on[sig] = idx
        for kb in rkeys:
            self.readers.setdefault(kb, []).append(idx)
        for kb in wkeys:
            self.last_write[kb] = idx
            self.readers[kb] = []
        self.ops.append({"eng": eng, "fn": fn, "sig": sig, "inc": inc,
                         "deps": deps, "cost": cost or self.DUR[eng],
                         "lat": lat, "line": line})

    def barrier(self):
        self.regions.append(len(self.ops))

    def _schedule(self, lo, hi):
        import heapq
        ops = self.ops
        indeg = {}
        succ = {i: [] for i in range(lo, hi)}
        for i in range(lo, hi):
            d = [j for j in ops[i]["deps"] if lo <= j < hi]
            indeg[i] = len(d)
            for j in d:
                succ[j].append(i)
        finish = {j: 0 for j in range(lo)}
        engine_time = {e: 0 for e in ENGINES}
        bind = {}
        blevel = {}
        for i in range(hi - 1, lo - 1, -1):
            blevel[i] = ops[i]["cost"] + ops[i]["lat"] + max(
                (blevel[k] for k in succ[i]), default=0)
        ready = []
        for i in range(lo, hi):
            if indeg[i] == 0:
                heapq.heappush(ready, (0, -blevel[i], i))
        order = {e: [] for e in ENGINES}
        pops = []
        scheduled = 0
        while ready:
            _, _, i = heapq.heappop(ready)
            pops.append(i)
            o = ops[i]
            e = o["eng"]
            dep_t = 0
            dep_i = None
            for j in o["deps"]:
                if j < lo:
                    continue
                f = finish.get(j)
                if ops[j]["eng"] != e:
                    lat = self.XLAT
                elif e == "pe":
                    lat = 0
                else:
                    lat = 150
                if f + lat > dep_t:
                    dep_t = f + lat
                    dep_i = j
            if dep_t > engine_time[e]:
                bind[i] = ("dep", dep_i)
            else:
                bind[i] = ("queue", order[e][-1] if order[e] else None)
            st = max(engine_time[e], dep_t)
            fin = st + o["cost"]
            engine_time[e] = fin
            finish[i] = fin + o["lat"]
            order[e].append(i)
            scheduled += 1
            for k in succ[i]:
                indeg[k] -= 1
                if indeg[k] == 0:
                    dt = max((finish.get(j, 0) for j in ops[k]["deps"] if j >= lo),
                             default=0)
                    heapq.heappush(ready, (dt, -blevel[k], k))
        assert scheduled == hi - lo, (scheduled, hi - lo)
        busy = {e: 0 for e in ENGINES}
        for i in range(lo, hi):
            busy[ops[i]["eng"]] += ops[i]["cost"]
        print(f"[sched] region {lo}:{hi} predicted makespan "
              f"{max(engine_time.values()) / 1e3:.1f}us "
              + " ".join(f"{e}={engine_time[e] / 1e3:.0f}" for e in ENGINES)
              + " | busy "
              + " ".join(f"{e}={busy[e] / 1e3:.0f}" for e in ENGINES))
        import os
        if os.environ.get("SCHED_CP"):
            cur = max((i for i in range(lo, hi)), key=lambda i: finish[i])
            agg = {}
            while cur is not None and cur >= lo:
                o = ops[cur]
                kind, prev = bind.get(cur, ("dep", None))
                k = (o["eng"], o["line"], kind)
                a = agg.setdefault(k, [0, 0])
                a[0] += 1
                a[1] += o["cost"] + (o["lat"] if kind == "dep" else 0)
                cur = prev
            for k, (n, tot) in sorted(agg.items(), key=lambda kv: -kv[1][1])[:16]:
                print(f"  [cp] {k[0]:4s} line={k[1]} {k[2]:5s} n={n:4d} "
                      f"tot={tot / 1e3:7.1f}us")
        self._last_pops = pops
        return order

    def emit(self, block, make_sem, handles):
        regions = self.regions + [len(self.ops)]
        region_lists = []
        count_of = {}
        counts = {}
        snapshots = []
        lo = 0
        self._pop_lists = []
        for hi in regions:
            order = self._schedule(lo, hi)
            self._pop_lists.append(self._last_pops)
            region_lists.append(order)
            for e in ENGINES:
                for i in order[e]:
                    sig = self.ops[i]["sig"]
                    counts[sig] = counts.get(sig, 0) + self.ops[i]["inc"]
                    count_of[i] = counts[sig]
            snapshots.append(dict(counts))
            lo = hi
        sems = {name: make_sem(name) for name in counts}
        final = dict(counts)

        ops = self.ops
        KNOW = {}
        KS = {}
        hw = {e: {} for e in ENGINES}
        fence_waits = {}
        for ri, pops in enumerate(self._pop_lists):
            if ri > 0:
                snap = snapshots[ri - 1]
                for e in ENGINES:
                    fw = []
                    h = hw[e]
                    for sig2, c in sorted(snap.items(), key=lambda kv: str(kv[0])):
                        if sig2 != e and c > h.get(sig2, 0):
                            fw.append((sig2, c))
                        if h.get(sig2, 0) < c:
                            h[sig2] = c
                    fence_waits[(ri, e)] = fw
            for i in pops:
                o = ops[i]
                e = o["eng"]
                needed = {}
                for j in o["deps"]:
                    sj = ops[j]["sig"]
                    if sj == "pe" and e == "pe":
                        continue
                    v = count_of[j]
                    if v > needed.get(sj, 0):
                        needed[sj] = v
                emitted = []
                h = hw[e]
                for sig2, v in sorted(needed.items(), key=lambda kv: str(kv[0])):
                    if v > h.get(sig2, 0):
                        emitted.append((sig2, v))
                        for s2, c2 in KNOW.get((sig2, v), {sig2: v}).items():
                            if h.get(s2, 0) < c2:
                                h[s2] = c2
                        if h.get(sig2, 0) < v:
                            h[sig2] = v
                o["waits_final"] = emitted
                ki = dict(h)
                own = count_of[i]
                if ki.get(o["sig"], 0) < own:
                    ki[o["sig"]] = own
                prev = KS.get(o["sig"])
                if prev:
                    for s2, c2 in prev.items():
                        if ki.get(s2, 0) < c2:
                            ki[s2] = c2
                KS[o["sig"]] = ki
                KNOW[(o["sig"], own)] = ki
        nw = sum(len(o.get("waits_final", ())) for o in ops)
        print(f"[emit] waits after transitive elimination: {nw}")

        def run_stream(eng):
            eh = handles[eng]
            for ri, order in enumerate(region_lists):
                if ri > 0:
                    for sig2, c in fence_waits[(ri, eng)]:
                        eh.wait_ge(sems[sig2], c)
                for i in order[eng]:
                    o = self.ops[i]
                    for sig2, vv in o["waits_final"]:
                        eh.wait_ge(sems[sig2], vv)
                    o["fn"]().then_inc(sems[o["sig"]], o["inc"])
            for sig2, c in sorted(final.items(), key=lambda kv: str(kv[0])):
                if sig2 != eng and c > hw[eng].get(sig2, 0):
                    eh.wait_ge(sems[sig2], c)

        for eng in ENGINES:
            deco = {"pe": block.tensor, "act": block.scalar,
                    "dve": block.vector, "pool": block.gpsimd,
                    "sp": block.sync}[eng]

            def body(handle, _eng=eng):
                run_stream(_eng)
            deco(body)


def build_program(n_chunks=N_CHUNKS, debug=False):
    nc = bass.Bass("TRN2", target_bir_lowering=False)
    SEG = n_chunks * CH

    xin = nc.dram_tensor("x_seg", [SEG, HID], F32, kind="ExternalInput")
    xpre_d = nc.dram_tensor("x_pre", [SEG, HID], F32, kind="ExternalInput")
    wq_d = nc.dram_tensor("Wq", [HID, 1024], F32, kind="ExternalInput")
    wk_d = nc.dram_tensor("Wk", [HID, 1024], F32, kind="ExternalInput")
    wv_d = nc.dram_tensor("Wv", [HID, 2048], F32, kind="ExternalInput")
    wg_d = nc.dram_tensor("Wg", [HID, 2048], F32, kind="ExternalInput")
    wgk1_d = nc.dram_tensor("Wgk1", [HID, 16], F32, kind="ExternalInput")
    wgk2x_d = nc.dram_tensor("Wgk2x", [17, 1024], BF16, kind="ExternalInput")
    onesrow_d = nc.dram_tensor("onesrow", [1, 128], BF16, kind="ExternalInput")
    biasrow_d = nc.dram_tensor("biasrow", [1, 1024], BF16, kind="ExternalInput")
    gnw_d = nc.dram_tensor("gnw", [1, 512], F32, kind="ExternalInput")
    wo_d = nc.dram_tensor("Wo", [2048, 1024], F32, kind="ExternalInput")
    wf_d = nc.dram_tensor("Wffn", [1024, 1024], F32, kind="ExternalInput")
    bf_d = nc.dram_tensor("bffn", [1, 1024], F32, kind="ExternalInput")
    role_d = nc.dram_tensor("role", [1, 1], F32, kind="ExternalInput")
    triu_d = nc.dram_tensor("triu", [128, 128], BF16, kind="ExternalInput")
    mask4_d = nc.dram_tensor("mask4", [128, 512], BF16, kind="ExternalInput")
    ones128_d = nc.dram_tensor("ones128", [128, 128], BF16, kind="ExternalInput")
    onescol_d = nc.dram_tensor("onescol", [128, 1], BF16, kind="ExternalInput")

    y = nc.dram_tensor("y_seg", [SEG, HID], F32, kind="ExternalOutput")
    spill_kind = "ExternalOutput" if debug else "Internal"
    osp_d = nc.dram_tensor("o_sp", [SEG, DV], BF16, kind=spill_kind)
    hsp_d = nc.dram_tensor("h_sp", [SEG, HID], BF16, kind=spill_kind)
    if debug:
        sdump_d = nc.dram_tensor("s_dump", [128, 4096], F32, kind="ExternalOutput")
        smid_d = nc.dram_tensor("smid_dump", [128, 4096], F32, kind="ExternalOutput")

    from contextlib import ExitStack
    with ExitStack() as es:
        def sb(name, shape, dtype, **kw):
            return es.enter_context(nc.sbuf_tensor(name, shape, dtype, **kw))

        triu = sb("triu_sb", [128, 128], BF16)
        mask4 = sb("mask4_sb", [128, 512], BF16)
        ones128 = sb("ones128_sb", [128, 128], BF16)
        onescol = sb("onescol_sb", [128, 1], BF16)
        gnwc = sb("gnwc", [128, 4], F32)
        rolec = sb("rolec", [128, 1], F32)
        magic = sb("magic", [128, 4], I32)
        ebl4 = [sb(f"ebl{i}", [128, 8], F32) for i in range(4)]
        ecc4 = [sb(f"ecc{i}", [128, 8], F32) for i in range(4)]
        t8a = sb("t8a", [128, 4], F32)
        t8i = sb("t8i", [128, 4], I32)
        t8c = sb("t8c", [128, 4], F32)
        t8j = sb("t8j", [128, 4], I32)
        t8e = sb("t8e", [128, 4], F32)
        t8f = sb("t8f", [128, 4], I32)
        lnc = sb("lnc", [128, 16], F32)
        lnc2 = sb("lnc2", [128, 16], F32)
        ccols = sb("ccols", [128, 8], F32)
        dump = sb("dump", [128, 512], F32)

        xt = [sb(f"xt{i}", [128, 1024], F32) for i in range(4)]
        hln = [sb(f"hln{i}", [128, 1024], BF16) for i in range(2)]
        hfm = [sb(f"hfm{i}", [128, 1024], BF16, is_transpose_dest=True) for i in range(4)]

        PT = [es.enter_context(nc.psum_tensor(f"PT{i}", [128, 1024], F32)) for i in range(4)]
        def PBv(i):
            return PT[i // 2][:, 512 * (i % 2):512 * (i % 2 + 1)]
        PB = [PBv(i) for i in range(8)]
        PBH = [f"PT{i // 2}:{i % 2}" for i in range(8)]
        PTH = [[f"PT{j}:0", f"PT{j}:1"] for j in range(4)]

        E = Emitter(nc)
        v = nc.vector
        s = nc.scalar
        t = nc.tensor
        g = nc.gpsimd
        sy = nc.sync

        def quake_rsqrt(dst, dst_h, src, src_h, tf, ti, n_iter=1):
            w = dst.shape[1]
            E.op("dve", lambda: v.tensor_single_scalar(
                out=ti[:, :w], in_=src.bitcast(I32), scalar=1,
                op=ALU.logical_shift_right), reads=[src_h], writes=[ti], cost=150)
            E.op("dve", lambda: v.tensor_tensor(
                out=dst.bitcast(I32), in0=magic[:, :w], in1=ti[:, :w],
                op=ALU.subtract), reads=[magic, ti], writes=[dst_h], cost=150)
            for _ in range(n_iter):
                E.op("dve", lambda: v.tensor_tensor(
                    out=tf[:, :w], in0=src, in1=dst, op=ALU.mult),
                    reads=[src_h, dst_h], writes=[tf], cost=150)
                E.op("dve", lambda: v.tensor_tensor(
                    out=tf[:, :w], in0=tf[:, :w], in1=dst, op=ALU.mult),
                    reads=[tf, dst_h], writes=[tf], cost=150)
                E.op("dve", lambda: v.tensor_scalar(
                    out=tf[:, :w], in0=tf[:, :w], scalar1=-0.5,
                    scalar2=1.5, op0=ALU.mult, op1=ALU.add),
                    reads=[tf], writes=[tf], cost=150)
                E.op("dve", lambda: v.tensor_tensor(
                    out=dst, in0=dst, in1=tf[:, :w], op=ALU.mult),
                    reads=[dst_h, tf], writes=[dst_h], cost=150)

        def layernorm(x_ap, out_ap, rh, wh, scr, tf, ti):
            E.op("dve", lambda: v.bn_stats(out=scr[:, 0:6], in_=x_ap[:, 0:512]),
                 reads=rh, writes=[scr], cost=600)
            E.op("dve", lambda: v.bn_stats(out=scr[:, 6:12], in_=x_ap[:, 512:1024]),
                 reads=rh, writes=[scr], cost=600)
            E.op("dve", lambda: v.bn_aggr(out=scr[:, 12:14], in_=scr[:, 0:12]),
                 reads=[scr], writes=[scr], cost=200)
            E.op("dve", lambda: v.tensor_scalar(
                out=scr[:, 14:15], in0=scr[:, 13:14], scalar1=1.0, scalar2=EPS,
                op0=ALU.mult, op1=ALU.add), reads=[scr], writes=[scr], cost=150)
            quake_rsqrt(scr[:, 15:16], scr, scr[:, 14:15], scr, tf, ti)
            E.op("dve", lambda: v.scalar_tensor_tensor(
                out=scr[:, 14:15], in0=scr[:, 12:13], scalar=-1.0, in1=scr[:, 15:16],
                op0=ALU.mult, op1=ALU.mult), reads=[scr], writes=[scr], cost=150)
            E.op("act", lambda: s.activation(out=out_ap[:, :], in_=x_ap[:, :], func=AF.Identity,
                                             scale=scr[:, 15:16], bias=scr[:, 14:15]),
                 reads=rh + [scr], writes=wh)

        def dmaT(dst, src, blocks, stream, eng="sp"):
            issuer = sy if eng == "sp" else s
            E.op(eng, lambda: issuer.dma_start(
                out=dst[:, :].rearrange("p (c t) -> p c t", c=blocks),
                in_=src[:, :].rearrange("t (c p) -> t c p", c=blocks),
                transpose=True), reads=[src], writes=[dst], stream=stream,
                cost=1400, lat=1500)

        E.op("pool", lambda: g.dma_start(out=triu[:, :], in_=triu_d[:, :]), writes=[triu], stream="c0")
        E.op("pool", lambda: g.dma_start(out=mask4[:, :], in_=mask4_d[:, :]), writes=[mask4], stream="c1")
        E.op("pool", lambda: g.dma_start(out=ones128[:, :], in_=ones128_d[:, :]), writes=[ones128], stream="c2")
        E.op("pool", lambda: g.dma_start(out=onescol[:, :], in_=onescol_d[:, :]), writes=[onescol], stream="c3")
        E.op("pool", lambda: g.dma_start(out=rolec[:, :], in_=role_d[:, :].broadcast_to([128, 1])),
             writes=[rolec], stream="c2")
        E.op("pool", lambda: g.dma_start(
            out=gnwc[:, :], in_=gnw_d[:, :].rearrange("o (c p) -> p (o c)", p=128),
            allow_slow_non_contiguous=True), writes=[gnwc], stream="c3")
        E.op("dve", lambda: v.memset(magic[:, :], MAGIC), writes=[magic])
        for _i, _v in enumerate([-LN16, 1.0 / 16.0, -1.0 / 16.0, 1.0 / 32.0, -1.0 / 32.0, 0.5]):
            E.op("dve", lambda _i=_i, _v=_v: v.memset(ccols[:, _i:_i + 1], _v), writes=[ccols])
        C_NLN16 = ccols[:, 0:1]
        C_116 = ccols[:, 1:2]
        C_N116 = ccols[:, 2:3]
        C_132 = ccols[:, 3:4]
        C_N132 = ccols[:, 4:5]
        C_HALF = ccols[:, 5:6]
        with ExitStack() as esA:
            def sbA(name, shape, dtype, **kw):
                return esA.enter_context(nc.sbuf_tensor(name, shape, dtype, **kw))

            S8 = [sbA(f"S{i}", [128, 512], F32) for i in range(8)]
            S8b = [sbA(f"Sb{i}", [128, 512], BF16) for i in range(8)]
            for _i in range(8):
                E.op("dve", lambda _i=_i: v.memset(S8[_i][:, :], 0.0), writes=[S8[_i]])

            wq = sbA("wq", [128, 8 * 1024], BF16)
            wk = sbA("wk", [128, 8 * 1024], BF16)
            wv = sbA("wv", [128, 8 * 2048], BF16)
            wgk1 = sbA("wgk1", [128, 8 * 16], BF16)


            def load_w(dst, src, stream, blocks=8, lat=14000):
                E.op("pool", lambda: g.dma_start(
                    out=dst[:, :].rearrange("p (k n) -> p k n", k=blocks),
                    in_=src[:, :].rearrange("(k p) n -> p k n", p=128)),
                    writes=[dst], stream=stream, cost=1100, lat=lat)

            load_w(wk, wk_d, "wa2")
            load_w(wv, wv_d, "wa3", lat=28000)
            load_w(wgk1, wgk1_d, "wa2", lat=2000)
            wgk2x = sbA("wgk2x", [17, 1024], BF16)
            E.op("pool", lambda: g.dma_start(out=wgk2x[:, :], in_=wgk2x_d[:, :]),
                 writes=[wgk2x], stream="wa2", cost=650, lat=1500)
            A17_2 = [sbA(f"A17_{i}", [17, 128], BF16) for i in range(2)]
            for _i in range(2):
                E.op("pool", lambda _i=_i: g.dma_start(
                    out=A17_2[_i][16:17, :], in_=onesrow_d[:, :]),
                    writes=[A17_2[_i]], stream="c3", cost=650, lat=1500)
            load_w(wq, wq_d, "wa1")

            raws = sbA("raw0", [128, 1024], F32)
            raw2 = [raws, raws]
            min02 = [sbA(f"min0{i}", [128, 1024], BF16) for i in range(2)]
            FA52 = [sbA(f"FA5{i}", [128, 1024], F32) for i in range(2)]
            EcB2 = [sbA(f"EcB{i}", [128, 1024], F32) for i in range(2)]
            EnB2 = [sbA(f"EnB{i}", [128, 1024], F32) for i in range(2)]
            E1c2 = [sbA(f"E1c{i}", [128, 1024], BF16) for i in range(3)]
            Engc2 = [sbA(f"Engc{i}", [128, 1024], BF16) for i in range(3)]
            EnS2 = [sbA(f"EnS{i}", [128, 1024], BF16) for i in range(3)]
            gku2 = [sbA(f"gku{i}", [128, 1024], BF16) for i in range(3)]
            qg = sbA("qg", [128, 1024], BF16)
            ki = sbA("ki", [128, 1024], BF16)
            ks = sbA("ks", [128, 1024], BF16)
            qgfm = sbA("qgfm", [128, 1024], BF16, is_transpose_dest=True)
            kifm = sbA("kifm", [128, 1024], BF16, is_transpose_dest=True)
            vbf = sbA("vbf", [128, 2048], BF16)
            scbf = sbA("scbf", [128, 512], BF16)
            ospt = sbA("ospt", [128, 2048], BF16)


            def a_s0(i):
                c, src_d, full = descs[i]
                par = i % 4
                x_sb, hl = xt[par], hln[i % 2]
                E.op("sp", lambda x_sb=x_sb, c=c, src_d=src_d: sy.dma_start(
                    out=x_sb[:, :], in_=src_d[c * CH:(c + 1) * CH, :]),
                    writes=[x_sb], stream=f"x{par}", cost=700, lat=4500)
                layernorm(x_sb, hl, [x_sb], [hl], lnc, t8a, t8i)
                dmaT(hfm[par], hl, 8, f"th{par}")
                if full:
                    E.op("pool", lambda c=c, par=par: g.dma_start(
                        out=hsp_d[c * CH:(c + 1) * CH, :], in_=hfm[par][:, :]),
                        reads=[hfm[par]], writes=[hsp_d], stream=f"hs{c % 2}",
                        cost=650, lat=4500)

            def a_s1a(i):
                hf = hfm[i % 4]
                gku = gku2[i % 3]
                raw, min0, FA5 = raw2[i % 2], min02[i % 2], FA52[i % 2]
                for k in range(8):
                    E.op("pe", lambda k=k, hf=hf: t.matmul(
                        PB[1][:16, 0:128], wgk1[:, 16 * k:16 * (k + 1)],
                        hf[:, 128 * k:128 * (k + 1)],
                        start=(k == 0), stop=(k == 7)),
                        reads=[wgk1, hf], writes=[PBH[1]])
                A17 = A17_2[i % 2]
                E.op("act", lambda A17=A17: s.copy(out=A17[0:16, :], in_=PB[1][:16, 0:128]),
                     reads=[PBH[1]], writes=[A17], cost=200)
                for nh in range(2):
                    E.op("pe", lambda nh=nh, A17=A17: t.matmul(
                        PB[nh][:, :], A17[:, :], wgk2x[:, 512 * nh:512 * (nh + 1)],
                        start=True, stop=True),
                        reads=[A17, wgk2x], writes=[PBH[nh]])
                E.op("act", lambda: s.copy(out=raw[:, :], in_=PT[0][:, :]),
                     reads=PTH[0], writes=[raw], cost=1150)
                E.op("dve", lambda min0=min0, raw=raw: v.tensor_scalar_min(
                    out=min0[:, :], in0=raw[:, :], scalar1=0.0),
                    reads=[raw], writes=[min0])
                E.op("act", lambda: s.activation(out=FA5[:, :], in_=raw[:, :], func=AF.Abs),
                     reads=[raw], writes=[FA5])
                E.op("act", lambda: s.activation(out=raw[:, :], in_=FA5[:, :], func=AF.Exp, scale=-1.0),
                     reads=[FA5], writes=[raw])
                E.op("act", lambda: s.activation(out=FA5[:, :], in_=raw[:, :], func=AF.Ln, bias=1.0),
                     reads=[raw], writes=[FA5])
                E.op("pool", lambda gku=gku, min0=min0, FA5=FA5: g.tensor_tensor(
                    out=gku[:, :], in0=min0[:, :], in1=FA5[:, :], op=ALU.subtract),
                    reads=[min0, FA5], writes=[gku], cost=2350)

            def a_s1b(i):
                c, _, full = descs[i]
                gku = gku2[i % 3]
                E1c, Engc, EnS = E1c2[i % 3], Engc2[i % 3], EnS2[i % 3]
                if full:
                    for nh in range(2):
                        E.op("pe", lambda nh=nh, gku=gku: t.matmul(
                            PB[nh][:, :], triu[:, :], gku[:, 512 * nh:512 * (nh + 1)],
                            start=True, stop=True), reads=[triu, gku], writes=[PBH[nh]])
                    E.op("act", lambda Engc=Engc: s.activation(
                        out=Engc[:, :], in_=PT[0][:, :], func=AF.Exp,
                        scale=C_N116), reads=PTH[0] + [ccols], writes=[Engc])
                    E.op("act", lambda E1c=E1c: s.activation(
                        out=E1c[:, :], in_=PT[0][:, :], func=AF.Exp,
                        scale=C_116, bias=C_NLN16), reads=PTH[0] + [ccols], writes=[E1c])
                for nh in range(2):
                    E.op("pe", lambda nh=nh, gku=gku: t.matmul(
                        PB[nh][:, :], ones128[:, :], gku[:, 512 * nh:512 * (nh + 1)],
                        start=True, stop=True), reads=[ones128, gku], writes=[PBH[nh]])
                E.op("act", lambda EnS=EnS: s.activation(
                    out=EnS[:, :], in_=PT[0][:, :], func=AF.Exp,
                    scale=C_116), reads=PTH[0] + [ccols], writes=[EnS])
                for blk in range(8):
                    E.op("pe", lambda blk=blk, gku=gku: t.matmul(
                        PB[0][:, blk:blk + 1], gku[:, 128 * blk:128 * (blk + 1)], onescol[:, :],
                        start=True, stop=True), reads=[gku, onescol], writes=[PBH[0]])
                ebl, ecc = ebl4[i % 4], ecc4[i % 4]
                E.op("act", lambda ebl=ebl: s.activation(out=ebl[:, :], in_=PB[0][:, 0:8], func=AF.Exp,
                                                 scale=C_116),
                     reads=[PBH[0], ccols], writes=[ebl], cost=250)
                if full:
                    E.op("act", lambda ecc=ecc: s.activation(out=ecc[:, :], in_=PB[0][:, 0:8], func=AF.Exp,
                                                     scale=C_132),
                         reads=[PBH[0], ccols], writes=[ecc], cost=250)

            def a_s2(i):
                c, _, full = descs[i]
                hf = hfm[i % 4]
                E1c, Engc, EnS = E1c2[i % 3], Engc2[i % 3], EnS2[i % 3]
                for k in range(8):
                    for nh in range(2):
                        E.op("pe", lambda k=k, nh=nh, hf=hf: t.matmul(
                            PB[4 + nh][:, :], hf[:, 128 * k:128 * (k + 1)],
                            wk[:, 1024 * k + 512 * nh:1024 * k + 512 * (nh + 1)],
                            start=(k == 0), stop=(k == 7)),
                            reads=[hf, wk], writes=[PBH[4 + nh]])
                if full:
                    for k in range(8):
                        for nh in range(2):
                            E.op("pe", lambda k=k, nh=nh, hf=hf: t.matmul(
                                PB[6 + nh][:, :], hf[:, 128 * k:128 * (k + 1)],
                                wq[:, 1024 * k + 512 * nh:1024 * k + 512 * (nh + 1)],
                                start=(k == 0), stop=(k == 7)),
                                reads=[hf, wq], writes=[PBH[6 + nh]])
                    E.op("dve", lambda: v.tensor_tensor(
                        out=qg[:, :], in0=PT[3][:, :], in1=E1c[:, :], op=ALU.mult),
                        reads=PTH[3] + [E1c], writes=[qg])
                    E.op("dve", lambda: v.tensor_tensor(
                        out=ki[:, :], in0=PT[2][:, :], in1=Engc[:, :], op=ALU.mult),
                        reads=PTH[2] + [Engc], writes=[ki])
                E.op("dve", lambda: v.tensor_tensor(
                    out=ks[:, :], in0=PT[2][:, :], in1=EnS[:, :], op=ALU.mult),
                    reads=PTH[2] + [EnS], writes=[ks])
                if full:
                    dmaT(qgfm, qg, 8, "tq")
                    dmaT(kifm, ki, 8, "tk")
                for k in range(8):
                    for ns in range(4):
                        E.op("pe", lambda k=k, ns=ns, hf=hf: t.matmul(
                            PB[2 + ns][:, :], hf[:, 128 * k:128 * (k + 1)],
                            wv[:, 2048 * k + 512 * ns:2048 * k + 512 * (ns + 1)],
                            start=(k == 0), stop=(k == 7)),
                            reads=[hf, wv], writes=[PBH[2 + ns]])
                E.op("dve", lambda: v.tensor_copy(out=vbf[:, 0:1024], in_=PT[1][:, :]),
                     reads=PTH[1], writes=[vbf])
                E.op("act", lambda: s.copy(out=vbf[:, 1024:2048], in_=PT[2][:, :]),
                     reads=PTH[2], writes=[vbf])

            def a_s3(i):
                c, _, full = descs[i]
                ebl, ecc = ebl4[i % 4], ecc4[i % 4]
                if full:
                    for blk in range(8):
                        if blk % 4 == 0:
                            E.op("dve", lambda blk=blk, ecc=ecc: v.tensor_scalar_mul(
                                out=S8b[blk][:, :], in0=S8[blk][:, :],
                                scalar1=ecc[:, blk:blk + 1]),
                                reads=[S8[blk], ecc], writes=[S8b[blk]])
                        else:
                            E.op("act", lambda blk=blk, ecc=ecc: s.activation(
                                out=S8b[blk][:, :], in_=S8[blk][:, :], func=AF.Identity,
                                scale=ecc[:, blk:blk + 1]),
                                reads=[S8[blk], ecc], writes=[S8b[blk]])
                    for h in range(4):
                        for kc in range(2):
                            blk = 2 * h + kc
                            E.op("pe", lambda h=h, kc=kc, blk=blk: t.matmul(
                                PB[6][:, 128 * h:128 * (h + 1)],
                                kifm[:, 128 * blk:128 * (blk + 1)],
                                qgfm[:, 128 * blk:128 * (blk + 1)],
                                start=(kc == 0), stop=(kc == 1)),
                                reads=[kifm, qgfm], writes=[PBH[6]])
                    E.op("dve", lambda: v.tensor_tensor(out=scbf[:, :], in0=PB[6][:, :],
                                                        in1=mask4[:, :], op=ALU.mult),
                         reads=[PBH[6], mask4], writes=[scbf])
                    for h in range(4):
                        E.op("pe", lambda h=h: t.matmul(
                            PB[2 + h][:, :], scbf[:, 128 * h:128 * (h + 1)],
                            vbf[:, 512 * h:512 * (h + 1)], start=True, stop=False),
                            reads=[scbf, vbf], writes=[PBH[2 + h]])
                        for kc in range(2):
                            blk = 2 * h + kc
                            E.op("pe", lambda h=h, kc=kc, blk=blk: t.matmul(
                                PB[2 + h][:, :], qgfm[:, 128 * blk:128 * (blk + 1)],
                                S8b[blk][:, :],
                                start=False, stop=(kc == 1)),
                                reads=[qgfm, S8b[blk]], writes=[PBH[2 + h]])
                    E.op("dve", lambda: v.tensor_copy(out=ospt[:, 0:1024], in_=PT[1][:, :]),
                         reads=PTH[1], writes=[ospt])
                    E.op("act", lambda: s.copy(out=ospt[:, 1024:2048], in_=PT[2][:, :]),
                         reads=PTH[2], writes=[ospt])
                    E.op("pool", lambda c=c: g.dma_start(
                        out=osp_d[c * CH:(c + 1) * CH, :], in_=ospt[:, :]),
                        reads=[ospt], writes=[osp_d], stream="so",
                        cost=650, lat=4500)
                for h in range(4):
                    for mc in range(2):
                        blk = 2 * h + mc
                        pb = PB[6 + (blk % 2)]; pbh = PBH[6 + (blk % 2)]
                        E.op("pe", lambda blk=blk, h=h, pb=pb: t.matmul(
                            pb[:, :], ks[:, 128 * blk:128 * (blk + 1)],
                            vbf[:, 512 * h:512 * (h + 1)], start=True, stop=True),
                            reads=[ks, vbf], writes=[pbh])
                        E.op("dve", lambda blk=blk, pb=pb, ebl=ebl: v.scalar_tensor_tensor(
                            out=S8[blk][:, :],
                            in0=S8[blk][:, :],
                            scalar=ebl[:, blk:blk + 1], in1=pb[:, :],
                            op0=ALU.mult, op1=ALU.add),
                            reads=[S8[blk], ebl, pbh], writes=[S8[blk]])

            descs = [(c, xpre_d, False) for c in range(n_chunks)] + \
                    [(c, xin, True) for c in range(n_chunks)]
            N = len(descs)

            def a_mask(i_done):
                if i_done == n_chunks - 1:
                    for blk in range(8):
                        E.op("dve", lambda blk=blk: v.tensor_scalar_mul(
                            out=S8[blk][:, :], in0=S8[blk][:, :], scalar1=rolec[:, :]),
                            reads=[S8[blk], rolec], writes=[S8[blk]])
                    if debug:
                        for blk in range(8):
                            E.op("pool", lambda blk=blk: g.dma_start(
                                out=smid_d[:, 512 * blk:512 * (blk + 1)], in_=S8[blk][:, :]),
                                reads=[S8[blk]], writes=[smid_d], stream="so")

            for j in range(min(3, N)):
                a_s0(j)
                a_s1a(j)
            a_s1b(0)
            if N > 3:
                a_s0(3)
                a_s1a(3)
            if N > 1:
                a_s1b(1)
            for i in range(N):
                if i + 2 < N:
                    a_s1b(i + 2)
                if i >= 1:
                    a_s3(i - 1)
                    a_mask(i - 1)
                a_s2(i)
                if i + 4 < N:
                    a_s0(i + 4)
                    a_s1a(i + 4)
            a_s3(N - 1)
            a_mask(N - 1)

            if debug:
                for blk in range(8):
                    E.op("pool", lambda blk=blk: g.dma_start(
                        out=sdump_d[:, 512 * blk:512 * (blk + 1)], in_=S8[blk][:, :]),
                        reads=[S8[blk]], writes=[sdump_d], stream="so")

        E.barrier()

        with ExitStack() as esB:
            def sbB(name, shape, dtype, **kw):
                return esB.enter_context(nc.sbuf_tensor(name, shape, dtype, **kw))

            wg = sbB("wg", [128, 8 * 2048], BF16)
            wo = sbB("wo", [128, 16 * 1024], BF16)
            w8t = sbB("w8t", [128, 8 * 1024], BF16)
            w8t8 = sbB("w8t8", [128, 8 * 1024], FP8)
            x8fm8 = sbB("x8fm8", [128, 1024], FP8)
            wstage2 = [sbB(f"wstage{i}", [128, 1024], F32) for i in range(2)]
            onesrowt = sbB("onesrowt", [1, 128], BF16)
            biasrow = sbB("biasrow_sb", [1, 1024], BF16)
            E.op("pool", lambda: g.dma_start(out=onesrowt[:, :], in_=onesrow_d[:, :]),
                 writes=[onesrowt], stream="c0", cost=650, lat=1500)
            E.op("pool", lambda: g.dma_start(out=biasrow[:, :], in_=biasrow_d[:, :]),
                 writes=[biasrow], stream="c1", cost=650, lat=1500)
            x8s = sbB("x8_0", [128, 1024], FP8)
            x8_2 = [x8s, x8s]
            x8b_2 = [sbB(f"x8b_{i}", [128, 1024], BF16) for i in range(2)]

            for wh in range(2):
                E.op("pool", lambda wh=wh: g.dma_start(
                    out=wg[:, 8192 * wh:8192 * (wh + 1)].rearrange("p (k n) -> p k n", k=4),
                    in_=wg_d[512 * wh:512 * (wh + 1), :].rearrange("(k p) n -> p k n", p=128)),
                    writes=[wg], stream=("wa1", "wa2")[wh], cost=1100, lat=14000)
            for dc in range(16):
                ws = wstage2[dc % 2]
                E.op("pool", lambda dc=dc, ws=ws: g.dma_start(
                    out=ws[:, :], in_=wo_d[128 * dc:128 * (dc + 1), :]),
                    writes=[ws], stream=("wa3", "so")[dc % 2], cost=650, lat=4500)
                E.op("dve", lambda dc=dc, ws=ws: v.tensor_scalar_mul(
                    out=wo[:, 1024 * dc:1024 * (dc + 1)], in0=ws[:, :],
                    scalar1=gnwc[:, dc % 4:dc % 4 + 1]),
                    reads=[ws, gnwc], writes=[wo])
            for fc in range(8):
                ws = wstage2[fc % 2]
                w8q, w8b = x8_2[fc % 2], x8b_2[fc % 2]
                E.op("pool", lambda fc=fc, ws=ws: g.dma_start(
                    out=ws[:, :], in_=wf_d[128 * fc:128 * (fc + 1), :]),
                    writes=[ws], stream=("wa3", "so")[fc % 2], cost=650, lat=4500)
                E.op("dve", lambda ws=ws: v.tensor_scalar(
                    out=ws[:, :], in0=ws[:, :], scalar1=448.0, scalar2=-448.0,
                    op0=ALU.min, op1=ALU.max), reads=[ws], writes=[ws])
                E.op("dve", lambda ws=ws, w8q=w8q: v.tensor_copy(out=w8q[:, :], in_=ws[:, :]),
                     reads=[ws], writes=[w8q])
                E.op("act", lambda w8q=w8q, w8b=w8b: s.copy(out=w8b[:, :], in_=w8q[:, :]),
                     reads=[w8q], writes=[w8b])
                E.op("sp", lambda fc=fc, w8b=w8b: sy.dma_start(
                    out=w8t[:, :].rearrange("p (blk f) -> p blk f", blk=8)[:, :, 128 * fc:128 * (fc + 1)],
                    in_=w8b[:, :].rearrange("f (blk p) -> f blk p", blk=8),
                    transpose=True), reads=[w8b], writes=[w8t], stream=("tq", "tk")[fc % 2],
                    cost=1400, lat=1500)
            for wh in range(2):
                E.op("dve", lambda wh=wh: v.tensor_copy(
                    out=w8t8[:, 4096 * wh:4096 * (wh + 1)],
                    in_=w8t[:, 4096 * wh:4096 * (wh + 1)]),
                    reads=[w8t], writes=[w8t8], cost=1300)

            G1_2 = [sbB(f"G1_{i}", [128, 2048], BF16) for i in range(3)]
            G2s = sbB("G2_0", [128, 2048], BF16)
            G2_2 = [G2s, G2s]
            ogs = sbB("og0", [128, 2048], BF16)
            og2 = [ogs, ogs]
            gate2 = [sbB(f"gate{i}", [128, 2048], BF16) for i in range(3)]
            ogfm2 = [sbB(f"ogfm{i}", [128, 2048], BF16, is_transpose_dest=True) for i in range(2)]
            oloc = [sbB(f"oloc{i}", [128, 2048], BF16) for i in range(3)]
            x8fms = sbB("x8fm0", [128, 1024], BF16, is_transpose_dest=True)
            x8fm2 = [x8fms, x8fms]
            racc2 = [sbB(f"racc{i}", [128, 4], F32) for i in range(2)]
            rstd2 = [sbB(f"rstd{i}", [128, 4], F32) for i in range(2)]
            XM2 = [sbB(f"XM{i}", [128, 1024], F32) for i in range(2)]
            XLns = sbB("XLn0", [128, 1024], F32)
            XLn2 = [XLns, XLns]
            OUT2 = wstage2

            def b_t0(c):
                par = c % 4
                x_sb = xt[par]
                E.op("sp", lambda c=c, x_sb=x_sb: sy.dma_start(
                    out=x_sb[:, :], in_=xin[c * CH:(c + 1) * CH, :]),
                    writes=[x_sb], stream=f"x{par}", cost=700, lat=4500)
                E.op("sp", lambda c=c, par=par: sy.dma_start(
                    out=hfm[par][:, :], in_=hsp_d[c * CH:(c + 1) * CH, :]),
                    reads=[hsp_d], writes=[hfm[par]], stream=f"th{par % 3}",
                    cost=700, lat=4500)

            def b_ol(c):
                E.op("pool", lambda c=c: g.dma_start(
                    out=oloc[c % 3][:, :], in_=osp_d[c * CH:(c + 1) * CH, :]),
                    reads=[osp_d], writes=[oloc[c % 3]], stream=f"ol{c % 3}",
                    cost=650, lat=4500)

            def b_t1a(c):
                hf = hfm[c % 4]
                G1 = G1_2[c % 3]
                for k in range(8):
                    for ns in range(4):
                        E.op("pe", lambda k=k, ns=ns, hf=hf: t.matmul(
                            PB[ns][:, :], hf[:, 128 * k:128 * (k + 1)],
                            wg[:, 2048 * k + 512 * ns:2048 * k + 512 * (ns + 1)],
                            start=(k == 0), stop=(k == 7)),
                            reads=[hf, wg], writes=[PBH[ns]])
                E.op("act", lambda G1=G1: s.copy(out=G1[:, 0:1024], in_=PT[0][:, :]),
                     reads=PTH[0], writes=[G1])
                E.op("act", lambda G1=G1: s.copy(out=G1[:, 1024:2048], in_=PT[1][:, :]),
                     reads=PTH[1], writes=[G1])

            def b_t1b(c):
                G1, G2 = G1_2[c % 3], G2_2[c % 2]
                E.op("act", lambda G1=G1, G2=G2: s.activation(out=G2[:, :], in_=G1[:, :], func=AF.Tanh,
                                                 scale=C_HALF),
                     reads=[G1, ccols], writes=[G2], cost=2100)
                gb = gate2[c % 3]
                E.op("dve", lambda gb=gb, G1=G1, G2=G2: v.scalar_tensor_tensor(
                    out=gb[:, :], in0=G2[:, :], scalar=1.0, in1=G1[:, :],
                    op0=ALU.add, op1=ALU.mult),
                    reads=[G1, G2], writes=[gb], cost=1300)

            def b_t2(c):
                ol = oloc[c % 3]
                racc, rstd = racc2[c % 2], rstd2[c % 2]
                og, ogfm = og2[c % 2], ogfm2[c % 2]
                gb = gate2[c % 3]
                XM = XM2[c % 2]
                x8, x8b = x8_2[c % 2], x8b_2[c % 2]
                x8fm = x8fm2[c % 2]
                for h in range(4):
                    E.op("act", lambda h=h, ol=ol, racc=racc: s.activation(
                        out=dump[:, :], in_=ol[:, 512 * h:512 * (h + 1)], func=AF.Square,
                        accum_out=racc[:, h:h + 1]),
                        reads=[ol], writes=[dump, racc])
                E.op("dve", lambda racc=racc: v.tensor_scalar(
                    out=racc[:, :], in0=racc[:, :], scalar1=4.0 / 512.0, scalar2=4.0 * EPS,
                    op0=ALU.mult, op1=ALU.add), reads=[racc], writes=[racc])
                quake_rsqrt(rstd[:, :], rstd, racc[:, :], racc, t8e, t8f)
                for h in range(4):
                    E.op("dve", lambda h=h, ol=ol, og=og, rstd=rstd, gb=gb: v.scalar_tensor_tensor(
                        out=og[:, 512 * h:512 * (h + 1)], in0=ol[:, 512 * h:512 * (h + 1)],
                        scalar=rstd[:, h:h + 1], in1=gb[:, 512 * h:512 * (h + 1)],
                        op0=ALU.mult, op1=ALU.mult),
                        reads=[ol, rstd, gb], writes=[og])
                dmaT(ogfm, og, 16, f"to{c % 2}")
                for dc in range(16):
                    for nh in range(2):
                        E.op("pe", lambda dc=dc, nh=nh, ogfm=ogfm: t.matmul(
                            PB[4 + nh][:, :], ogfm[:, 128 * dc:128 * (dc + 1)],
                            wo[:, 1024 * dc + 512 * nh:1024 * dc + 512 * (nh + 1)],
                            start=(dc == 0), stop=(dc == 15)),
                            reads=[ogfm, wo], writes=[PBH[4 + nh]])
                E.op("dve", lambda c=c, XM=XM: v.tensor_tensor(
                    out=XM[:, :], in0=PT[2][:, :], in1=xt[c % 4][:, :], op=ALU.add),
                    reads=PTH[2] + [xt[c % 4]], writes=[XM])
                XLn = XLn2[c % 2]
                layernorm(XM, XLn, [XM], [XLn], lnc2, t8c, t8j)
                E.op("dve", lambda XLn=XLn, x8=x8: v.tensor_copy(out=x8[:, :], in_=XLn[:, :]),
                     reads=[XLn], writes=[x8])
                E.op("act", lambda x8=x8, x8b=x8b: s.copy(out=x8b[:, :], in_=x8[:, :]),
                     reads=[x8], writes=[x8b])
                dmaT(x8fm, x8b, 8, f"t8{c % 2}")
                E.op("dve", lambda x8fm=x8fm: v.tensor_copy(out=x8fm8[:, :], in_=x8fm[:, :]),
                     reads=[x8fm], writes=[x8fm8], cost=400)

            def b_t3(c):
                XM = XM2[c % 2]
                OUT = OUT2[c % 2]
                x8fm = x8fm2[c % 2]
                for nh in range(2):
                    E.op("pe", lambda nh=nh: t.matmul(
                        PB[6 + nh][:, :], onesrowt[:, :],
                        biasrow[:, 512 * nh:512 * (nh + 1)],
                        start=True, stop=False),
                        reads=[onesrowt, biasrow], writes=[PBH[6 + nh]], cost=120)
                for k in range(8):
                    for nh in range(2):
                        E.op("pe", lambda k=k, nh=nh: t.matmul(
                            PB[6 + nh][:, :], x8fm8[:, 128 * k:128 * (k + 1)],
                            w8t8[:, 1024 * k + 512 * nh:1024 * k + 512 * (nh + 1)],
                            start=False, stop=(k == 7)),
                            reads=[x8fm8, w8t8], writes=[PBH[6 + nh]], cost=130)
                E.op("dve", lambda XM=XM, OUT=OUT: v.tensor_tensor(
                    out=OUT[:, :], in0=PT[3][:, :], in1=XM[:, :], op=ALU.add),
                    reads=PTH[3] + [XM], writes=[OUT])
                E.op("pool", lambda c=c, OUT=OUT: g.dma_start(
                    out=y[c * CH:(c + 1) * CH, :], in_=OUT[:, :]),
                    reads=[OUT], writes=[y], stream=f"y{c % 2}",
                    cost=650, lat=4500)

            for j in range(min(3, n_chunks)):
                b_t0(j)
                b_ol(j)
            for j in range(min(3, n_chunks)):
                b_t1a(j)
            for j in range(min(2, n_chunks)):
                b_t1b(j)
            for c in range(n_chunks):
                if c + 3 < n_chunks:
                    b_t0(c + 3)
                    b_t1a(c + 3)
                if c + 2 < n_chunks:
                    b_t1b(c + 2)
                if c >= 1:
                    b_t3(c - 1)
                b_t2(c)
                if c + 3 < n_chunks:
                    b_ol(c + 3)
            b_t3(n_chunks - 1)

        block = es.enter_context(nc.Block())
        handles = {"pe": nc.tensor, "act": nc.scalar, "dve": nc.vector,
                   "pool": nc.gpsimd, "sp": nc.sync}
        E.emit(block, lambda name: es.enter_context(nc.semaphore(f"sem_{name}")), handles)

    return nc


def _consts():
    tri = np.triu(np.ones((128, 128), dtype=np.float32))
    mask4 = np.tile(tri, (1, 4)).astype(ml_dtypes.bfloat16)
    mmat = (tri - 0.5).astype(ml_dtypes.bfloat16)
    nmat = (1.0 - tri).astype(ml_dtypes.bfloat16)
    onescol = np.ones((128, 1), dtype=ml_dtypes.bfloat16)
    return mmat, mask4, nmat, onescol


def make_in_maps(x, inputs, n_chunks=N_CHUNKS, n_cores=8):
    tri, mask4, ones128, onescol = _consts()
    seg = n_chunks * CH
    common = dict(
        Wq=np.asarray(inputs["L0_attn_Wq"], np.float32),
        Wk=np.asarray(inputs["L0_attn_Wk"], np.float32),
        Wv=np.asarray(inputs["L0_attn_Wv"], np.float32),
        Wg=np.asarray(inputs["L0_attn_Wg"], np.float32),
        Wgk1=np.asarray(inputs["L0_attn_Wgk1"], np.float32),
        Wgk2x=np.vstack([
            np.asarray(inputs["L0_attn_Wgk2"], np.float32),
            np.asarray(inputs["L0_attn_bgk2"], np.float32).reshape(1, 1024),
        ]).astype(ml_dtypes.bfloat16),
        onesrow=np.ones((1, 128), dtype=ml_dtypes.bfloat16),
        biasrow=np.asarray(inputs["L0_ffn_bias"], np.float32).reshape(1, 1024).astype(ml_dtypes.bfloat16),
        gnw=np.asarray(inputs["L0_attn_g_norm_w"], np.float32).reshape(1, 512),
        Wo=np.asarray(inputs["L0_attn_Wo"], np.float32),
        Wffn=np.asarray(inputs["L0_ffn_weight"], np.float32),
        bffn=np.asarray(inputs["L0_ffn_bias"], np.float32).reshape(1, 1024),
        triu=tri, mask4=mask4, ones128=ones128, onescol=onescol,
    )
    zeros = np.zeros((seg, HID), np.float32)
    in_maps = []
    for core in range(n_cores):
        b, h = core // 2, core % 2
        m = dict(common)
        m["x_seg"] = np.ascontiguousarray(x[b, h * seg:(h + 1) * seg, :])
        m["x_pre"] = np.ascontiguousarray(x[b, 0:seg, :]) if h == 1 else zeros
        m["role"] = np.array([[float(h)]], np.float32)
        in_maps.append(m)
    return in_maps


def kernel(L0_attn_hidden_states, L0_attn_Wq, L0_attn_Wk, L0_attn_Wv, L0_attn_Wg,
           L0_attn_Wgk1, L0_attn_Wgk2, L0_attn_bgk2, L0_attn_g_norm_w, L0_attn_Wo,
           L0_ffn_weight, L0_ffn_bias):
    x = np.asarray(L0_attn_hidden_states, dtype=np.float32)
    assert x.shape == (4, 4096, 1024)
    inputs = dict(
        L0_attn_Wq=L0_attn_Wq, L0_attn_Wk=L0_attn_Wk, L0_attn_Wv=L0_attn_Wv,
        L0_attn_Wg=L0_attn_Wg, L0_attn_Wgk1=L0_attn_Wgk1, L0_attn_Wgk2=L0_attn_Wgk2,
        L0_attn_bgk2=L0_attn_bgk2, L0_attn_g_norm_w=L0_attn_g_norm_w,
        L0_attn_Wo=L0_attn_Wo, L0_ffn_weight=L0_ffn_weight, L0_ffn_bias=L0_ffn_bias,
    )
    nc = build_program()
    in_maps = make_in_maps(x, inputs)
    res = run_bass_kernel_spmd(nc, in_maps, core_ids=list(range(8)))
    out = np.empty((4, 4096, 1024), np.float32)
    for core in range(8):
        b, h = core // 2, core % 2
        out[b, h * 2048:(h + 1) * 2048, :] = res.results[core]["y_seg"]
    return out


# version 35
# speedup vs baseline: 1.0381x; 1.0145x over previous
import math
import numpy as np
import ml_dtypes

import concourse.bass as bass
import concourse.mybir as mybir
from concourse.bass_utils import run_bass_kernel_spmd

F32 = mybir.dt.float32
BF16 = mybir.dt.bfloat16
FP8 = mybir.dt.float8e4
I32 = mybir.dt.int32
AF = mybir.ActivationFunctionType
ALU = mybir.AluOpType

HID = 1024
DV = 2048
CH = 128
N_CHUNKS = 16
EPS = 1e-5
LN16 = math.log(16.0)
MAGIC = 0x5F3759DF

ENGINES = ("pe", "act", "dve", "pool", "sp")


class Emitter:

    DUR = {"pe": 240, "dve": 900, "act": 1150, "pool": 2400, "sp": 1000}
    XLAT = 410

    def __init__(self, nc):
        self.nc = nc
        self.ops = []
        self.last_write = {}
        self.readers = {}
        self.last_on = {}
        self.regions = []

    @staticmethod
    def _key(b):
        if isinstance(b, str):
            return b
        t = getattr(b, "tensor", b)
        n = getattr(t, "name", None)
        return n if n is not None else id(t)

    def op(self, eng, fn, reads=(), writes=(), stream=None, inc=1, cost=None,
           lat=0):
        import sys
        line = sys._getframe(1).f_lineno
        idx = len(self.ops)
        deps = set()
        if stream is not None:
            inc = 16
        rkeys = [self._key(b) for b in reads]
        wkeys = [self._key(b) for b in writes]
        for kb in rkeys:
            lw = self.last_write.get(kb)
            if lw is not None:
                deps.add(lw)
        for kb in wkeys:
            lw = self.last_write.get(kb)
            if lw is not None:
                deps.add(lw)
            for r in self.readers.get(kb, ()):
                deps.add(r)
        sig = stream if stream is not None else eng
        if stream is not None and sig in self.last_on:
            deps.add(self.last_on[sig])
        if stream is not None:
            self.last_on[sig] = idx
        for kb in rkeys:
            self.readers.setdefault(kb, []).append(idx)
        for kb in wkeys:
            self.last_write[kb] = idx
            self.readers[kb] = []
        self.ops.append({"eng": eng, "fn": fn, "sig": sig, "inc": inc,
                         "deps": deps, "cost": cost or self.DUR[eng],
                         "lat": lat, "line": line})

    def barrier(self):
        self.regions.append(len(self.ops))

    def _schedule(self, lo, hi):
        import heapq
        ops = self.ops
        indeg = {}
        succ = {i: [] for i in range(lo, hi)}
        for i in range(lo, hi):
            d = [j for j in ops[i]["deps"] if lo <= j < hi]
            indeg[i] = len(d)
            for j in d:
                succ[j].append(i)
        finish = {j: 0 for j in range(lo)}
        engine_time = {e: 0 for e in ENGINES}
        bind = {}
        blevel = {}
        for i in range(hi - 1, lo - 1, -1):
            blevel[i] = ops[i]["cost"] + ops[i]["lat"] + max(
                (blevel[k] for k in succ[i]), default=0)
        ready = []
        for i in range(lo, hi):
            if indeg[i] == 0:
                heapq.heappush(ready, (0, -blevel[i], i))
        order = {e: [] for e in ENGINES}
        pops = []
        scheduled = 0
        while ready:
            _, _, i = heapq.heappop(ready)
            pops.append(i)
            o = ops[i]
            e = o["eng"]
            dep_t = 0
            dep_i = None
            for j in o["deps"]:
                if j < lo:
                    continue
                f = finish.get(j)
                if ops[j]["eng"] != e:
                    lat = self.XLAT
                elif e == "pe":
                    lat = 0
                else:
                    lat = 150
                if f + lat > dep_t:
                    dep_t = f + lat
                    dep_i = j
            if dep_t > engine_time[e]:
                bind[i] = ("dep", dep_i)
            else:
                bind[i] = ("queue", order[e][-1] if order[e] else None)
            st = max(engine_time[e], dep_t)
            fin = st + o["cost"]
            engine_time[e] = fin
            finish[i] = fin + o["lat"]
            order[e].append(i)
            scheduled += 1
            for k in succ[i]:
                indeg[k] -= 1
                if indeg[k] == 0:
                    dt = max((finish.get(j, 0) for j in ops[k]["deps"] if j >= lo),
                             default=0)
                    heapq.heappush(ready, (dt, -blevel[k], k))
        assert scheduled == hi - lo, (scheduled, hi - lo)
        busy = {e: 0 for e in ENGINES}
        for i in range(lo, hi):
            busy[ops[i]["eng"]] += ops[i]["cost"]
        print(f"[sched] region {lo}:{hi} predicted makespan "
              f"{max(engine_time.values()) / 1e3:.1f}us "
              + " ".join(f"{e}={engine_time[e] / 1e3:.0f}" for e in ENGINES)
              + " | busy "
              + " ".join(f"{e}={busy[e] / 1e3:.0f}" for e in ENGINES))
        import os
        if os.environ.get("SCHED_CP"):
            cur = max((i for i in range(lo, hi)), key=lambda i: finish[i])
            agg = {}
            while cur is not None and cur >= lo:
                o = ops[cur]
                kind, prev = bind.get(cur, ("dep", None))
                k = (o["eng"], o["line"], kind)
                a = agg.setdefault(k, [0, 0])
                a[0] += 1
                a[1] += o["cost"] + (o["lat"] if kind == "dep" else 0)
                cur = prev
            for k, (n, tot) in sorted(agg.items(), key=lambda kv: -kv[1][1])[:16]:
                print(f"  [cp] {k[0]:4s} line={k[1]} {k[2]:5s} n={n:4d} "
                      f"tot={tot / 1e3:7.1f}us")
        self._last_pops = pops
        return order

    def emit(self, block, make_sem, handles):
        regions = self.regions + [len(self.ops)]
        region_lists = []
        count_of = {}
        counts = {}
        snapshots = []
        lo = 0
        self._pop_lists = []
        for hi in regions:
            order = self._schedule(lo, hi)
            self._pop_lists.append(self._last_pops)
            region_lists.append(order)
            for e in ENGINES:
                for i in order[e]:
                    sig = self.ops[i]["sig"]
                    counts[sig] = counts.get(sig, 0) + self.ops[i]["inc"]
                    count_of[i] = counts[sig]
            snapshots.append(dict(counts))
            lo = hi
        sems = {name: make_sem(name) for name in counts}
        final = dict(counts)

        ops = self.ops
        KNOW = {}
        KS = {}
        hw = {e: {} for e in ENGINES}
        fence_waits = {}
        for ri, pops in enumerate(self._pop_lists):
            if ri > 0:
                snap = snapshots[ri - 1]
                for e in ENGINES:
                    fw = []
                    h = hw[e]
                    for sig2, c in sorted(snap.items(), key=lambda kv: str(kv[0])):
                        if sig2 != e and c > h.get(sig2, 0):
                            fw.append((sig2, c))
                        if h.get(sig2, 0) < c:
                            h[sig2] = c
                    fence_waits[(ri, e)] = fw
            for i in pops:
                o = ops[i]
                e = o["eng"]
                needed = {}
                for j in o["deps"]:
                    sj = ops[j]["sig"]
                    if sj == "pe" and e == "pe":
                        continue
                    v = count_of[j]
                    if v > needed.get(sj, 0):
                        needed[sj] = v
                emitted = []
                h = hw[e]
                for sig2, v in sorted(needed.items(), key=lambda kv: str(kv[0])):
                    if v > h.get(sig2, 0):
                        emitted.append((sig2, v))
                        for s2, c2 in KNOW.get((sig2, v), {sig2: v}).items():
                            if h.get(s2, 0) < c2:
                                h[s2] = c2
                        if h.get(sig2, 0) < v:
                            h[sig2] = v
                o["waits_final"] = emitted
                ki = dict(h)
                own = count_of[i]
                if ki.get(o["sig"], 0) < own:
                    ki[o["sig"]] = own
                prev = KS.get(o["sig"])
                if prev:
                    for s2, c2 in prev.items():
                        if ki.get(s2, 0) < c2:
                            ki[s2] = c2
                KS[o["sig"]] = ki
                KNOW[(o["sig"], own)] = ki
        nw = sum(len(o.get("waits_final", ())) for o in ops)
        print(f"[emit] waits after transitive elimination: {nw}")

        def run_stream(eng):
            eh = handles[eng]
            for ri, order in enumerate(region_lists):
                if ri > 0:
                    for sig2, c in fence_waits[(ri, eng)]:
                        eh.wait_ge(sems[sig2], c)
                for i in order[eng]:
                    o = self.ops[i]
                    for sig2, vv in o["waits_final"]:
                        eh.wait_ge(sems[sig2], vv)
                    o["fn"]().then_inc(sems[o["sig"]], o["inc"])
            for sig2, c in sorted(final.items(), key=lambda kv: str(kv[0])):
                if sig2 != eng and c > hw[eng].get(sig2, 0):
                    eh.wait_ge(sems[sig2], c)

        for eng in ENGINES:
            deco = {"pe": block.tensor, "act": block.scalar,
                    "dve": block.vector, "pool": block.gpsimd,
                    "sp": block.sync}[eng]

            def body(handle, _eng=eng):
                run_stream(_eng)
            deco(body)


def build_program(n_chunks=N_CHUNKS, debug=False):
    nc = bass.Bass("TRN2", target_bir_lowering=False)
    SEG = n_chunks * CH

    xin = nc.dram_tensor("x_seg", [SEG, HID], F32, kind="ExternalInput")
    xpre_d = nc.dram_tensor("x_pre", [SEG, HID], F32, kind="ExternalInput")
    wq_d = nc.dram_tensor("Wq", [HID, 1024], F32, kind="ExternalInput")
    wk_d = nc.dram_tensor("Wk", [HID, 1024], F32, kind="ExternalInput")
    wv_d = nc.dram_tensor("Wv", [HID, 2048], F32, kind="ExternalInput")
    wg_d = nc.dram_tensor("Wg", [HID, 2048], F32, kind="ExternalInput")
    wgk1_d = nc.dram_tensor("Wgk1", [HID, 16], F32, kind="ExternalInput")
    wgk2x_d = nc.dram_tensor("Wgk2x", [17, 1024], BF16, kind="ExternalInput")
    onesrow_d = nc.dram_tensor("onesrow", [1, 128], BF16, kind="ExternalInput")
    biasrow_d = nc.dram_tensor("biasrow", [1, 1024], BF16, kind="ExternalInput")
    gnw_d = nc.dram_tensor("gnw", [1, 512], F32, kind="ExternalInput")
    wo_d = nc.dram_tensor("Wo", [2048, 1024], F32, kind="ExternalInput")
    wf_d = nc.dram_tensor("Wffn", [1024, 1024], F32, kind="ExternalInput")
    bf_d = nc.dram_tensor("bffn", [1, 1024], F32, kind="ExternalInput")
    role_d = nc.dram_tensor("role", [1, 1], F32, kind="ExternalInput")
    triu_d = nc.dram_tensor("triu", [128, 128], BF16, kind="ExternalInput")
    mask4_d = nc.dram_tensor("mask4", [128, 512], BF16, kind="ExternalInput")
    ones128_d = nc.dram_tensor("ones128", [128, 128], BF16, kind="ExternalInput")
    onescol_d = nc.dram_tensor("onescol", [128, 1], BF16, kind="ExternalInput")

    y = nc.dram_tensor("y_seg", [SEG, HID], F32, kind="ExternalOutput")
    spill_kind = "ExternalOutput" if debug else "Internal"
    osp_d = nc.dram_tensor("o_sp", [SEG, DV], BF16, kind=spill_kind)
    hsp_d = nc.dram_tensor("h_sp", [SEG, HID], BF16, kind=spill_kind)
    if debug:
        sdump_d = nc.dram_tensor("s_dump", [128, 4096], F32, kind="ExternalOutput")
        smid_d = nc.dram_tensor("smid_dump", [128, 4096], F32, kind="ExternalOutput")

    from contextlib import ExitStack
    with ExitStack() as es:
        def sb(name, shape, dtype, **kw):
            return es.enter_context(nc.sbuf_tensor(name, shape, dtype, **kw))

        triu = sb("triu_sb", [128, 128], BF16)
        mask4 = sb("mask4_sb", [128, 512], BF16)
        ones128 = sb("ones128_sb", [128, 128], BF16)
        onescol = sb("onescol_sb", [128, 1], BF16)
        gnwc = sb("gnwc", [128, 4], F32)
        rolec = sb("rolec", [128, 1], F32)
        magic = sb("magic", [128, 4], I32)
        ebl4 = [sb(f"ebl{i}", [128, 8], F32) for i in range(4)]
        ecc4 = [sb(f"ecc{i}", [128, 8], F32) for i in range(4)]
        t8a = sb("t8a", [128, 4], F32)
        t8i = sb("t8i", [128, 4], I32)
        t8c = sb("t8c", [128, 4], F32)
        t8j = sb("t8j", [128, 4], I32)
        t8e = sb("t8e", [128, 4], F32)
        t8f = sb("t8f", [128, 4], I32)
        lnc = sb("lnc", [128, 16], F32)
        lnc2 = sb("lnc2", [128, 16], F32)
        ccols = sb("ccols", [128, 8], F32)
        zeros1k = sb("zeros1k", [128, 1024], F32)
        dump = sb("dump", [128, 512], F32)

        xt = [sb(f"xt{i}", [128, 1024], F32) for i in range(4)]
        hln = [sb(f"hln{i}", [128, 1024], BF16) for i in range(2)]
        hfm = [sb(f"hfm{i}", [128, 1024], BF16, is_transpose_dest=True) for i in range(4)]

        PT = [es.enter_context(nc.psum_tensor(f"PT{i}", [128, 1024], F32)) for i in range(4)]
        def PBv(i):
            return PT[i // 2][:, 512 * (i % 2):512 * (i % 2 + 1)]
        PB = [PBv(i) for i in range(8)]
        PBH = [f"PT{i // 2}:{i % 2}" for i in range(8)]
        PTH = [[f"PT{j}:0", f"PT{j}:1"] for j in range(4)]

        E = Emitter(nc)
        v = nc.vector
        s = nc.scalar
        t = nc.tensor
        g = nc.gpsimd
        sy = nc.sync

        def quake_rsqrt(dst, dst_h, src, src_h, tf, ti, n_iter=1):
            w = dst.shape[1]
            E.op("dve", lambda: v.tensor_single_scalar(
                out=ti[:, :w], in_=src.bitcast(I32), scalar=1,
                op=ALU.logical_shift_right), reads=[src_h], writes=[ti], cost=150)
            E.op("dve", lambda: v.tensor_tensor(
                out=dst.bitcast(I32), in0=magic[:, :w], in1=ti[:, :w],
                op=ALU.subtract), reads=[magic, ti], writes=[dst_h], cost=150)
            for _ in range(n_iter):
                E.op("dve", lambda: v.tensor_tensor(
                    out=tf[:, :w], in0=src, in1=dst, op=ALU.mult),
                    reads=[src_h, dst_h], writes=[tf], cost=150)
                E.op("dve", lambda: v.tensor_tensor(
                    out=tf[:, :w], in0=tf[:, :w], in1=dst, op=ALU.mult),
                    reads=[tf, dst_h], writes=[tf], cost=150)
                E.op("dve", lambda: v.tensor_scalar(
                    out=tf[:, :w], in0=tf[:, :w], scalar1=-0.5,
                    scalar2=1.5, op0=ALU.mult, op1=ALU.add),
                    reads=[tf], writes=[tf], cost=150)
                E.op("dve", lambda: v.tensor_tensor(
                    out=dst, in0=dst, in1=tf[:, :w], op=ALU.mult),
                    reads=[dst_h, tf], writes=[dst_h], cost=150)

        def layernorm(x_ap, out_ap, rh, wh, scr, tf, ti):
            E.op("dve", lambda: v.bn_stats(out=scr[:, 0:6], in_=x_ap[:, 0:512]),
                 reads=rh, writes=[scr], cost=600)
            E.op("dve", lambda: v.bn_stats(out=scr[:, 6:12], in_=x_ap[:, 512:1024]),
                 reads=rh, writes=[scr], cost=600)
            E.op("dve", lambda: v.bn_aggr(out=scr[:, 12:14], in_=scr[:, 0:12]),
                 reads=[scr], writes=[scr], cost=200)
            E.op("dve", lambda: v.tensor_scalar(
                out=scr[:, 14:15], in0=scr[:, 13:14], scalar1=1.0, scalar2=EPS,
                op0=ALU.mult, op1=ALU.add), reads=[scr], writes=[scr], cost=150)
            quake_rsqrt(scr[:, 15:16], scr, scr[:, 14:15], scr, tf, ti)
            E.op("dve", lambda: v.scalar_tensor_tensor(
                out=scr[:, 14:15], in0=scr[:, 12:13], scalar=-1.0, in1=scr[:, 15:16],
                op0=ALU.mult, op1=ALU.mult), reads=[scr], writes=[scr], cost=150)
            E.op("act", lambda: s.activation(out=out_ap[:, :], in_=x_ap[:, :], func=AF.Identity,
                                             scale=scr[:, 15:16], bias=scr[:, 14:15]),
                 reads=rh + [scr], writes=wh)

        def dmaT(dst, src, blocks, stream, eng="sp"):
            issuer = sy if eng == "sp" else s
            E.op(eng, lambda: issuer.dma_start(
                out=dst[:, :].rearrange("p (c t) -> p c t", c=blocks),
                in_=src[:, :].rearrange("t (c p) -> t c p", c=blocks),
                transpose=True), reads=[src], writes=[dst], stream=stream,
                cost=1400, lat=1500)

        E.op("pool", lambda: g.dma_start(out=triu[:, :], in_=triu_d[:, :]), writes=[triu], stream="c0")
        E.op("pool", lambda: g.dma_start(out=mask4[:, :], in_=mask4_d[:, :]), writes=[mask4], stream="c1")
        E.op("pool", lambda: g.dma_start(out=ones128[:, :], in_=ones128_d[:, :]), writes=[ones128], stream="c2")
        E.op("pool", lambda: g.dma_start(out=onescol[:, :], in_=onescol_d[:, :]), writes=[onescol], stream="c3")
        E.op("pool", lambda: g.dma_start(out=rolec[:, :], in_=role_d[:, :].broadcast_to([128, 1])),
             writes=[rolec], stream="c2")
        E.op("pool", lambda: g.dma_start(
            out=gnwc[:, :], in_=gnw_d[:, :].rearrange("o (c p) -> p (o c)", p=128),
            allow_slow_non_contiguous=True), writes=[gnwc], stream="c3")
        E.op("dve", lambda: v.memset(magic[:, :], MAGIC), writes=[magic])
        E.op("dve", lambda: v.memset(zeros1k[:, :], 0.0), writes=[zeros1k], cost=1200)
        for _i, _v in enumerate([-LN16, 1.0 / 16.0, -1.0 / 16.0, 1.0 / 32.0, -1.0 / 32.0, 0.5]):
            E.op("dve", lambda _i=_i, _v=_v: v.memset(ccols[:, _i:_i + 1], _v), writes=[ccols])
        C_NLN16 = ccols[:, 0:1]
        C_116 = ccols[:, 1:2]
        C_N116 = ccols[:, 2:3]
        C_132 = ccols[:, 3:4]
        C_N132 = ccols[:, 4:5]
        C_HALF = ccols[:, 5:6]
        with ExitStack() as esA:
            def sbA(name, shape, dtype, **kw):
                return esA.enter_context(nc.sbuf_tensor(name, shape, dtype, **kw))

            S8 = [sbA(f"S{i}", [128, 512], F32) for i in range(8)]
            S8b = [sbA(f"Sb{i}", [128, 512], BF16) for i in range(8)]
            for _i in range(8):
                E.op("dve", lambda _i=_i: v.memset(S8[_i][:, :], 0.0), writes=[S8[_i]])

            wq = sbA("wq", [128, 8 * 1024], BF16)
            wk = sbA("wk", [128, 8 * 1024], BF16)
            wv = sbA("wv", [128, 8 * 2048], BF16)
            wgk1 = sbA("wgk1", [128, 8 * 16], BF16)


            def load_w(dst, src, stream, blocks=8, lat=14000):
                E.op("pool", lambda: g.dma_start(
                    out=dst[:, :].rearrange("p (k n) -> p k n", k=blocks),
                    in_=src[:, :].rearrange("(k p) n -> p k n", p=128)),
                    writes=[dst], stream=stream, cost=1100, lat=lat)

            load_w(wk, wk_d, "wa2")
            load_w(wv, wv_d, "wa3", lat=28000)
            load_w(wgk1, wgk1_d, "wa2", lat=2000)
            wgk2x = sbA("wgk2x", [17, 1024], BF16)
            E.op("pool", lambda: g.dma_start(out=wgk2x[:, :], in_=wgk2x_d[:, :]),
                 writes=[wgk2x], stream="wa2", cost=650, lat=1500)
            A17_2 = [sbA(f"A17_{i}", [17, 128], BF16) for i in range(2)]
            for _i in range(2):
                E.op("pool", lambda _i=_i: g.dma_start(
                    out=A17_2[_i][16:17, :], in_=onesrow_d[:, :]),
                    writes=[A17_2[_i]], stream="c3", cost=650, lat=1500)
            load_w(wq, wq_d, "wa1")

            raws = sbA("raw0", [128, 1024], F32)
            raw2 = [raws, raws]
            min02 = [sbA(f"min0{i}", [128, 1024], BF16) for i in range(2)]
            FA52 = [sbA(f"FA5{i}", [128, 1024], F32) for i in range(2)]
            EcB2 = [sbA(f"EcB{i}", [128, 1024], F32) for i in range(2)]
            EnB2 = [sbA(f"EnB{i}", [128, 1024], F32) for i in range(2)]
            E1c2 = [sbA(f"E1c{i}", [128, 1024], BF16) for i in range(3)]
            Engc2 = [sbA(f"Engc{i}", [128, 1024], BF16) for i in range(3)]
            EnS2 = [sbA(f"EnS{i}", [128, 1024], BF16) for i in range(3)]
            gku2 = [sbA(f"gku{i}", [128, 1024], BF16) for i in range(3)]
            qg = sbA("qg", [128, 1024], BF16)
            ki = sbA("ki", [128, 1024], BF16)
            ks = sbA("ks", [128, 1024], BF16)
            qgfm = sbA("qgfm", [128, 1024], BF16, is_transpose_dest=True)
            kifm = sbA("kifm", [128, 1024], BF16, is_transpose_dest=True)
            vbf = sbA("vbf", [128, 2048], BF16)
            scbf = sbA("scbf", [128, 512], BF16)
            ospt = sbA("ospt", [128, 2048], BF16)


            def a_s0(i):
                c, src_d, full = descs[i]
                par = i % 4
                x_sb, hl = xt[par], hln[i % 2]
                E.op("sp", lambda x_sb=x_sb, c=c, src_d=src_d: sy.dma_start(
                    out=x_sb[:, :], in_=src_d[c * CH:(c + 1) * CH, :]),
                    writes=[x_sb], stream=f"x{par}", cost=700, lat=4500)
                layernorm(x_sb, hl, [x_sb], [hl], lnc, t8a, t8i)
                dmaT(hfm[par], hl, 8, f"th{par}")
                if full:
                    E.op("pool", lambda c=c, par=par: g.dma_start(
                        out=hsp_d[c * CH:(c + 1) * CH, :], in_=hfm[par][:, :]),
                        reads=[hfm[par]], writes=[hsp_d], stream=f"hs{c % 2}",
                        cost=650, lat=4500)

            def a_s1a(i):
                hf = hfm[i % 4]
                gku = gku2[i % 3]
                raw, min0, FA5 = raw2[i % 2], min02[i % 2], FA52[i % 2]
                for k in range(8):
                    E.op("pe", lambda k=k, hf=hf: t.matmul(
                        PB[1][:16, 0:128], wgk1[:, 16 * k:16 * (k + 1)],
                        hf[:, 128 * k:128 * (k + 1)],
                        start=(k == 0), stop=(k == 7)),
                        reads=[wgk1, hf], writes=[PBH[1]])
                A17 = A17_2[i % 2]
                E.op("act", lambda A17=A17: s.copy(out=A17[0:16, :], in_=PB[1][:16, 0:128]),
                     reads=[PBH[1]], writes=[A17], cost=200)
                for nh in range(2):
                    E.op("pe", lambda nh=nh, A17=A17: t.matmul(
                        PB[nh][:, :], A17[:, :], wgk2x[:, 512 * nh:512 * (nh + 1)],
                        start=True, stop=True),
                        reads=[A17, wgk2x], writes=[PBH[nh]])
                E.op("act", lambda: s.copy(out=raw[:, :], in_=PT[0][:, :]),
                     reads=PTH[0], writes=[raw], cost=1150)
                E.op("dve", lambda min0=min0, raw=raw: v.tensor_scalar_min(
                    out=min0[:, :], in0=raw[:, :], scalar1=0.0),
                    reads=[raw], writes=[min0])
                E.op("act", lambda: s.activation(out=FA5[:, :], in_=raw[:, :], func=AF.Abs),
                     reads=[raw], writes=[FA5])
                E.op("act", lambda: s.activation(out=raw[:, :], in_=FA5[:, :], func=AF.Exp, scale=-1.0),
                     reads=[FA5], writes=[raw])
                E.op("act", lambda: s.activation(out=FA5[:, :], in_=raw[:, :], func=AF.Ln, bias=1.0),
                     reads=[raw], writes=[FA5])
                E.op("pool", lambda gku=gku, min0=min0, FA5=FA5: g.tensor_tensor(
                    out=gku[:, :], in0=min0[:, :], in1=FA5[:, :], op=ALU.subtract),
                    reads=[min0, FA5], writes=[gku], cost=2350)

            def a_s1b(i):
                c, _, full = descs[i]
                gku = gku2[i % 3]
                E1c, Engc, EnS = E1c2[i % 3], Engc2[i % 3], EnS2[i % 3]
                if full:
                    for nh in range(2):
                        E.op("pe", lambda nh=nh, gku=gku: t.matmul(
                            PB[nh][:, :], triu[:, :], gku[:, 512 * nh:512 * (nh + 1)],
                            start=True, stop=True), reads=[triu, gku], writes=[PBH[nh]])
                    E.op("act", lambda Engc=Engc: s.activation(
                        out=Engc[:, :], in_=PT[0][:, :], func=AF.Exp,
                        scale=C_N116), reads=PTH[0] + [ccols], writes=[Engc])
                    E.op("act", lambda E1c=E1c: s.activation(
                        out=E1c[:, :], in_=PT[0][:, :], func=AF.Exp,
                        scale=C_116, bias=C_NLN16), reads=PTH[0] + [ccols], writes=[E1c])
                for nh in range(2):
                    E.op("pe", lambda nh=nh, gku=gku: t.matmul(
                        PB[nh][:, :], ones128[:, :], gku[:, 512 * nh:512 * (nh + 1)],
                        start=True, stop=True), reads=[ones128, gku], writes=[PBH[nh]])
                E.op("act", lambda EnS=EnS: s.activation(
                    out=EnS[:, :], in_=PT[0][:, :], func=AF.Exp,
                    scale=C_116), reads=PTH[0] + [ccols], writes=[EnS])
                for blk in range(8):
                    E.op("pe", lambda blk=blk, gku=gku: t.matmul(
                        PB[0][:, blk:blk + 1], gku[:, 128 * blk:128 * (blk + 1)], onescol[:, :],
                        start=True, stop=True), reads=[gku, onescol], writes=[PBH[0]])
                ebl, ecc = ebl4[i % 4], ecc4[i % 4]
                E.op("act", lambda ebl=ebl: s.activation(out=ebl[:, :], in_=PB[0][:, 0:8], func=AF.Exp,
                                                 scale=C_116),
                     reads=[PBH[0], ccols], writes=[ebl], cost=250)
                if full:
                    E.op("act", lambda ecc=ecc: s.activation(out=ecc[:, :], in_=PB[0][:, 0:8], func=AF.Exp,
                                                     scale=C_132),
                         reads=[PBH[0], ccols], writes=[ecc], cost=250)

            def a_s2(i):
                c, _, full = descs[i]
                hf = hfm[i % 4]
                E1c, Engc, EnS = E1c2[i % 3], Engc2[i % 3], EnS2[i % 3]
                for k in range(8):
                    for nh in range(2):
                        E.op("pe", lambda k=k, nh=nh, hf=hf: t.matmul(
                            PB[4 + nh][:, :], hf[:, 128 * k:128 * (k + 1)],
                            wk[:, 1024 * k + 512 * nh:1024 * k + 512 * (nh + 1)],
                            start=(k == 0), stop=(k == 7)),
                            reads=[hf, wk], writes=[PBH[4 + nh]])
                if full:
                    for k in range(8):
                        for nh in range(2):
                            E.op("pe", lambda k=k, nh=nh, hf=hf: t.matmul(
                                PB[6 + nh][:, :], hf[:, 128 * k:128 * (k + 1)],
                                wq[:, 1024 * k + 512 * nh:1024 * k + 512 * (nh + 1)],
                                start=(k == 0), stop=(k == 7)),
                                reads=[hf, wq], writes=[PBH[6 + nh]])
                    E.op("dve", lambda: v.tensor_tensor(
                        out=qg[:, :], in0=PT[3][:, :], in1=E1c[:, :], op=ALU.mult),
                        reads=PTH[3] + [E1c], writes=[qg])
                    E.op("dve", lambda: v.tensor_tensor(
                        out=ki[:, :], in0=PT[2][:, :], in1=Engc[:, :], op=ALU.mult),
                        reads=PTH[2] + [Engc], writes=[ki])
                E.op("dve", lambda: v.tensor_tensor(
                    out=ks[:, :], in0=PT[2][:, :], in1=EnS[:, :], op=ALU.mult),
                    reads=PTH[2] + [EnS], writes=[ks])
                if full:
                    dmaT(qgfm, qg, 8, "tq")
                    dmaT(kifm, ki, 8, "tk")
                for k in range(8):
                    for ns in range(4):
                        E.op("pe", lambda k=k, ns=ns, hf=hf: t.matmul(
                            PB[2 + ns][:, :], hf[:, 128 * k:128 * (k + 1)],
                            wv[:, 2048 * k + 512 * ns:2048 * k + 512 * (ns + 1)],
                            start=(k == 0), stop=(k == 7)),
                            reads=[hf, wv], writes=[PBH[2 + ns]])
                E.op("dve", lambda: v.tensor_copy(out=vbf[:, 0:1024], in_=PT[1][:, :]),
                     reads=PTH[1], writes=[vbf])
                E.op("act", lambda: s.copy(out=vbf[:, 1024:2048], in_=PT[2][:, :]),
                     reads=PTH[2], writes=[vbf])

            def a_s3(i):
                c, _, full = descs[i]
                ebl, ecc = ebl4[i % 4], ecc4[i % 4]
                if full:
                    for blk in range(8):
                        if blk % 4 == 0:
                            E.op("dve", lambda blk=blk, ecc=ecc: v.tensor_scalar_mul(
                                out=S8b[blk][:, :], in0=S8[blk][:, :],
                                scalar1=ecc[:, blk:blk + 1]),
                                reads=[S8[blk], ecc], writes=[S8b[blk]])
                        else:
                            E.op("act", lambda blk=blk, ecc=ecc: s.activation(
                                out=S8b[blk][:, :], in_=S8[blk][:, :], func=AF.Identity,
                                scale=ecc[:, blk:blk + 1]),
                                reads=[S8[blk], ecc], writes=[S8b[blk]])
                    for h in range(4):
                        for kc in range(2):
                            blk = 2 * h + kc
                            E.op("pe", lambda h=h, kc=kc, blk=blk: t.matmul(
                                PB[6][:, 128 * h:128 * (h + 1)],
                                kifm[:, 128 * blk:128 * (blk + 1)],
                                qgfm[:, 128 * blk:128 * (blk + 1)],
                                start=(kc == 0), stop=(kc == 1)),
                                reads=[kifm, qgfm], writes=[PBH[6]])
                    E.op("dve", lambda: v.tensor_tensor(out=scbf[:, :], in0=PB[6][:, :],
                                                        in1=mask4[:, :], op=ALU.mult),
                         reads=[PBH[6], mask4], writes=[scbf])
                    for h in range(4):
                        E.op("pe", lambda h=h: t.matmul(
                            PB[2 + h][:, :], scbf[:, 128 * h:128 * (h + 1)],
                            vbf[:, 512 * h:512 * (h + 1)], start=True, stop=False),
                            reads=[scbf, vbf], writes=[PBH[2 + h]])
                        for kc in range(2):
                            blk = 2 * h + kc
                            E.op("pe", lambda h=h, kc=kc, blk=blk: t.matmul(
                                PB[2 + h][:, :], qgfm[:, 128 * blk:128 * (blk + 1)],
                                S8b[blk][:, :],
                                start=False, stop=(kc == 1)),
                                reads=[qgfm, S8b[blk]], writes=[PBH[2 + h]])
                    E.op("dve", lambda: v.tensor_copy(out=ospt[:, 0:1024], in_=PT[1][:, :]),
                         reads=PTH[1], writes=[ospt])
                    E.op("act", lambda: s.copy(out=ospt[:, 1024:2048], in_=PT[2][:, :]),
                         reads=PTH[2], writes=[ospt])
                    E.op("pool", lambda c=c: g.dma_start(
                        out=osp_d[c * CH:(c + 1) * CH, :], in_=ospt[:, :]),
                        reads=[ospt], writes=[osp_d], stream="so",
                        cost=650, lat=4500)
                for h in range(4):
                    for mc in range(2):
                        blk = 2 * h + mc
                        pb = PB[6 + (blk % 2)]; pbh = PBH[6 + (blk % 2)]
                        E.op("pe", lambda blk=blk, h=h, pb=pb: t.matmul(
                            pb[:, :], ks[:, 128 * blk:128 * (blk + 1)],
                            vbf[:, 512 * h:512 * (h + 1)], start=True, stop=True),
                            reads=[ks, vbf], writes=[pbh])
                        E.op("dve", lambda blk=blk, pb=pb, ebl=ebl: v.scalar_tensor_tensor(
                            out=S8[blk][:, :],
                            in0=S8[blk][:, :],
                            scalar=ebl[:, blk:blk + 1], in1=pb[:, :],
                            op0=ALU.mult, op1=ALU.add),
                            reads=[S8[blk], ebl, pbh], writes=[S8[blk]])

            descs = [(c, xpre_d, False) for c in range(n_chunks)] + \
                    [(c, xin, True) for c in range(n_chunks)]
            N = len(descs)

            def a_mask(i_done):
                if i_done == n_chunks - 1:
                    for blk in range(8):
                        E.op("dve", lambda blk=blk: v.tensor_scalar_mul(
                            out=S8[blk][:, :], in0=S8[blk][:, :], scalar1=rolec[:, :]),
                            reads=[S8[blk], rolec], writes=[S8[blk]])
                    if debug:
                        for blk in range(8):
                            E.op("pool", lambda blk=blk: g.dma_start(
                                out=smid_d[:, 512 * blk:512 * (blk + 1)], in_=S8[blk][:, :]),
                                reads=[S8[blk]], writes=[smid_d], stream="so")

            for j in range(min(3, N)):
                a_s0(j)
                a_s1a(j)
            a_s1b(0)
            if N > 3:
                a_s0(3)
                a_s1a(3)
            if N > 1:
                a_s1b(1)
            for i in range(N):
                if i + 2 < N:
                    a_s1b(i + 2)
                if i >= 1:
                    a_s3(i - 1)
                    a_mask(i - 1)
                a_s2(i)
                if i + 4 < N:
                    a_s0(i + 4)
                    a_s1a(i + 4)
            a_s3(N - 1)
            a_mask(N - 1)

            if debug:
                for blk in range(8):
                    E.op("pool", lambda blk=blk: g.dma_start(
                        out=sdump_d[:, 512 * blk:512 * (blk + 1)], in_=S8[blk][:, :]),
                        reads=[S8[blk]], writes=[sdump_d], stream="so")

        E.barrier()

        with ExitStack() as esB:
            def sbB(name, shape, dtype, **kw):
                return esB.enter_context(nc.sbuf_tensor(name, shape, dtype, **kw))

            wg = sbB("wg", [128, 8 * 2048], BF16)
            wo = sbB("wo", [128, 16 * 1024], BF16)
            w8t = sbB("w8t", [128, 8 * 1024], BF16)
            wstage2 = [sbB(f"wstage{i}", [128, 1024], F32) for i in range(2)]
            onesrowt = sbB("onesrowt", [1, 128], BF16)
            biasrow = sbB("biasrow_sb", [1, 1024], BF16)
            E.op("pool", lambda: g.dma_start(out=onesrowt[:, :], in_=onesrow_d[:, :]),
                 writes=[onesrowt], stream="c0", cost=650, lat=1500)
            E.op("pool", lambda: g.dma_start(out=biasrow[:, :], in_=biasrow_d[:, :]),
                 writes=[biasrow], stream="c1", cost=650, lat=1500)
            x8_2 = [sbB(f"x8_{i}", [128, 1024], FP8) for i in range(2)]
            x8b_2 = [sbB(f"x8b_{i}", [128, 1024], BF16) for i in range(2)]

            for wh in range(2):
                E.op("pool", lambda wh=wh: g.dma_start(
                    out=wg[:, 8192 * wh:8192 * (wh + 1)].rearrange("p (k n) -> p k n", k=4),
                    in_=wg_d[512 * wh:512 * (wh + 1), :].rearrange("(k p) n -> p k n", p=128)),
                    writes=[wg], stream=("wa1", "wa2")[wh], cost=1100, lat=14000)
            for dc in range(16):
                ws = wstage2[dc % 2]
                E.op("pool", lambda dc=dc, ws=ws: g.dma_start(
                    out=ws[:, :], in_=wo_d[128 * dc:128 * (dc + 1), :]),
                    writes=[ws], stream=("wa3", "so")[dc % 2], cost=650, lat=4500)
                E.op("dve", lambda dc=dc, ws=ws: v.tensor_scalar_mul(
                    out=wo[:, 1024 * dc:1024 * (dc + 1)], in0=ws[:, :],
                    scalar1=gnwc[:, dc % 4:dc % 4 + 1]),
                    reads=[ws, gnwc], writes=[wo])
            for fc in range(8):
                ws = wstage2[fc % 2]
                w8q, w8b = x8_2[fc % 2], x8b_2[fc % 2]
                E.op("pool", lambda fc=fc, ws=ws: g.dma_start(
                    out=ws[:, :], in_=wf_d[128 * fc:128 * (fc + 1), :]),
                    writes=[ws], stream=("wa3", "so")[fc % 2], cost=650, lat=4500)
                E.op("dve", lambda ws=ws: v.tensor_scalar(
                    out=ws[:, :], in0=ws[:, :], scalar1=448.0, scalar2=-448.0,
                    op0=ALU.min, op1=ALU.max), reads=[ws], writes=[ws])
                E.op("dve", lambda ws=ws, w8q=w8q: v.tensor_copy(out=w8q[:, :], in_=ws[:, :]),
                     reads=[ws], writes=[w8q])
                E.op("act", lambda w8q=w8q, w8b=w8b: s.copy(out=w8b[:, :], in_=w8q[:, :]),
                     reads=[w8q], writes=[w8b])
                E.op("sp", lambda fc=fc, w8b=w8b: sy.dma_start(
                    out=w8t[:, :].rearrange("p (blk f) -> p blk f", blk=8)[:, :, 128 * fc:128 * (fc + 1)],
                    in_=w8b[:, :].rearrange("f (blk p) -> f blk p", blk=8),
                    transpose=True), reads=[w8b], writes=[w8t], stream=("tq", "tk")[fc % 2],
                    cost=1400, lat=1500)

            G1_2 = [sbB(f"G1_{i}", [128, 2048], BF16) for i in range(3)]
            G2s = sbB("G2_0", [128, 2048], BF16)
            G2_2 = [G2s, G2s]
            ogs = sbB("og0", [128, 2048], BF16)
            og2 = [ogs, ogs]
            gate2 = [sbB(f"gate{i}", [128, 2048], BF16) for i in range(3)]
            ogfm2 = [sbB(f"ogfm{i}", [128, 2048], BF16, is_transpose_dest=True) for i in range(2)]
            oloc = [sbB(f"oloc{i}", [128, 2048], BF16) for i in range(4)]
            x8fms = sbB("x8fm0", [128, 1024], BF16, is_transpose_dest=True)
            x8fm2 = [x8fms, x8fms]
            racc2 = [sbB(f"racc{i}", [128, 4], F32) for i in range(2)]
            rstd2 = [sbB(f"rstd{i}", [128, 4], F32) for i in range(2)]
            XM2 = [sbB(f"XM{i}", [128, 1024], F32) for i in range(2)]
            XLns = sbB("XLn0", [128, 1024], F32)
            XLn2 = [XLns, XLns]
            OUT2 = wstage2

            def b_t0(c):
                par = c % 4
                x_sb = xt[par]
                E.op("sp", lambda c=c, x_sb=x_sb: sy.dma_start(
                    out=x_sb[:, :], in_=xin[c * CH:(c + 1) * CH, :]),
                    writes=[x_sb], stream=f"x{par}", cost=700, lat=4500)
                E.op("pool", lambda c=c: g.dma_start(
                    out=oloc[c % 4][:, :], in_=osp_d[c * CH:(c + 1) * CH, :]),
                    reads=[osp_d], writes=[oloc[c % 4]], stream=f"ol{c % 4}",
                    cost=650, lat=4500),
                E.op("sp", lambda c=c, par=par: sy.dma_start(
                    out=hfm[par][:, :], in_=hsp_d[c * CH:(c + 1) * CH, :]),
                    reads=[hsp_d], writes=[hfm[par]], stream=f"th{par % 3}",
                    cost=700, lat=4500)

            def b_t1a(c):
                hf = hfm[c % 4]
                G1 = G1_2[c % 3]
                for k in range(8):
                    for ns in range(4):
                        E.op("pe", lambda k=k, ns=ns, hf=hf: t.matmul(
                            PB[ns][:, :], hf[:, 128 * k:128 * (k + 1)],
                            wg[:, 2048 * k + 512 * ns:2048 * k + 512 * (ns + 1)],
                            start=(k == 0), stop=(k == 7)),
                            reads=[hf, wg], writes=[PBH[ns]])
                E.op("act", lambda G1=G1: s.copy(out=G1[:, 0:1024], in_=PT[0][:, :]),
                     reads=PTH[0], writes=[G1])
                E.op("act", lambda G1=G1: s.copy(out=G1[:, 1024:2048], in_=PT[1][:, :]),
                     reads=PTH[1], writes=[G1])

            def b_t1b(c):
                G1, G2 = G1_2[c % 3], G2_2[c % 2]
                E.op("act", lambda G1=G1, G2=G2: s.activation(out=G2[:, :], in_=G1[:, :], func=AF.Tanh,
                                                 scale=C_HALF),
                     reads=[G1, ccols], writes=[G2], cost=2100)
                gb = gate2[c % 3]
                E.op("dve", lambda gb=gb, G1=G1, G2=G2: v.scalar_tensor_tensor(
                    out=gb[:, :], in0=G2[:, :], scalar=1.0, in1=G1[:, :],
                    op0=ALU.add, op1=ALU.mult),
                    reads=[G1, G2], writes=[gb], cost=1300)

            def b_t2(c):
                ol = oloc[c % 4]
                racc, rstd = racc2[c % 2], rstd2[c % 2]
                og, ogfm = og2[c % 2], ogfm2[c % 2]
                gb = gate2[c % 3]
                XM = XM2[c % 2]
                x8, x8b = x8_2[c % 2], x8b_2[c % 2]
                x8fm = x8fm2[c % 2]
                for h in range(4):
                    E.op("act", lambda h=h, ol=ol, racc=racc: s.activation(
                        out=dump[:, :], in_=ol[:, 512 * h:512 * (h + 1)], func=AF.Square,
                        accum_out=racc[:, h:h + 1]),
                        reads=[ol], writes=[dump, racc])
                E.op("dve", lambda racc=racc: v.tensor_scalar(
                    out=racc[:, :], in0=racc[:, :], scalar1=4.0 / 512.0, scalar2=4.0 * EPS,
                    op0=ALU.mult, op1=ALU.add), reads=[racc], writes=[racc])
                quake_rsqrt(rstd[:, :], rstd, racc[:, :], racc, t8e, t8f)
                for h in range(4):
                    E.op("dve", lambda h=h, ol=ol, og=og, rstd=rstd, gb=gb: v.scalar_tensor_tensor(
                        out=og[:, 512 * h:512 * (h + 1)], in0=ol[:, 512 * h:512 * (h + 1)],
                        scalar=rstd[:, h:h + 1], in1=gb[:, 512 * h:512 * (h + 1)],
                        op0=ALU.mult, op1=ALU.mult),
                        reads=[ol, rstd, gb], writes=[og])
                dmaT(ogfm, og, 16, f"to{c % 2}")
                for dc in range(16):
                    for nh in range(2):
                        E.op("pe", lambda dc=dc, nh=nh, ogfm=ogfm: t.matmul(
                            PB[4 + nh][:, :], ogfm[:, 128 * dc:128 * (dc + 1)],
                            wo[:, 1024 * dc + 512 * nh:1024 * dc + 512 * (nh + 1)],
                            start=(dc == 0), stop=(dc == 15)),
                            reads=[ogfm, wo], writes=[PBH[4 + nh]])
                E.op("dve", lambda c=c, XM=XM: v.tensor_tensor(
                    out=XM[:, :], in0=PT[2][:, :], in1=xt[c % 4][:, :], op=ALU.add),
                    reads=PTH[2] + [xt[c % 4]], writes=[XM])
                XLn = XLn2[c % 2]
                layernorm(XM, XLn, [XM], [XLn], lnc2, t8c, t8j)
                E.op("dve", lambda XLn=XLn, x8=x8: v.tensor_copy(out=x8[:, :], in_=XLn[:, :]),
                     reads=[XLn], writes=[x8])
                E.op("act", lambda x8=x8, x8b=x8b: s.copy(out=x8b[:, :], in_=x8[:, :]),
                     reads=[x8], writes=[x8b])
                dmaT(x8fm, x8b, 8, f"t8{c % 2}")

            def b_t3(c):
                XM = XM2[c % 2]
                OUT = OUT2[c % 2]
                x8fm = x8fm2[c % 2]
                for nh in range(2):
                    E.op("pe", lambda nh=nh: t.matmul(
                        PB[6 + nh][:, :], onesrowt[:, :],
                        biasrow[:, 512 * nh:512 * (nh + 1)],
                        start=True, stop=False),
                        reads=[onesrowt, biasrow], writes=[PBH[6 + nh]], cost=120)
                for k in range(8):
                    for nh in range(2):
                        E.op("pe", lambda k=k, nh=nh, x8fm=x8fm: t.matmul(
                            PB[6 + nh][:, :], x8fm[:, 128 * k:128 * (k + 1)],
                            w8t[:, 1024 * k + 512 * nh:1024 * k + 512 * (nh + 1)],
                            start=False, stop=(k == 7)),
                            reads=[x8fm, w8t], writes=[PBH[6 + nh]])
                E.op("dve", lambda XM=XM, OUT=OUT: v.tensor_tensor(
                    out=OUT[:, :], in0=PT[3][:, :], in1=XM[:, :], op=ALU.add),
                    reads=PTH[3] + [XM], writes=[OUT])
                E.op("pool", lambda c=c, OUT=OUT: g.dma_start(
                    out=y[c * CH:(c + 1) * CH, :], in_=OUT[:, :]),
                    reads=[OUT], writes=[y], stream=f"y{c % 2}",
                    cost=650, lat=4500)

            for j in range(min(3, n_chunks)):
                b_t0(j)
            for j in range(min(3, n_chunks)):
                b_t1a(j)
            for j in range(min(2, n_chunks)):
                b_t1b(j)
            for c in range(n_chunks):
                if c + 3 < n_chunks:
                    b_t0(c + 3)
                    b_t1a(c + 3)
                if c + 2 < n_chunks:
                    b_t1b(c + 2)
                if c >= 1:
                    b_t3(c - 1)
                b_t2(c)
            b_t3(n_chunks - 1)

        block = es.enter_context(nc.Block())
        handles = {"pe": nc.tensor, "act": nc.scalar, "dve": nc.vector,
                   "pool": nc.gpsimd, "sp": nc.sync}
        E.emit(block, lambda name: es.enter_context(nc.semaphore(f"sem_{name}")), handles)

    return nc


def _consts():
    tri = np.triu(np.ones((128, 128), dtype=np.float32))
    mask4 = np.tile(tri, (1, 4)).astype(ml_dtypes.bfloat16)
    mmat = (tri - 0.5).astype(ml_dtypes.bfloat16)
    nmat = (1.0 - tri).astype(ml_dtypes.bfloat16)
    onescol = np.ones((128, 1), dtype=ml_dtypes.bfloat16)
    return mmat, mask4, nmat, onescol


def make_in_maps(x, inputs, n_chunks=N_CHUNKS, n_cores=8):
    tri, mask4, ones128, onescol = _consts()
    seg = n_chunks * CH
    common = dict(
        Wq=np.asarray(inputs["L0_attn_Wq"], np.float32),
        Wk=np.asarray(inputs["L0_attn_Wk"], np.float32),
        Wv=np.asarray(inputs["L0_attn_Wv"], np.float32),
        Wg=np.asarray(inputs["L0_attn_Wg"], np.float32),
        Wgk1=np.asarray(inputs["L0_attn_Wgk1"], np.float32),
        Wgk2x=np.vstack([
            np.asarray(inputs["L0_attn_Wgk2"], np.float32),
            np.asarray(inputs["L0_attn_bgk2"], np.float32).reshape(1, 1024),
        ]).astype(ml_dtypes.bfloat16),
        onesrow=np.ones((1, 128), dtype=ml_dtypes.bfloat16),
        biasrow=np.asarray(inputs["L0_ffn_bias"], np.float32).reshape(1, 1024).astype(ml_dtypes.bfloat16),
        gnw=np.asarray(inputs["L0_attn_g_norm_w"], np.float32).reshape(1, 512),
        Wo=np.asarray(inputs["L0_attn_Wo"], np.float32),
        Wffn=np.asarray(inputs["L0_ffn_weight"], np.float32),
        bffn=np.asarray(inputs["L0_ffn_bias"], np.float32).reshape(1, 1024),
        triu=tri, mask4=mask4, ones128=ones128, onescol=onescol,
    )
    zeros = np.zeros((seg, HID), np.float32)
    in_maps = []
    for core in range(n_cores):
        b, h = core // 2, core % 2
        m = dict(common)
        m["x_seg"] = np.ascontiguousarray(x[b, h * seg:(h + 1) * seg, :])
        m["x_pre"] = np.ascontiguousarray(x[b, 0:seg, :]) if h == 1 else zeros
        m["role"] = np.array([[float(h)]], np.float32)
        in_maps.append(m)
    return in_maps


def kernel(L0_attn_hidden_states, L0_attn_Wq, L0_attn_Wk, L0_attn_Wv, L0_attn_Wg,
           L0_attn_Wgk1, L0_attn_Wgk2, L0_attn_bgk2, L0_attn_g_norm_w, L0_attn_Wo,
           L0_ffn_weight, L0_ffn_bias):
    x = np.asarray(L0_attn_hidden_states, dtype=np.float32)
    assert x.shape == (4, 4096, 1024)
    inputs = dict(
        L0_attn_Wq=L0_attn_Wq, L0_attn_Wk=L0_attn_Wk, L0_attn_Wv=L0_attn_Wv,
        L0_attn_Wg=L0_attn_Wg, L0_attn_Wgk1=L0_attn_Wgk1, L0_attn_Wgk2=L0_attn_Wgk2,
        L0_attn_bgk2=L0_attn_bgk2, L0_attn_g_norm_w=L0_attn_g_norm_w,
        L0_attn_Wo=L0_attn_Wo, L0_ffn_weight=L0_ffn_weight, L0_ffn_bias=L0_ffn_bias,
    )
    nc = build_program()
    in_maps = make_in_maps(x, inputs)
    res = run_bass_kernel_spmd(nc, in_maps, core_ids=list(range(8)))
    out = np.empty((4, 4096, 1024), np.float32)
    for core in range(8):
        b, h = core // 2, core % 2
        out[b, h * 2048:(h + 1) * 2048, :] = res.results[core]["y_seg"]
    return out


# version 36 (final)
# speedup vs baseline: 1.0399x; 1.0017x over previous
import math
import numpy as np
import ml_dtypes

import concourse.bass as bass
import concourse.mybir as mybir
from concourse.bass_utils import run_bass_kernel_spmd

F32 = mybir.dt.float32
BF16 = mybir.dt.bfloat16
FP8 = mybir.dt.float8e4
I32 = mybir.dt.int32
AF = mybir.ActivationFunctionType
ALU = mybir.AluOpType

HID = 1024
DV = 2048
CH = 128
N_CHUNKS = 16
EPS = 1e-5
LN16 = math.log(16.0)
MAGIC = 0x5F3759DF

ENGINES = ("pe", "act", "dve", "pool", "sp")


class Emitter:

    DUR = {"pe": 240, "dve": 900, "act": 1150, "pool": 2400, "sp": 1000}
    XLAT = 350

    def __init__(self, nc):
        self.nc = nc
        self.ops = []
        self.last_write = {}
        self.readers = {}
        self.last_on = {}
        self.regions = []

    @staticmethod
    def _key(b):
        if isinstance(b, str):
            return b
        t = getattr(b, "tensor", b)
        n = getattr(t, "name", None)
        return n if n is not None else id(t)

    def op(self, eng, fn, reads=(), writes=(), stream=None, inc=1, cost=None,
           lat=0):
        import sys
        line = sys._getframe(1).f_lineno
        idx = len(self.ops)
        deps = set()
        if stream is not None:
            inc = 16
        rkeys = [self._key(b) for b in reads]
        wkeys = [self._key(b) for b in writes]
        for kb in rkeys:
            lw = self.last_write.get(kb)
            if lw is not None:
                deps.add(lw)
        for kb in wkeys:
            lw = self.last_write.get(kb)
            if lw is not None:
                deps.add(lw)
            for r in self.readers.get(kb, ()):
                deps.add(r)
        sig = stream if stream is not None else eng
        if stream is not None and sig in self.last_on:
            deps.add(self.last_on[sig])
        if stream is not None:
            self.last_on[sig] = idx
        for kb in rkeys:
            self.readers.setdefault(kb, []).append(idx)
        for kb in wkeys:
            self.last_write[kb] = idx
            self.readers[kb] = []
        self.ops.append({"eng": eng, "fn": fn, "sig": sig, "inc": inc,
                         "deps": deps, "cost": cost or self.DUR[eng],
                         "lat": lat, "line": line})

    def barrier(self):
        self.regions.append(len(self.ops))

    def _schedule(self, lo, hi):
        import heapq
        ops = self.ops
        indeg = {}
        succ = {i: [] for i in range(lo, hi)}
        for i in range(lo, hi):
            d = [j for j in ops[i]["deps"] if lo <= j < hi]
            indeg[i] = len(d)
            for j in d:
                succ[j].append(i)
        finish = {j: 0 for j in range(lo)}
        engine_time = {e: 0 for e in ENGINES}
        bind = {}
        blevel = {}
        for i in range(hi - 1, lo - 1, -1):
            blevel[i] = ops[i]["cost"] + ops[i]["lat"] + max(
                (blevel[k] for k in succ[i]), default=0)
        ready = []
        for i in range(lo, hi):
            if indeg[i] == 0:
                heapq.heappush(ready, (0, -blevel[i], i))
        order = {e: [] for e in ENGINES}
        pops = []
        scheduled = 0
        while ready:
            _, _, i = heapq.heappop(ready)
            pops.append(i)
            o = ops[i]
            e = o["eng"]
            dep_t = 0
            dep_i = None
            for j in o["deps"]:
                if j < lo:
                    continue
                f = finish.get(j)
                if ops[j]["eng"] != e:
                    lat = self.XLAT
                elif e == "pe":
                    lat = 0
                else:
                    lat = 150
                if f + lat > dep_t:
                    dep_t = f + lat
                    dep_i = j
            if dep_t > engine_time[e]:
                bind[i] = ("dep", dep_i)
            else:
                bind[i] = ("queue", order[e][-1] if order[e] else None)
            st = max(engine_time[e], dep_t)
            fin = st + o["cost"]
            engine_time[e] = fin
            finish[i] = fin + o["lat"]
            order[e].append(i)
            scheduled += 1
            for k in succ[i]:
                indeg[k] -= 1
                if indeg[k] == 0:
                    dt = max((finish.get(j, 0) for j in ops[k]["deps"] if j >= lo),
                             default=0)
                    heapq.heappush(ready, (dt, -blevel[k], k))
        assert scheduled == hi - lo, (scheduled, hi - lo)
        busy = {e: 0 for e in ENGINES}
        for i in range(lo, hi):
            busy[ops[i]["eng"]] += ops[i]["cost"]
        print(f"[sched] region {lo}:{hi} predicted makespan "
              f"{max(engine_time.values()) / 1e3:.1f}us "
              + " ".join(f"{e}={engine_time[e] / 1e3:.0f}" for e in ENGINES)
              + " | busy "
              + " ".join(f"{e}={busy[e] / 1e3:.0f}" for e in ENGINES))
        import os
        if os.environ.get("SCHED_CP"):
            cur = max((i for i in range(lo, hi)), key=lambda i: finish[i])
            agg = {}
            while cur is not None and cur >= lo:
                o = ops[cur]
                kind, prev = bind.get(cur, ("dep", None))
                k = (o["eng"], o["line"], kind)
                a = agg.setdefault(k, [0, 0])
                a[0] += 1
                a[1] += o["cost"] + (o["lat"] if kind == "dep" else 0)
                cur = prev
            for k, (n, tot) in sorted(agg.items(), key=lambda kv: -kv[1][1])[:16]:
                print(f"  [cp] {k[0]:4s} line={k[1]} {k[2]:5s} n={n:4d} "
                      f"tot={tot / 1e3:7.1f}us")
        self._last_pops = pops
        return order

    def emit(self, block, make_sem, handles):
        regions = self.regions + [len(self.ops)]
        region_lists = []
        count_of = {}
        counts = {}
        snapshots = []
        lo = 0
        self._pop_lists = []
        for hi in regions:
            order = self._schedule(lo, hi)
            self._pop_lists.append(self._last_pops)
            region_lists.append(order)
            for e in ENGINES:
                for i in order[e]:
                    sig = self.ops[i]["sig"]
                    counts[sig] = counts.get(sig, 0) + self.ops[i]["inc"]
                    count_of[i] = counts[sig]
            snapshots.append(dict(counts))
            lo = hi
        sems = {name: make_sem(name) for name in counts}
        final = dict(counts)

        ops = self.ops
        KNOW = {}
        KS = {}
        hw = {e: {} for e in ENGINES}
        fence_waits = {}
        for ri, pops in enumerate(self._pop_lists):
            if ri > 0:
                snap = snapshots[ri - 1]
                for e in ENGINES:
                    fw = []
                    h = hw[e]
                    for sig2, c in sorted(snap.items(), key=lambda kv: str(kv[0])):
                        if sig2 != e and c > h.get(sig2, 0):
                            fw.append((sig2, c))
                        if h.get(sig2, 0) < c:
                            h[sig2] = c
                    fence_waits[(ri, e)] = fw
            for i in pops:
                o = ops[i]
                e = o["eng"]
                needed = {}
                for j in o["deps"]:
                    sj = ops[j]["sig"]
                    if sj == "pe" and e == "pe":
                        continue
                    v = count_of[j]
                    if v > needed.get(sj, 0):
                        needed[sj] = v
                emitted = []
                h = hw[e]
                for sig2, v in sorted(needed.items(), key=lambda kv: str(kv[0])):
                    if v > h.get(sig2, 0):
                        emitted.append((sig2, v))
                        for s2, c2 in KNOW.get((sig2, v), {sig2: v}).items():
                            if h.get(s2, 0) < c2:
                                h[s2] = c2
                        if h.get(sig2, 0) < v:
                            h[sig2] = v
                o["waits_final"] = emitted
                ki = dict(h)
                own = count_of[i]
                if ki.get(o["sig"], 0) < own:
                    ki[o["sig"]] = own
                prev = KS.get(o["sig"])
                if prev:
                    for s2, c2 in prev.items():
                        if ki.get(s2, 0) < c2:
                            ki[s2] = c2
                KS[o["sig"]] = ki
                KNOW[(o["sig"], own)] = ki
        nw = sum(len(o.get("waits_final", ())) for o in ops)
        print(f"[emit] waits after transitive elimination: {nw}")

        def run_stream(eng):
            eh = handles[eng]
            for ri, order in enumerate(region_lists):
                if ri > 0:
                    for sig2, c in fence_waits[(ri, eng)]:
                        eh.wait_ge(sems[sig2], c)
                for i in order[eng]:
                    o = self.ops[i]
                    for sig2, vv in o["waits_final"]:
                        eh.wait_ge(sems[sig2], vv)
                    o["fn"]().then_inc(sems[o["sig"]], o["inc"])
            for sig2, c in sorted(final.items(), key=lambda kv: str(kv[0])):
                if sig2 != eng and c > hw[eng].get(sig2, 0):
                    eh.wait_ge(sems[sig2], c)

        for eng in ENGINES:
            deco = {"pe": block.tensor, "act": block.scalar,
                    "dve": block.vector, "pool": block.gpsimd,
                    "sp": block.sync}[eng]

            def body(handle, _eng=eng):
                run_stream(_eng)
            deco(body)


def build_program(n_chunks=N_CHUNKS, debug=False):
    nc = bass.Bass("TRN2", target_bir_lowering=False)
    SEG = n_chunks * CH

    xin = nc.dram_tensor("x_seg", [SEG, HID], F32, kind="ExternalInput")
    xpre_d = nc.dram_tensor("x_pre", [SEG, HID], F32, kind="ExternalInput")
    wq_d = nc.dram_tensor("Wq", [HID, 1024], F32, kind="ExternalInput")
    wk_d = nc.dram_tensor("Wk", [HID, 1024], F32, kind="ExternalInput")
    wv_d = nc.dram_tensor("Wv", [HID, 2048], F32, kind="ExternalInput")
    wg_d = nc.dram_tensor("Wg", [HID, 2048], F32, kind="ExternalInput")
    wgk1_d = nc.dram_tensor("Wgk1", [HID, 16], F32, kind="ExternalInput")
    wgk2x_d = nc.dram_tensor("Wgk2x", [17, 1024], BF16, kind="ExternalInput")
    onesrow_d = nc.dram_tensor("onesrow", [1, 128], BF16, kind="ExternalInput")
    biasrow_d = nc.dram_tensor("biasrow", [1, 1024], BF16, kind="ExternalInput")
    gnw_d = nc.dram_tensor("gnw", [1, 512], F32, kind="ExternalInput")
    wo_d = nc.dram_tensor("Wo", [2048, 1024], F32, kind="ExternalInput")
    wf_d = nc.dram_tensor("Wffn", [1024, 1024], F32, kind="ExternalInput")
    bf_d = nc.dram_tensor("bffn", [1, 1024], F32, kind="ExternalInput")
    role_d = nc.dram_tensor("role", [1, 1], F32, kind="ExternalInput")
    triu_d = nc.dram_tensor("triu", [128, 128], BF16, kind="ExternalInput")
    mask4_d = nc.dram_tensor("mask4", [128, 512], BF16, kind="ExternalInput")
    ones128_d = nc.dram_tensor("ones128", [128, 128], BF16, kind="ExternalInput")
    onescol_d = nc.dram_tensor("onescol", [128, 1], BF16, kind="ExternalInput")

    y = nc.dram_tensor("y_seg", [SEG, HID], F32, kind="ExternalOutput")
    spill_kind = "ExternalOutput" if debug else "Internal"
    osp_d = nc.dram_tensor("o_sp", [SEG, DV], BF16, kind=spill_kind)
    hsp_d = nc.dram_tensor("h_sp", [SEG, HID], BF16, kind=spill_kind)
    if debug:
        sdump_d = nc.dram_tensor("s_dump", [128, 4096], F32, kind="ExternalOutput")
        smid_d = nc.dram_tensor("smid_dump", [128, 4096], F32, kind="ExternalOutput")

    from contextlib import ExitStack
    with ExitStack() as es:
        def sb(name, shape, dtype, **kw):
            return es.enter_context(nc.sbuf_tensor(name, shape, dtype, **kw))

        triu = sb("triu_sb", [128, 128], BF16)
        mask4 = sb("mask4_sb", [128, 512], BF16)
        ones128 = sb("ones128_sb", [128, 128], BF16)
        onescol = sb("onescol_sb", [128, 1], BF16)
        gnwc = sb("gnwc", [128, 4], F32)
        rolec = sb("rolec", [128, 1], F32)
        magic = sb("magic", [128, 4], I32)
        ebl4 = [sb(f"ebl{i}", [128, 8], F32) for i in range(4)]
        ecc4 = [sb(f"ecc{i}", [128, 8], F32) for i in range(4)]
        t8a = sb("t8a", [128, 4], F32)
        t8i = sb("t8i", [128, 4], I32)
        t8c = sb("t8c", [128, 4], F32)
        t8j = sb("t8j", [128, 4], I32)
        t8e = sb("t8e", [128, 4], F32)
        t8f = sb("t8f", [128, 4], I32)
        lnc = sb("lnc", [128, 16], F32)
        lnc2 = sb("lnc2", [128, 16], F32)
        ccols = sb("ccols", [128, 8], F32)
        zeros1k = sb("zeros1k", [128, 1024], F32)
        dump = sb("dump", [128, 512], F32)

        xt = [sb(f"xt{i}", [128, 1024], F32) for i in range(4)]
        hln = [sb(f"hln{i}", [128, 1024], BF16) for i in range(2)]
        hfm = [sb(f"hfm{i}", [128, 1024], BF16, is_transpose_dest=True) for i in range(4)]

        PT = [es.enter_context(nc.psum_tensor(f"PT{i}", [128, 1024], F32)) for i in range(4)]
        def PBv(i):
            return PT[i // 2][:, 512 * (i % 2):512 * (i % 2 + 1)]
        PB = [PBv(i) for i in range(8)]
        PBH = [f"PT{i // 2}:{i % 2}" for i in range(8)]
        PTH = [[f"PT{j}:0", f"PT{j}:1"] for j in range(4)]

        E = Emitter(nc)
        v = nc.vector
        s = nc.scalar
        t = nc.tensor
        g = nc.gpsimd
        sy = nc.sync

        def quake_rsqrt(dst, dst_h, src, src_h, tf, ti, n_iter=1):
            w = dst.shape[1]
            E.op("dve", lambda: v.tensor_single_scalar(
                out=ti[:, :w], in_=src.bitcast(I32), scalar=1,
                op=ALU.logical_shift_right), reads=[src_h], writes=[ti], cost=150)
            E.op("dve", lambda: v.tensor_tensor(
                out=dst.bitcast(I32), in0=magic[:, :w], in1=ti[:, :w],
                op=ALU.subtract), reads=[magic, ti], writes=[dst_h], cost=150)
            for _ in range(n_iter):
                E.op("dve", lambda: v.tensor_tensor(
                    out=tf[:, :w], in0=src, in1=dst, op=ALU.mult),
                    reads=[src_h, dst_h], writes=[tf], cost=150)
                E.op("dve", lambda: v.tensor_tensor(
                    out=tf[:, :w], in0=tf[:, :w], in1=dst, op=ALU.mult),
                    reads=[tf, dst_h], writes=[tf], cost=150)
                E.op("dve", lambda: v.tensor_scalar(
                    out=tf[:, :w], in0=tf[:, :w], scalar1=-0.5,
                    scalar2=1.5, op0=ALU.mult, op1=ALU.add),
                    reads=[tf], writes=[tf], cost=150)
                E.op("dve", lambda: v.tensor_tensor(
                    out=dst, in0=dst, in1=tf[:, :w], op=ALU.mult),
                    reads=[dst_h, tf], writes=[dst_h], cost=150)

        def layernorm(x_ap, out_ap, rh, wh, scr, tf, ti):
            E.op("dve", lambda: v.bn_stats(out=scr[:, 0:6], in_=x_ap[:, 0:512]),
                 reads=rh, writes=[scr], cost=600)
            E.op("dve", lambda: v.bn_stats(out=scr[:, 6:12], in_=x_ap[:, 512:1024]),
                 reads=rh, writes=[scr], cost=600)
            E.op("dve", lambda: v.bn_aggr(out=scr[:, 12:14], in_=scr[:, 0:12]),
                 reads=[scr], writes=[scr], cost=200)
            E.op("dve", lambda: v.tensor_scalar(
                out=scr[:, 14:15], in0=scr[:, 13:14], scalar1=1.0, scalar2=EPS,
                op0=ALU.mult, op1=ALU.add), reads=[scr], writes=[scr], cost=150)
            quake_rsqrt(scr[:, 15:16], scr, scr[:, 14:15], scr, tf, ti)
            E.op("dve", lambda: v.scalar_tensor_tensor(
                out=scr[:, 14:15], in0=scr[:, 12:13], scalar=-1.0, in1=scr[:, 15:16],
                op0=ALU.mult, op1=ALU.mult), reads=[scr], writes=[scr], cost=150)
            E.op("act", lambda: s.activation(out=out_ap[:, :], in_=x_ap[:, :], func=AF.Identity,
                                             scale=scr[:, 15:16], bias=scr[:, 14:15]),
                 reads=rh + [scr], writes=wh)

        def dmaT(dst, src, blocks, stream, eng="sp"):
            issuer = sy if eng == "sp" else s
            E.op(eng, lambda: issuer.dma_start(
                out=dst[:, :].rearrange("p (c t) -> p c t", c=blocks),
                in_=src[:, :].rearrange("t (c p) -> t c p", c=blocks),
                transpose=True), reads=[src], writes=[dst], stream=stream,
                cost=1400, lat=1500)

        E.op("pool", lambda: g.dma_start(out=triu[:, :], in_=triu_d[:, :]), writes=[triu], stream="c0")
        E.op("pool", lambda: g.dma_start(out=mask4[:, :], in_=mask4_d[:, :]), writes=[mask4], stream="c1")
        E.op("pool", lambda: g.dma_start(out=ones128[:, :], in_=ones128_d[:, :]), writes=[ones128], stream="c2")
        E.op("pool", lambda: g.dma_start(out=onescol[:, :], in_=onescol_d[:, :]), writes=[onescol], stream="c3")
        E.op("pool", lambda: g.dma_start(out=rolec[:, :], in_=role_d[:, :].broadcast_to([128, 1])),
             writes=[rolec], stream="c2")
        E.op("pool", lambda: g.dma_start(
            out=gnwc[:, :], in_=gnw_d[:, :].rearrange("o (c p) -> p (o c)", p=128),
            allow_slow_non_contiguous=True), writes=[gnwc], stream="c3")
        E.op("dve", lambda: v.memset(magic[:, :], MAGIC), writes=[magic])
        E.op("dve", lambda: v.memset(zeros1k[:, :], 0.0), writes=[zeros1k], cost=1200)
        for _i, _v in enumerate([-LN16, 1.0 / 16.0, -1.0 / 16.0, 1.0 / 32.0, -1.0 / 32.0, 0.5]):
            E.op("dve", lambda _i=_i, _v=_v: v.memset(ccols[:, _i:_i + 1], _v), writes=[ccols])
        C_NLN16 = ccols[:, 0:1]
        C_116 = ccols[:, 1:2]
        C_N116 = ccols[:, 2:3]
        C_132 = ccols[:, 3:4]
        C_N132 = ccols[:, 4:5]
        C_HALF = ccols[:, 5:6]
        with ExitStack() as esA:
            def sbA(name, shape, dtype, **kw):
                return esA.enter_context(nc.sbuf_tensor(name, shape, dtype, **kw))

            S8 = [sbA(f"S{i}", [128, 512], F32) for i in range(8)]
            S8b = [sbA(f"Sb{i}", [128, 512], BF16) for i in range(8)]
            for _i in range(8):
                E.op("dve", lambda _i=_i: v.memset(S8[_i][:, :], 0.0), writes=[S8[_i]])

            wq = sbA("wq", [128, 8 * 1024], BF16)
            wk = sbA("wk", [128, 8 * 1024], BF16)
            wv = sbA("wv", [128, 8 * 2048], BF16)
            wgk1 = sbA("wgk1", [128, 8 * 16], BF16)


            def load_w(dst, src, stream, blocks=8, lat=14000):
                E.op("pool", lambda: g.dma_start(
                    out=dst[:, :].rearrange("p (k n) -> p k n", k=blocks),
                    in_=src[:, :].rearrange("(k p) n -> p k n", p=128)),
                    writes=[dst], stream=stream, cost=1100, lat=lat)

            load_w(wk, wk_d, "wa2")
            load_w(wv, wv_d, "wa3", lat=28000)
            load_w(wgk1, wgk1_d, "wa2", lat=2000)
            wgk2x = sbA("wgk2x", [17, 1024], BF16)
            E.op("pool", lambda: g.dma_start(out=wgk2x[:, :], in_=wgk2x_d[:, :]),
                 writes=[wgk2x], stream="wa2", cost=650, lat=1500)
            A17_2 = [sbA(f"A17_{i}", [17, 128], BF16) for i in range(2)]
            for _i in range(2):
                E.op("pool", lambda _i=_i: g.dma_start(
                    out=A17_2[_i][16:17, :], in_=onesrow_d[:, :]),
                    writes=[A17_2[_i]], stream="c3", cost=650, lat=1500)
            load_w(wq, wq_d, "wa1")

            raws = sbA("raw0", [128, 1024], F32)
            raw2 = [raws, raws]
            min02 = [sbA(f"min0{i}", [128, 1024], BF16) for i in range(2)]
            FA52 = [sbA(f"FA5{i}", [128, 1024], F32) for i in range(2)]
            EcB2 = [sbA(f"EcB{i}", [128, 1024], F32) for i in range(2)]
            EnB2 = [sbA(f"EnB{i}", [128, 1024], F32) for i in range(2)]
            E1c2 = [sbA(f"E1c{i}", [128, 1024], BF16) for i in range(3)]
            Engc2 = [sbA(f"Engc{i}", [128, 1024], BF16) for i in range(3)]
            EnS2 = [sbA(f"EnS{i}", [128, 1024], BF16) for i in range(3)]
            gku2 = [sbA(f"gku{i}", [128, 1024], BF16) for i in range(3)]
            qg = sbA("qg", [128, 1024], BF16)
            ki = sbA("ki", [128, 1024], BF16)
            ks = sbA("ks", [128, 1024], BF16)
            qgfm = sbA("qgfm", [128, 1024], BF16, is_transpose_dest=True)
            kifm = sbA("kifm", [128, 1024], BF16, is_transpose_dest=True)
            vbf = sbA("vbf", [128, 2048], BF16)
            scbf = sbA("scbf", [128, 512], BF16)
            ospt = sbA("ospt", [128, 2048], BF16)


            def a_s0(i):
                c, src_d, full = descs[i]
                par = i % 4
                x_sb, hl = xt[par], hln[i % 2]
                E.op("sp", lambda x_sb=x_sb, c=c, src_d=src_d: sy.dma_start(
                    out=x_sb[:, :], in_=src_d[c * CH:(c + 1) * CH, :]),
                    writes=[x_sb], stream=f"x{par}", cost=700, lat=4500)
                layernorm(x_sb, hl, [x_sb], [hl], lnc, t8a, t8i)
                dmaT(hfm[par], hl, 8, f"th{par}")
                if full:
                    E.op("pool", lambda c=c, par=par: g.dma_start(
                        out=hsp_d[c * CH:(c + 1) * CH, :], in_=hfm[par][:, :]),
                        reads=[hfm[par]], writes=[hsp_d], stream=f"hs{c % 2}",
                        cost=650, lat=4500)

            def a_s1a(i):
                hf = hfm[i % 4]
                gku = gku2[i % 3]
                raw, min0, FA5 = raw2[i % 2], min02[i % 2], FA52[i % 2]
                for k in range(8):
                    E.op("pe", lambda k=k, hf=hf: t.matmul(
                        PB[1][:16, 0:128], wgk1[:, 16 * k:16 * (k + 1)],
                        hf[:, 128 * k:128 * (k + 1)],
                        start=(k == 0), stop=(k == 7)),
                        reads=[wgk1, hf], writes=[PBH[1]])
                A17 = A17_2[i % 2]
                E.op("act", lambda A17=A17: s.copy(out=A17[0:16, :], in_=PB[1][:16, 0:128]),
                     reads=[PBH[1]], writes=[A17], cost=200)
                for nh in range(2):
                    E.op("pe", lambda nh=nh, A17=A17: t.matmul(
                        PB[nh][:, :], A17[:, :], wgk2x[:, 512 * nh:512 * (nh + 1)],
                        start=True, stop=True),
                        reads=[A17, wgk2x], writes=[PBH[nh]])
                E.op("act", lambda: s.copy(out=raw[:, :], in_=PT[0][:, :]),
                     reads=PTH[0], writes=[raw], cost=1150)
                E.op("dve", lambda min0=min0, raw=raw: v.tensor_scalar_min(
                    out=min0[:, :], in0=raw[:, :], scalar1=0.0),
                    reads=[raw], writes=[min0])
                E.op("act", lambda: s.activation(out=FA5[:, :], in_=raw[:, :], func=AF.Abs),
                     reads=[raw], writes=[FA5])
                E.op("act", lambda: s.activation(out=raw[:, :], in_=FA5[:, :], func=AF.Exp, scale=-1.0),
                     reads=[FA5], writes=[raw])
                E.op("act", lambda: s.activation(out=FA5[:, :], in_=raw[:, :], func=AF.Ln, bias=1.0),
                     reads=[raw], writes=[FA5])
                E.op("pool", lambda gku=gku, min0=min0, FA5=FA5: g.tensor_tensor(
                    out=gku[:, :], in0=min0[:, :], in1=FA5[:, :], op=ALU.subtract),
                    reads=[min0, FA5], writes=[gku], cost=2350)

            def a_s1b(i):
                c, _, full = descs[i]
                gku = gku2[i % 3]
                E1c, Engc, EnS = E1c2[i % 3], Engc2[i % 3], EnS2[i % 3]
                if full:
                    for nh in range(2):
                        E.op("pe", lambda nh=nh, gku=gku: t.matmul(
                            PB[nh][:, :], triu[:, :], gku[:, 512 * nh:512 * (nh + 1)],
                            start=True, stop=True), reads=[triu, gku], writes=[PBH[nh]])
                    E.op("act", lambda Engc=Engc: s.activation(
                        out=Engc[:, :], in_=PT[0][:, :], func=AF.Exp,
                        scale=C_N116), reads=PTH[0] + [ccols], writes=[Engc])
                    E.op("act", lambda E1c=E1c: s.activation(
                        out=E1c[:, :], in_=PT[0][:, :], func=AF.Exp,
                        scale=C_116, bias=C_NLN16), reads=PTH[0] + [ccols], writes=[E1c])
                for nh in range(2):
                    E.op("pe", lambda nh=nh, gku=gku: t.matmul(
                        PB[nh][:, :], ones128[:, :], gku[:, 512 * nh:512 * (nh + 1)],
                        start=True, stop=True), reads=[ones128, gku], writes=[PBH[nh]])
                E.op("act", lambda EnS=EnS: s.activation(
                    out=EnS[:, :], in_=PT[0][:, :], func=AF.Exp,
                    scale=C_116), reads=PTH[0] + [ccols], writes=[EnS])
                for blk in range(8):
                    E.op("pe", lambda blk=blk, gku=gku: t.matmul(
                        PB[0][:, blk:blk + 1], gku[:, 128 * blk:128 * (blk + 1)], onescol[:, :],
                        start=True, stop=True), reads=[gku, onescol], writes=[PBH[0]])
                ebl, ecc = ebl4[i % 4], ecc4[i % 4]
                E.op("act", lambda ebl=ebl: s.activation(out=ebl[:, :], in_=PB[0][:, 0:8], func=AF.Exp,
                                                 scale=C_116),
                     reads=[PBH[0], ccols], writes=[ebl], cost=250)
                if full:
                    E.op("act", lambda ecc=ecc: s.activation(out=ecc[:, :], in_=PB[0][:, 0:8], func=AF.Exp,
                                                     scale=C_132),
                         reads=[PBH[0], ccols], writes=[ecc], cost=250)

            def a_s2(i):
                c, _, full = descs[i]
                hf = hfm[i % 4]
                E1c, Engc, EnS = E1c2[i % 3], Engc2[i % 3], EnS2[i % 3]
                for k in range(8):
                    for nh in range(2):
                        E.op("pe", lambda k=k, nh=nh, hf=hf: t.matmul(
                            PB[4 + nh][:, :], hf[:, 128 * k:128 * (k + 1)],
                            wk[:, 1024 * k + 512 * nh:1024 * k + 512 * (nh + 1)],
                            start=(k == 0), stop=(k == 7)),
                            reads=[hf, wk], writes=[PBH[4 + nh]])
                if full:
                    for k in range(8):
                        for nh in range(2):
                            E.op("pe", lambda k=k, nh=nh, hf=hf: t.matmul(
                                PB[6 + nh][:, :], hf[:, 128 * k:128 * (k + 1)],
                                wq[:, 1024 * k + 512 * nh:1024 * k + 512 * (nh + 1)],
                                start=(k == 0), stop=(k == 7)),
                                reads=[hf, wq], writes=[PBH[6 + nh]])
                    E.op("dve", lambda: v.tensor_tensor(
                        out=qg[:, :], in0=PT[3][:, :], in1=E1c[:, :], op=ALU.mult),
                        reads=PTH[3] + [E1c], writes=[qg])
                    E.op("dve", lambda: v.tensor_tensor(
                        out=ki[:, :], in0=PT[2][:, :], in1=Engc[:, :], op=ALU.mult),
                        reads=PTH[2] + [Engc], writes=[ki])
                E.op("dve", lambda: v.tensor_tensor(
                    out=ks[:, :], in0=PT[2][:, :], in1=EnS[:, :], op=ALU.mult),
                    reads=PTH[2] + [EnS], writes=[ks])
                if full:
                    dmaT(qgfm, qg, 8, "tq")
                    dmaT(kifm, ki, 8, "tk")
                for k in range(8):
                    for ns in range(4):
                        E.op("pe", lambda k=k, ns=ns, hf=hf: t.matmul(
                            PB[2 + ns][:, :], hf[:, 128 * k:128 * (k + 1)],
                            wv[:, 2048 * k + 512 * ns:2048 * k + 512 * (ns + 1)],
                            start=(k == 0), stop=(k == 7)),
                            reads=[hf, wv], writes=[PBH[2 + ns]])
                E.op("dve", lambda: v.tensor_copy(out=vbf[:, 0:1024], in_=PT[1][:, :]),
                     reads=PTH[1], writes=[vbf])
                E.op("act", lambda: s.copy(out=vbf[:, 1024:2048], in_=PT[2][:, :]),
                     reads=PTH[2], writes=[vbf])

            def a_s3(i):
                c, _, full = descs[i]
                ebl, ecc = ebl4[i % 4], ecc4[i % 4]
                if full:
                    for blk in range(8):
                        if blk % 4 == 0:
                            E.op("dve", lambda blk=blk, ecc=ecc: v.tensor_scalar_mul(
                                out=S8b[blk][:, :], in0=S8[blk][:, :],
                                scalar1=ecc[:, blk:blk + 1]),
                                reads=[S8[blk], ecc], writes=[S8b[blk]])
                        else:
                            E.op("act", lambda blk=blk, ecc=ecc: s.activation(
                                out=S8b[blk][:, :], in_=S8[blk][:, :], func=AF.Identity,
                                scale=ecc[:, blk:blk + 1]),
                                reads=[S8[blk], ecc], writes=[S8b[blk]])
                    for h in range(4):
                        for kc in range(2):
                            blk = 2 * h + kc
                            E.op("pe", lambda h=h, kc=kc, blk=blk: t.matmul(
                                PB[6][:, 128 * h:128 * (h + 1)],
                                kifm[:, 128 * blk:128 * (blk + 1)],
                                qgfm[:, 128 * blk:128 * (blk + 1)],
                                start=(kc == 0), stop=(kc == 1)),
                                reads=[kifm, qgfm], writes=[PBH[6]])
                    E.op("dve", lambda: v.tensor_tensor(out=scbf[:, :], in0=PB[6][:, :],
                                                        in1=mask4[:, :], op=ALU.mult),
                         reads=[PBH[6], mask4], writes=[scbf])
                    for h in range(4):
                        E.op("pe", lambda h=h: t.matmul(
                            PB[2 + h][:, :], scbf[:, 128 * h:128 * (h + 1)],
                            vbf[:, 512 * h:512 * (h + 1)], start=True, stop=False),
                            reads=[scbf, vbf], writes=[PBH[2 + h]])
                        for kc in range(2):
                            blk = 2 * h + kc
                            E.op("pe", lambda h=h, kc=kc, blk=blk: t.matmul(
                                PB[2 + h][:, :], qgfm[:, 128 * blk:128 * (blk + 1)],
                                S8b[blk][:, :],
                                start=False, stop=(kc == 1)),
                                reads=[qgfm, S8b[blk]], writes=[PBH[2 + h]])
                    E.op("dve", lambda: v.tensor_copy(out=ospt[:, 0:1024], in_=PT[1][:, :]),
                         reads=PTH[1], writes=[ospt])
                    E.op("act", lambda: s.copy(out=ospt[:, 1024:2048], in_=PT[2][:, :]),
                         reads=PTH[2], writes=[ospt])
                    E.op("pool", lambda c=c: g.dma_start(
                        out=osp_d[c * CH:(c + 1) * CH, :], in_=ospt[:, :]),
                        reads=[ospt], writes=[osp_d], stream="so",
                        cost=650, lat=4500)
                for h in range(4):
                    for mc in range(2):
                        blk = 2 * h + mc
                        pb = PB[6 + (blk % 2)]; pbh = PBH[6 + (blk % 2)]
                        E.op("pe", lambda blk=blk, h=h, pb=pb: t.matmul(
                            pb[:, :], ks[:, 128 * blk:128 * (blk + 1)],
                            vbf[:, 512 * h:512 * (h + 1)], start=True, stop=True),
                            reads=[ks, vbf], writes=[pbh])
                        E.op("dve", lambda blk=blk, pb=pb, ebl=ebl: v.scalar_tensor_tensor(
                            out=S8[blk][:, :],
                            in0=S8[blk][:, :],
                            scalar=ebl[:, blk:blk + 1], in1=pb[:, :],
                            op0=ALU.mult, op1=ALU.add),
                            reads=[S8[blk], ebl, pbh], writes=[S8[blk]])

            descs = [(c, xpre_d, False) for c in range(n_chunks)] + \
                    [(c, xin, True) for c in range(n_chunks)]
            N = len(descs)

            def a_mask(i_done):
                if i_done == n_chunks - 1:
                    for blk in range(8):
                        E.op("dve", lambda blk=blk: v.tensor_scalar_mul(
                            out=S8[blk][:, :], in0=S8[blk][:, :], scalar1=rolec[:, :]),
                            reads=[S8[blk], rolec], writes=[S8[blk]])
                    if debug:
                        for blk in range(8):
                            E.op("pool", lambda blk=blk: g.dma_start(
                                out=smid_d[:, 512 * blk:512 * (blk + 1)], in_=S8[blk][:, :]),
                                reads=[S8[blk]], writes=[smid_d], stream="so")

            for j in range(min(3, N)):
                a_s0(j)
                a_s1a(j)
            a_s1b(0)
            if N > 3:
                a_s0(3)
                a_s1a(3)
            if N > 1:
                a_s1b(1)
            for i in range(N):
                if i + 2 < N:
                    a_s1b(i + 2)
                if i >= 1:
                    a_s3(i - 1)
                    a_mask(i - 1)
                a_s2(i)
                if i + 4 < N:
                    a_s0(i + 4)
                    a_s1a(i + 4)
            a_s3(N - 1)
            a_mask(N - 1)

            if debug:
                for blk in range(8):
                    E.op("pool", lambda blk=blk: g.dma_start(
                        out=sdump_d[:, 512 * blk:512 * (blk + 1)], in_=S8[blk][:, :]),
                        reads=[S8[blk]], writes=[sdump_d], stream="so")

        E.barrier()

        with ExitStack() as esB:
            def sbB(name, shape, dtype, **kw):
                return esB.enter_context(nc.sbuf_tensor(name, shape, dtype, **kw))

            wg = sbB("wg", [128, 8 * 2048], BF16)
            wo = sbB("wo", [128, 16 * 1024], BF16)
            w8t = sbB("w8t", [128, 8 * 1024], BF16)
            wstage2 = [sbB(f"wstage{i}", [128, 1024], F32) for i in range(2)]
            onesrowt = sbB("onesrowt", [1, 128], BF16)
            biasrow = sbB("biasrow_sb", [1, 1024], BF16)
            E.op("pool", lambda: g.dma_start(out=onesrowt[:, :], in_=onesrow_d[:, :]),
                 writes=[onesrowt], stream="c0", cost=650, lat=1500)
            E.op("pool", lambda: g.dma_start(out=biasrow[:, :], in_=biasrow_d[:, :]),
                 writes=[biasrow], stream="c1", cost=650, lat=1500)
            x8_2 = [sbB(f"x8_{i}", [128, 1024], FP8) for i in range(2)]
            x8b_2 = [sbB(f"x8b_{i}", [128, 1024], BF16) for i in range(2)]

            for wh in range(2):
                E.op("pool", lambda wh=wh: g.dma_start(
                    out=wg[:, 8192 * wh:8192 * (wh + 1)].rearrange("p (k n) -> p k n", k=4),
                    in_=wg_d[512 * wh:512 * (wh + 1), :].rearrange("(k p) n -> p k n", p=128)),
                    writes=[wg], stream=("wa1", "wa2")[wh], cost=1100, lat=14000)
            for dc in range(16):
                ws = wstage2[dc % 2]
                E.op("pool", lambda dc=dc, ws=ws: g.dma_start(
                    out=ws[:, :], in_=wo_d[128 * dc:128 * (dc + 1), :]),
                    writes=[ws], stream=("wa3", "so")[dc % 2], cost=650, lat=4500)
                E.op("dve", lambda dc=dc, ws=ws: v.tensor_scalar_mul(
                    out=wo[:, 1024 * dc:1024 * (dc + 1)], in0=ws[:, :],
                    scalar1=gnwc[:, dc % 4:dc % 4 + 1]),
                    reads=[ws, gnwc], writes=[wo])
            for fc in range(8):
                ws = wstage2[fc % 2]
                w8q, w8b = x8_2[fc % 2], x8b_2[fc % 2]
                E.op("pool", lambda fc=fc, ws=ws: g.dma_start(
                    out=ws[:, :], in_=wf_d[128 * fc:128 * (fc + 1), :]),
                    writes=[ws], stream=("wa3", "so")[fc % 2], cost=650, lat=4500)
                E.op("dve", lambda ws=ws: v.tensor_scalar(
                    out=ws[:, :], in0=ws[:, :], scalar1=448.0, scalar2=-448.0,
                    op0=ALU.min, op1=ALU.max), reads=[ws], writes=[ws])
                E.op("dve", lambda ws=ws, w8q=w8q: v.tensor_copy(out=w8q[:, :], in_=ws[:, :]),
                     reads=[ws], writes=[w8q])
                E.op("act", lambda w8q=w8q, w8b=w8b: s.copy(out=w8b[:, :], in_=w8q[:, :]),
                     reads=[w8q], writes=[w8b])
                E.op("sp", lambda fc=fc, w8b=w8b: sy.dma_start(
                    out=w8t[:, :].rearrange("p (blk f) -> p blk f", blk=8)[:, :, 128 * fc:128 * (fc + 1)],
                    in_=w8b[:, :].rearrange("f (blk p) -> f blk p", blk=8),
                    transpose=True), reads=[w8b], writes=[w8t], stream=("tq", "tk")[fc % 2],
                    cost=1400, lat=1500)

            G1_2 = [sbB(f"G1_{i}", [128, 2048], BF16) for i in range(3)]
            G2s = sbB("G2_0", [128, 2048], BF16)
            G2_2 = [G2s, G2s]
            ogs = sbB("og0", [128, 2048], BF16)
            og2 = [ogs, ogs]
            gate2 = [sbB(f"gate{i}", [128, 2048], BF16) for i in range(3)]
            ogfm2 = [sbB(f"ogfm{i}", [128, 2048], BF16, is_transpose_dest=True) for i in range(2)]
            oloc = [sbB(f"oloc{i}", [128, 2048], BF16) for i in range(4)]
            x8fms = sbB("x8fm0", [128, 1024], BF16, is_transpose_dest=True)
            x8fm2 = [x8fms, x8fms]
            racc2 = [sbB(f"racc{i}", [128, 4], F32) for i in range(2)]
            rstd2 = [sbB(f"rstd{i}", [128, 4], F32) for i in range(2)]
            XM2 = [sbB(f"XM{i}", [128, 1024], F32) for i in range(2)]
            XLns = sbB("XLn0", [128, 1024], F32)
            XLn2 = [XLns, XLns]
            OUT2 = wstage2

            def b_t0(c):
                par = c % 4
                x_sb = xt[par]
                E.op("sp", lambda c=c, x_sb=x_sb: sy.dma_start(
                    out=x_sb[:, :], in_=xin[c * CH:(c + 1) * CH, :]),
                    writes=[x_sb], stream=f"x{par}", cost=700, lat=4500)
                E.op("pool", lambda c=c: g.dma_start(
                    out=oloc[c % 4][:, :], in_=osp_d[c * CH:(c + 1) * CH, :]),
                    reads=[osp_d], writes=[oloc[c % 4]], stream=f"ol{c % 4}",
                    cost=650, lat=4500),
                E.op("sp", lambda c=c, par=par: sy.dma_start(
                    out=hfm[par][:, :], in_=hsp_d[c * CH:(c + 1) * CH, :]),
                    reads=[hsp_d], writes=[hfm[par]], stream=f"th{par % 3}",
                    cost=700, lat=4500)

            def b_t1a(c):
                hf = hfm[c % 4]
                G1 = G1_2[c % 3]
                for k in range(8):
                    for ns in range(4):
                        E.op("pe", lambda k=k, ns=ns, hf=hf: t.matmul(
                            PB[ns][:, :], hf[:, 128 * k:128 * (k + 1)],
                            wg[:, 2048 * k + 512 * ns:2048 * k + 512 * (ns + 1)],
                            start=(k == 0), stop=(k == 7)),
                            reads=[hf, wg], writes=[PBH[ns]])
                E.op("act", lambda G1=G1: s.copy(out=G1[:, 0:1024], in_=PT[0][:, :]),
                     reads=PTH[0], writes=[G1])
                E.op("act", lambda G1=G1: s.copy(out=G1[:, 1024:2048], in_=PT[1][:, :]),
                     reads=PTH[1], writes=[G1])

            def b_t1b(c):
                G1, G2 = G1_2[c % 3], G2_2[c % 2]
                E.op("act", lambda G1=G1, G2=G2: s.activation(out=G2[:, :], in_=G1[:, :], func=AF.Tanh,
                                                 scale=C_HALF),
                     reads=[G1, ccols], writes=[G2], cost=2100)
                gb = gate2[c % 3]
                E.op("dve", lambda gb=gb, G1=G1, G2=G2: v.scalar_tensor_tensor(
                    out=gb[:, :], in0=G2[:, :], scalar=1.0, in1=G1[:, :],
                    op0=ALU.add, op1=ALU.mult),
                    reads=[G1, G2], writes=[gb], cost=1300)

            def b_t2(c):
                ol = oloc[c % 4]
                racc, rstd = racc2[c % 2], rstd2[c % 2]
                og, ogfm = og2[c % 2], ogfm2[c % 2]
                gb = gate2[c % 3]
                XM = XM2[c % 2]
                x8, x8b = x8_2[c % 2], x8b_2[c % 2]
                x8fm = x8fm2[c % 2]
                for h in range(4):
                    E.op("act", lambda h=h, ol=ol, racc=racc: s.activation(
                        out=dump[:, :], in_=ol[:, 512 * h:512 * (h + 1)], func=AF.Square,
                        accum_out=racc[:, h:h + 1]),
                        reads=[ol], writes=[dump, racc])
                E.op("dve", lambda racc=racc: v.tensor_scalar(
                    out=racc[:, :], in0=racc[:, :], scalar1=4.0 / 512.0, scalar2=4.0 * EPS,
                    op0=ALU.mult, op1=ALU.add), reads=[racc], writes=[racc])
                quake_rsqrt(rstd[:, :], rstd, racc[:, :], racc, t8e, t8f)
                for h in range(4):
                    E.op("dve", lambda h=h, ol=ol, og=og, rstd=rstd, gb=gb: v.scalar_tensor_tensor(
                        out=og[:, 512 * h:512 * (h + 1)], in0=ol[:, 512 * h:512 * (h + 1)],
                        scalar=rstd[:, h:h + 1], in1=gb[:, 512 * h:512 * (h + 1)],
                        op0=ALU.mult, op1=ALU.mult),
                        reads=[ol, rstd, gb], writes=[og])
                dmaT(ogfm, og, 16, f"to{c % 2}")
                for dc in range(16):
                    for nh in range(2):
                        E.op("pe", lambda dc=dc, nh=nh, ogfm=ogfm: t.matmul(
                            PB[4 + nh][:, :], ogfm[:, 128 * dc:128 * (dc + 1)],
                            wo[:, 1024 * dc + 512 * nh:1024 * dc + 512 * (nh + 1)],
                            start=(dc == 0), stop=(dc == 15)),
                            reads=[ogfm, wo], writes=[PBH[4 + nh]])
                E.op("dve", lambda c=c, XM=XM: v.tensor_tensor(
                    out=XM[:, :], in0=PT[2][:, :], in1=xt[c % 4][:, :], op=ALU.add),
                    reads=PTH[2] + [xt[c % 4]], writes=[XM])
                XLn = XLn2[c % 2]
                layernorm(XM, XLn, [XM], [XLn], lnc2, t8c, t8j)
                E.op("dve", lambda XLn=XLn, x8=x8: v.tensor_copy(out=x8[:, :], in_=XLn[:, :]),
                     reads=[XLn], writes=[x8])
                E.op("act", lambda x8=x8, x8b=x8b: s.copy(out=x8b[:, :], in_=x8[:, :]),
                     reads=[x8], writes=[x8b])
                dmaT(x8fm, x8b, 8, f"t8{c % 2}")

            def b_t3(c):
                XM = XM2[c % 2]
                OUT = OUT2[c % 2]
                x8fm = x8fm2[c % 2]
                for nh in range(2):
                    E.op("pe", lambda nh=nh: t.matmul(
                        PB[6 + nh][:, :], onesrowt[:, :],
                        biasrow[:, 512 * nh:512 * (nh + 1)],
                        start=True, stop=False),
                        reads=[onesrowt, biasrow], writes=[PBH[6 + nh]], cost=120)
                for k in range(8):
                    for nh in range(2):
                        E.op("pe", lambda k=k, nh=nh, x8fm=x8fm: t.matmul(
                            PB[6 + nh][:, :], x8fm[:, 128 * k:128 * (k + 1)],
                            w8t[:, 1024 * k + 512 * nh:1024 * k + 512 * (nh + 1)],
                            start=False, stop=(k == 7)),
                            reads=[x8fm, w8t], writes=[PBH[6 + nh]])
                E.op("dve", lambda XM=XM, OUT=OUT: v.tensor_tensor(
                    out=OUT[:, :], in0=PT[3][:, :], in1=XM[:, :], op=ALU.add),
                    reads=PTH[3] + [XM], writes=[OUT])
                E.op("pool", lambda c=c, OUT=OUT: g.dma_start(
                    out=y[c * CH:(c + 1) * CH, :], in_=OUT[:, :]),
                    reads=[OUT], writes=[y], stream=f"y{c % 2}",
                    cost=650, lat=4500)

            for j in range(min(3, n_chunks)):
                b_t0(j)
            for j in range(min(3, n_chunks)):
                b_t1a(j)
            for j in range(min(2, n_chunks)):
                b_t1b(j)
            for c in range(n_chunks):
                if c + 3 < n_chunks:
                    b_t0(c + 3)
                    b_t1a(c + 3)
                if c + 2 < n_chunks:
                    b_t1b(c + 2)
                if c >= 1:
                    b_t3(c - 1)
                b_t2(c)
            b_t3(n_chunks - 1)

        block = es.enter_context(nc.Block())
        handles = {"pe": nc.tensor, "act": nc.scalar, "dve": nc.vector,
                   "pool": nc.gpsimd, "sp": nc.sync}
        E.emit(block, lambda name: es.enter_context(nc.semaphore(f"sem_{name}")), handles)

    return nc


def _consts():
    tri = np.triu(np.ones((128, 128), dtype=np.float32))
    mask4 = np.tile(tri, (1, 4)).astype(ml_dtypes.bfloat16)
    mmat = (tri - 0.5).astype(ml_dtypes.bfloat16)
    nmat = (1.0 - tri).astype(ml_dtypes.bfloat16)
    onescol = np.ones((128, 1), dtype=ml_dtypes.bfloat16)
    return mmat, mask4, nmat, onescol


def make_in_maps(x, inputs, n_chunks=N_CHUNKS, n_cores=8):
    tri, mask4, ones128, onescol = _consts()
    seg = n_chunks * CH
    common = dict(
        Wq=np.asarray(inputs["L0_attn_Wq"], np.float32),
        Wk=np.asarray(inputs["L0_attn_Wk"], np.float32),
        Wv=np.asarray(inputs["L0_attn_Wv"], np.float32),
        Wg=np.asarray(inputs["L0_attn_Wg"], np.float32),
        Wgk1=np.asarray(inputs["L0_attn_Wgk1"], np.float32),
        Wgk2x=np.vstack([
            np.asarray(inputs["L0_attn_Wgk2"], np.float32),
            np.asarray(inputs["L0_attn_bgk2"], np.float32).reshape(1, 1024),
        ]).astype(ml_dtypes.bfloat16),
        onesrow=np.ones((1, 128), dtype=ml_dtypes.bfloat16),
        biasrow=np.asarray(inputs["L0_ffn_bias"], np.float32).reshape(1, 1024).astype(ml_dtypes.bfloat16),
        gnw=np.asarray(inputs["L0_attn_g_norm_w"], np.float32).reshape(1, 512),
        Wo=np.asarray(inputs["L0_attn_Wo"], np.float32),
        Wffn=np.asarray(inputs["L0_ffn_weight"], np.float32),
        bffn=np.asarray(inputs["L0_ffn_bias"], np.float32).reshape(1, 1024),
        triu=tri, mask4=mask4, ones128=ones128, onescol=onescol,
    )
    zeros = np.zeros((seg, HID), np.float32)
    in_maps = []
    for core in range(n_cores):
        b, h = core // 2, core % 2
        m = dict(common)
        m["x_seg"] = np.ascontiguousarray(x[b, h * seg:(h + 1) * seg, :])
        m["x_pre"] = np.ascontiguousarray(x[b, 0:seg, :]) if h == 1 else zeros
        m["role"] = np.array([[float(h)]], np.float32)
        in_maps.append(m)
    return in_maps


def kernel(L0_attn_hidden_states, L0_attn_Wq, L0_attn_Wk, L0_attn_Wv, L0_attn_Wg,
           L0_attn_Wgk1, L0_attn_Wgk2, L0_attn_bgk2, L0_attn_g_norm_w, L0_attn_Wo,
           L0_ffn_weight, L0_ffn_bias):
    x = np.asarray(L0_attn_hidden_states, dtype=np.float32)
    assert x.shape == (4, 4096, 1024)
    inputs = dict(
        L0_attn_Wq=L0_attn_Wq, L0_attn_Wk=L0_attn_Wk, L0_attn_Wv=L0_attn_Wv,
        L0_attn_Wg=L0_attn_Wg, L0_attn_Wgk1=L0_attn_Wgk1, L0_attn_Wgk2=L0_attn_Wgk2,
        L0_attn_bgk2=L0_attn_bgk2, L0_attn_g_norm_w=L0_attn_g_norm_w,
        L0_attn_Wo=L0_attn_Wo, L0_ffn_weight=L0_ffn_weight, L0_ffn_bias=L0_ffn_bias,
    )
    nc = build_program()
    in_maps = make_in_maps(x, inputs)
    res = run_bass_kernel_spmd(nc, in_maps, core_ids=list(range(8)))
    out = np.empty((4, 4096, 1024), np.float32)
    for core in range(8):
        b, h = core // 2, core % 2
        out[b, h * 2048:(h + 1) * 2048, :] = res.results[core]["y_seg"]
    return out
